# version 11
# baseline (speedup 1.0000x reference)
import contextlib
import numpy as np
import ml_dtypes
import concourse.bass as bass
import concourse.mybir as mybir
from concourse.bass_utils import run_bass_kernel_spmd

F32 = mybir.dt.float32
BF16 = mybir.dt.bfloat16
U8 = mybir.dt.uint8
AF = mybir.ActivationFunctionType
ALU = mybir.AluOpType
AX = mybir.AxisListType

NCORES = 8
NT = 64
TPB = 16
NB = 4
NU = 80
D = 1024
H = 16
EPS = 1e-5

PASS_TILES = [list(range(i, min(i + 3, NT))) for i in range(0, NT, 3)]
NPASS = len(PASS_TILES)
MMK = dict(skip_group_check=True)


class Ctr:
    def __init__(self, handle, name):
        self.h = handle
        self.name = name
        self.n = 0


class Emitter:
    ENGS = ("sync", "scalar", "vector", "tensor", "gpsimd")

    def __init__(self):
        self.ops = {e: [] for e in self.ENGS}
        self.last_wait = {}
        self.future = []

    def emit(self, eng, fn, waits=(), inc=None, inc_n=1, future=False,
             is_dma=False, nodrain=False, tset=None):
        wl = []
        for (sem, val) in waits:
            if val <= 0:
                continue
            if not future:
                assert val <= sem.n, \
                    f"[{eng}] wait on future count {sem.name} {val} > {sem.n}"
            else:
                self.future.append((sem, val))
            key = (eng, sem.name)
            if self.last_wait.get(key, 0) >= val:
                continue
            self.last_wait[key] = val
            wl.append((sem, val))
        self.ops[eng].append((fn, wl, inc, inc_n, is_dma or nodrain, tset))
        if inc is not None:
            inc.n += inc_n

    def check(self):
        for (sem, val) in self.future:
            assert val <= sem.n, f"future wait {sem.name} {val} > final {sem.n}"

    def replay(self, eng_name, eng):
        POLICY = "C"
        last_tset = None
        prev_inc = False
        for (fn, waits, inc, inc_n, skip_drain, tset) in self.ops[eng_name]:
            drained = False
            if eng_name == "scalar" and POLICY in ("A", "C"):
                eng.drain()
                drained = True
            if eng_name == "vector" and POLICY == "A":
                eng.drain()
                drained = True
            if not drained and eng_name in ("vector", "scalar") and prev_inc:
                eng.drain()
                drained = True
            if (not drained and eng_name == "vector" and not skip_drain):
                eng.drain()
                drained = True
            if eng_name == "scalar" and tset is not None:
                if last_tset is not None and tset != last_tset and not drained:
                    eng.drain()
                last_tset = tset
            prev_inc = inc is not None
            for (sem, val) in waits:
                eng.wait_ge(sem.h, val)
            ins = fn(eng)
            if inc is not None:
                assert ins is not None
                ins.then_inc(inc.h, inc_n)


class Carver:
    def __init__(self, arena_ap, base, limit):
        self.a = arena_ap
        self.off = base
        self.limit = limit

    def alloc(self, shape, dtype):
        esz = {F32: 4, BF16: 2, U8: 1}[dtype]
        nb = shape[-1] * esz
        off = (self.off + 63) & ~63
        self.off = off + nb
        assert self.off <= self.limit, f"carver overflow {self.off} > {self.limit}"
        return self.a[0:shape[0], off:off + nb].bitcast(dtype)


def build_nc(nt=NT, npasses=None, wconv=True, debug=False, acut=99, kvterms=4):
    NT_ = nt
    NB_ = nt // TPB
    PT = [list(range(i, min(i + 3, nt))) for i in range(0, nt, 3)]
    if npasses is not None:
        PT = PT[:npasses]
    NP_ = len(PT)
    nc = bass.Bass("TRN2", target_bir_lowering=False, debug=False,
                   detect_race_conditions=False)

    def din(name, shape, dtype):
        return nc.dram_tensor(name, list(shape), dtype, kind="ExternalInput").ap()

    x_d = din("x", (NT * 128, D), F32)
    qkvwT_d = din("qkv_wT", (D, 3 * D), F32)
    p1wT_d = din("p1_wT", (D, D), F32)
    p2wT_d = din("p2_wT", (D, D), F32)
    rw_d = din("router_w", (D, 64), F32)
    wall_d = din("wall", (NU, 524288), F32)
    bias1_d = din("bias1", (128, 2 * NU), F32)
    b2e_d = din("b2ext", (66, D), F32)
    qb_d = din("qb", (128, 8), F32)
    p1b_d = din("p1b", (128, 8), F32)
    rows7_d = din("rows7", (1, 7 * D), F32)
    rb_d = din("rb", (1, 64), F32)
    tempeo_d = din("temp_eo", (1, 16), F32)
    abias_d = din("attn_bias", (16, 64, 64), F32)
    oh_d = din("oh", (64, 64 * 128), BF16)
    idf_d = din("idf", (128, 128), F32)
    onesf_d = din("onesf", (1, 128), F32)
    onesb_d = din("onesb", (1, 128), BF16)
    onescol_d = din("onescol", (128, 1), BF16)
    cols4_d = din("cols4", (128, 4), F32)

    out_d = nc.dram_tensor("out", [NT * 128, D], F32, kind="ExternalOutput").ap()
    wscr_d = nc.dram_tensor("wscr", [NU, 128, 4096], BF16, kind="Internal").ap()
    qscr_d = nc.dram_tensor("qscr", [NT, 8, 128, 128], BF16, kind="Internal").ap()

    em = Emitter()
    ctx = contextlib.ExitStack()

    def S(name):
        return Ctr(ctx.enter_context(nc.semaphore(name)), name)

    s_small = S("small"); s_conv = S("conv"); s_wconv = S("wconv")
    s_init = S("init"); s_initc = S("initc"); s_initcS = S("initcS")
    s_xldA2 = [S("xldA0"), S("xldA1")]; s_xfreeA = S("xfreeA")
    s_trA = S("trA"); s_trcA = S("trcA")
    s_qpeA = S("qpeA"); s_qevA = S("qevA")
    s_kpeA = S("kpeA"); s_vpeA = S("vpeA"); s_kvpeA = S("kvpeA"); s_lpeA = S("lpeA")
    s_kevA = S("kevA"); s_vevA = S("vevA"); s_levA = S("levA")
    s_k16A = S("k16A"); s_v16A = S("v16A"); s_kvfA = S("kvfA")
    s_xnA = S("xnA"); s_sqA = S("sqA"); s_rstdA = S("rstdA")
    s_hsubA = S("hsubA"); s_varA = S("varA")
    s_sqxA = S("sqxA"); s_sumA = S("sumA")
    s_fA = S("fA"); s_fP = S("fP"); s_fD = S("fD"); s_fM = S("fM")
    s_qsp2 = [S("qsp0"), S("qsp1")]
    s_a3d = S("a3d"); s_a3e = S("a3e"); s_a3p = S("a3p"); s_a3t = S("a3t")
    s_cwT = S("cwT"); s_ohld = S("ohld")
    s_xldB = S("xldB"); s_qldB = S("qldB")
    s_wld3 = [S("wld0"), S("wld1"), S("wld2")]
    s_xnB = S("xnB"); s_trB = S("trB"); s_trcB = S("trcB")
    s_peh = S("peh"); s_silu = S("silu"); s_hsc = S("hsc")
    s_cwbpe = S("cwbpe"); s_cwbc = S("cwbc"); s_mm2 = S("mm2"); s_rdone = S("rdone")
    s_gev = S("gev"); s_attpe = S("attpe"); s_attev = S("attev")
    s_p1pe = S("p1pe"); s_gelev = S("gelev"); s_p2pe = S("p2pe")
    s_sev = S("sev"); s_odve = S("odve"); s_odma = S("odma")

    def sbuf(name, shape, dtype):
        return ctx.enter_context(nc.sbuf_tensor(name, list(shape), dtype)).ap()

    stats_m = sbuf("stats_m", [128, NT], F32)
    stats_r = sbuf("stats_r", [128, NT], F32)
    cwT = sbuf("cwT", [66, 64 * 128], BF16)
    kv_sb = sbuf("kv_sb", [128, NB * 1024], BF16)
    p1_sb = sbuf("p1_sb", [128, 8 * 1024], BF16)
    p2_sb = sbuf("p2_sb", [128, 8 * 1024], BF16)
    router_sb = sbuf("router_sb", [128, 8 * 64], BF16)
    b2e_sb = sbuf("b2e_sb", [66, 1024], BF16)
    bias1_sb = sbuf("bias1_sb", [128, 2 * NU], F32)
    qb_sb = sbuf("qb_sb", [128, 8], F32)
    p1b_sb = sbuf("p1b_sb", [128, 8], F32)
    p2b_bf = sbuf("p2b_bf", [1, 1024], BF16)
    onesf_sb = sbuf("onesf_sb", [1, 128], F32)
    onesb_sb = sbuf("onesb_sb", [1, 128], BF16)
    idf_sb = sbuf("idf_sb", [128, 128], F32)
    tempcol = sbuf("tempcol", [128, 8], F32)
    tempeo_sb = sbuf("tempeo_sb", [1, 16], F32)
    rbrow_sb = sbuf("rbrow_sb", [1, 64], F32)
    eps_col = sbuf("eps_col", [128, 1], F32)
    onescol_sb = sbuf("onescol_sb", [128, 1], BF16)
    cols4_sb = sbuf("cols4_sb", [128, 4], F32)
    kb_bf = sbuf("kb_bf", [1, 1024], BF16)
    vb_bf = sbuf("vb_bf", [1, 1024], BF16)
    rb_bf = sbuf("rb_bf", [1, 64], BF16)

    arena = ctx.enter_context(nc.sbuf_tensor("ov", [128, 143168], U8)).ap()

    ca = Carver(arena, 0, 143168)
    qkv_sb = ca.alloc([128, 8 * 3072], BF16)
    vg_bc = ca.alloc([128, 1024], F32)
    vb2_bc = ca.alloc([128, 1024], F32)
    gkgv_pair = ca.alloc([128, 1024], F32)
    bkbv_pair = ca.alloc([128, 1024], F32)
    kvtmp2 = ca.alloc([128, 1024], F32)
    knsum_col = ca.alloc([128, 16], F32)
    krow_off = (ca.off + 63) & ~63
    krow_sb = ca.alloc([1, 2048], F32)
    vrow32_sb = arena[32:33, krow_off:krow_off + 4096].bitcast(F32)
    brow_sb = ca.alloc([1, 1024], F32)
    r7base = ca.off
    rows7_sb = ca.alloc([1, 7 * 1024], F32)
    ca.off = r7base
    xa = [ca.alloc([128, 1024], F32) for _ in range(2)]
    xn_tm_a = ca.alloc([128, 1024], F32)
    kvtmp = ca.alloc([128, 1024], F32)
    sq_buf = ca.alloc([128, 1024], F32)
    kf_buf = ca.alloc([128, 1024], F32)
    k16_sb = ca.alloc([128, 1024], BF16)
    v16_sb = ca.alloc([128, 1024], BF16)
    q_tile = [ca.alloc([128, 8 * 128], BF16) for _ in range(2)]
    xnfmA = ca.alloc([128, 8 * 128], BF16)
    smA = ca.alloc([128, 16], F32)
    st16 = ca.alloc([128, 64], F32)
    a3buf = ca.alloc([128, 512], F32)
    a3max = ca.alloc([128, 16], F32)
    a3cw = ca.alloc([128, 256], F32)
    abias_sb = ca.alloc([128, 1024], F32)
    logits_all = ca.alloc([128, NT * 64], F32)

    cb = Carver(arena, 0, 143168)
    x_tm = [cb.alloc([128, 3 * 1024], F32) for _ in range(2)]
    xn_tm = cb.alloc([128, 3 * 1024], F32)
    xn_fm = cb.alloc([128, 8 * 384], BF16)
    q_pass = [cb.alloc([128, 8 * 384], BF16) for _ in range(2)]
    att_bf = cb.alloc([128, 8 * 384], BF16)
    h1_bf = cb.alloc([128, 8 * 384], BF16)
    s_tm = cb.alloc([128, 3 * 1024], F32)
    g_tm = cb.alloc([128, 3 * 1024], F32)
    hs_sb = [cb.alloc([128, 2 * 384], BF16) for _ in range(2)]
    hsc_sb = [cb.alloc([128, 2 * 384], BF16) for _ in range(2)]
    cwb_sb = [cb.alloc([128, 384], BF16) for _ in range(2)]
    wu_sb = [cb.alloc([128, 4096], BF16) for _ in range(3)]
    oh_sb = cb.alloc([64, 64 * 128], BF16)

    PS = ctx.enter_context(nc.psum_tensor("PS", [128, 4096], F32)).ap()

    def dma(eng, dst, src, inc, waits=(), future=False):
        em.emit(eng, lambda e, d=dst, s=src: e.dma_start(out=d, in_=s),
                waits=waits, inc=inc, inc_n=16, future=future, is_dma=True)

    dma("sync", bias1_sb, bias1_d, s_small)
    dma("sync", qb_sb, qb_d, s_small)
    dma("sync", p1b_sb, p1b_d, s_small)
    dma("sync", rows7_sb, rows7_d, s_small)
    dma("sync", onesf_sb, onesf_d, s_small)
    dma("sync", onesb_sb, onesb_d, s_small)
    dma("sync", idf_sb, idf_d, s_small)
    dma("sync", tempeo_sb, tempeo_d, s_small)
    dma("sync", rbrow_sb, rb_d, s_small)
    dma("sync", onescol_sb, onescol_d, s_small)
    dma("sync", cols4_sb, cols4_d, s_small)
    em.emit("vector", lambda e: e.memset(abias_sb, 0.0), inc=s_initc)
    em.emit("vector", lambda e: e.memset(eps_col, EPS), inc=s_initc)
    em.emit("vector", lambda e: e.memset(PS[32:33, 3072:4096], 0.0), inc=s_initc)
    EPSC = s_initc.n
    if debug == 1:
        em.emit("vector", lambda e: e.memset(logits_all, 0.0), inc=s_initc)
        em.emit("vector", lambda e: e.memset(stats_m, 0.0), inc=s_initc)
        em.emit("vector", lambda e: e.memset(stats_r, 0.0), inc=s_initc)
        em.emit("vector", lambda e: e.memset(kv_sb, 0.0), inc=s_initc)
        em.emit("vector", lambda e: e.memset(cwT[0:64, :], 0.0), inc=s_initc)
    ab_even = abias_sb.rearrange("p (c x) -> p c x", x=128)[0:64, :, 0:64]
    ab_odd = abias_sb.rearrange("p (c x) -> p c x", x=128)[64:128, :, 64:128]
    ab_src = abias_d.rearrange("(c par) dd e -> par dd c e", par=2)
    dma("sync", ab_even, ab_src[0], s_small, waits=[(s_initc, 1)])
    dma("sync", ab_odd, ab_src[1], s_small)
    SMALL_ALL = s_small.n

    dma("gpsimd", qkv_sb.rearrange("p (kc n) -> p kc n", n=3072),
        qkvwT_d.rearrange("(kc p) n -> p kc n", p=128), s_conv)
    dma("gpsimd", p1_sb.rearrange("p (kc n) -> p kc n", n=1024),
        p1wT_d.rearrange("(kc p) n -> p kc n", p=128), s_conv)
    dma("gpsimd", p2_sb.rearrange("p (kc n) -> p kc n", n=1024),
        p2wT_d.rearrange("(kc p) n -> p kc n", p=128), s_conv)
    dma("gpsimd", router_sb.rearrange("p (kc n) -> p kc n", n=64),
        rw_d.rearrange("(kc p) n -> p kc n", p=128), s_conv)
    dma("gpsimd", b2e_sb, b2e_d, s_conv)

    bc_targets = {4: vg_bc, 5: vb2_bc}
    for r in (4, 5):
        for hh in range(2):
            em.emit("tensor", lambda e, r=r, hh=hh: e.matmul(
                PS[:, (r % 2) * 512:(r % 2) * 512 + 512],
                onesf_sb,
                rows7_sb[:, r * 1024 + hh * 512: r * 1024 + hh * 512 + 512],
                start=True, stop=True, **MMK),
                waits=[(s_small, SMALL_ALL), (s_initc, s_initc.n)], inc=s_init)
            em.emit("vector", lambda e, r=r, hh=hh: e.tensor_copy(
                bc_targets[r][:, hh * 512: hh * 512 + 512],
                PS[:, (r % 2) * 512:(r % 2) * 512 + 512]),
                waits=[(s_init, s_init.n)], inc=s_initc)
    em.emit("scalar", lambda e: e.copy(kb_bf, rows7_sb[:, 0:1024]),
            waits=[(s_small, SMALL_ALL)], inc=s_initcS)
    em.emit("scalar", lambda e: e.copy(vb_bf, rows7_sb[:, 1024:2048]), inc=s_initcS)
    em.emit("scalar", lambda e: e.copy(rb_bf, rbrow_sb), inc=s_initcS)
    em.emit("vector", lambda e: e.tensor_scalar(
        gkgv_pair, vg_bc, cols4_sb[:, 0:1], None, ALU.mult),
        waits=[(s_initc, s_initc.n)], inc=s_initc)
    em.emit("vector", lambda e: e.tensor_scalar(
        bkbv_pair, vb2_bc, cols4_sb[:, 1:2], 2048.0, ALU.mult, ALU.mult),
        inc=s_initc)
    em.emit("tensor", lambda e: e.matmul(PS[0:64, 3 * 512: 3 * 512 + 8],
                                         onesf_sb[0:1, 0:64], tempeo_sb[0:1, 0:8],
                                         start=True, stop=True, **MMK), inc=s_init)
    em.emit("tensor", lambda e: e.matmul(PS[64:128, 3 * 512: 3 * 512 + 8],
                                         onesf_sb[0:1, 0:64], tempeo_sb[0:1, 8:16],
                                         start=True, stop=True, **MMK), inc=s_init)
    em.emit("scalar", lambda e: e.activation(tempcol, PS[:, 3 * 512: 3 * 512 + 8],
                                             AF.Copy, scale=1.0 / 2048.0),
            waits=[(s_init, s_init.n)], inc=s_initcS)
    em.emit("scalar", lambda e: e.copy(p2b_bf, rows7_sb[:, 6 * 1024: 7 * 1024]),
            waits=[(s_small, SMALL_ALL)], inc=s_initcS)
    INIT_EVAC = s_initc.n
    INIT_S = s_initcS.n

    for u in (range(NU) if wconv else []):
        w1src = wall_d[u, 0:262144].rearrange("(kc p jc j) -> p kc jc j",
                                              kc=8, p=128, jc=2, j=128)
        w2src = wall_d[u, 262144:524288].rearrange("(jcc p dd) -> p jcc dd",
                                                   jcc=2, p=128, dd=1024)
        w1dst = wscr_d[u][:, 0:2048].rearrange("p (kc jc j) -> p kc jc j",
                                               kc=8, jc=2, j=128)
        w2dst = wscr_d[u][:, 2048:4096].rearrange("p (jcc dd) -> p jcc dd",
                                                  jcc=2, dd=1024)
        dma("gpsimd", w1dst, w1src, s_wconv)
        dma("gpsimd", w2dst, w2src, s_wconv)

    def emit_xln(j):
        pj = j % 2
        xb = xa[pj]
        B = pj * 8
        xw = [(s_xfreeA, j - 1), (s_sqxA, j - 1)]
        if j < 2:
            xw += [(s_initc, INIT_EVAC), (s_initcS, INIT_S)]
        dma("sync", xb, x_d[j * 128:(j + 1) * 128, :], s_xldA2[pj],
            waits=xw, future=(j > 1))
        em.emit("vector", lambda e, xb=xb, B=B: e.reduce_sum(
            smA[:, B:B + 1], xb, AX.X),
            waits=[(s_xldA2[pj], 16 * (j // 2 + 1)), (s_initc, INIT_EVAC),
                   (s_initcS, INIT_S)], nodrain=True)
        em.emit("vector", lambda e, B=B: e.scalar_tensor_tensor(
            smA[:, B + 3:B + 4], smA[:, B:B + 1], 1.0 / (1024.0 * 1024.0),
            smA[:, B:B + 1], ALU.mult, ALU.mult))
        sqw = [(s_xldA2[pj], 16 * (j // 2 + 1)), (s_initc, EPSC)]
        if j > 0:
            sqw.append((s_trA, 8 * j))
        em.emit("scalar", lambda e, xb=xb, B=B: e.activation(
            xn_tm_a, xb, AF.Square, accum_out=smA[:, B + 1:B + 2]),
            waits=sqw, inc=s_sqxA, tset="arith")
        em.emit("vector", lambda e, B=B: e.tensor_scalar(
            smA[:, B + 2:B + 3], smA[:, B + 1:B + 2], 1.0 / 1024.0, None,
            ALU.mult),
            waits=[(s_sqxA, j + 1)], nodrain=True)
        em.emit("vector", lambda e, j=j, B=B: e.tensor_scalar(
            stats_m[:, j:j + 1], smA[:, B:B + 1], 1.0 / 1024.0, None,
            ALU.mult), nodrain=True)
        em.emit("vector", lambda e, B=B: e.tensor_tensor(
            smA[:, B + 4:B + 5], smA[:, B + 2:B + 3], smA[:, B + 3:B + 4],
            ALU.subtract), inc=s_varA, nodrain=True)
        em.emit("scalar", lambda e, B=B: e.activation(
            smA[:, B + 2:B + 3], smA[:, B + 4:B + 5], AF.Sqrt, bias=eps_col),
            waits=[(s_varA, s_varA.n)], inc=s_sqA, tset="arith")
        em.emit("vector", lambda e, j=j, B=B: e.reciprocal(
            stats_r[:, j:j + 1], smA[:, B + 2:B + 3]),
            waits=[(s_sqA, s_sqA.n)], inc=s_rstdA, nodrain=True)
        em.emit("vector", lambda e, j=j, B=B: e.tensor_tensor(
            smA[:, B + 5:B + 6], stats_m[:, j:j + 1], stats_r[:, j:j + 1],
            ALU.mult))
        em.emit("vector", lambda e, xb=xb, j=j, B=B: e.tensor_scalar(
            xn_tm_a, xb, stats_r[:, j:j + 1], smA[:, B + 5:B + 6],
            ALU.mult, ALU.subtract), inc=s_xnA)
        em.emit("vector", lambda e: e.engine_nop(), inc=s_xfreeA, nodrain=True)

    def emit_A(j):
        tj = j % TPB
        bj = j // TPB
        pj = j % 2
        qtb = q_tile[pj]
        for c in (range(8) if acut >= 2 else []):
            bnk = c % 2
            w = [(s_xnA, j + 1)]
            if c >= 2:
                w.append((s_trcA, 8 * j + c - 1))
            if j == 0:
                w += [(s_initc, INIT_EVAC), (s_initcS, INIT_S)]
            else:
                if c == 0:
                    w += [(s_kevA, j), (s_levA, j)]
                    if tj == 0:
                        w.append((s_kvfA, bj))
            em.emit("tensor", lambda e, c=c, bnk=bnk: e.transpose(
                PS[:, bnk * 512 + (c // 2) * 128: bnk * 512 + (c // 2) * 128 + 128],
                xn_tm_a[:, c * 128:(c + 1) * 128], idf_sb),
                waits=w, inc=s_trA)
            em.emit("scalar", lambda e, c=c, bnk=bnk: e.activation(
                xnfmA[:, c * 128:(c + 1) * 128],
                PS[:, bnk * 512 + (c // 2) * 128: bnk * 512 + (c // 2) * 128 + 128],
                AF.Copy),
                waits=[(s_trA, s_trA.n)], inc=s_trcA)

        for kc in (range(8) if acut >= 3 else []):
            for mc in range(8):
                bnk = 2 + mc % 2
                reg = (mc // 2) * 128
                w = []
                if mc == 0:
                    w.append((s_trcA, 8 * j + kc + 1))
                    if kc == 0:
                        w.append((s_conv, 80))
                        if j == 0:
                            w += [(s_initc, INIT_EVAC), (s_initcS, INIT_S)]
                        else:
                            w += [(s_vevA, j), (s_qevA, 8 * j)]
                em.emit("tensor", lambda e, kc=kc, mc=mc, bnk=bnk, reg=reg: e.matmul(
                    PS[:, bnk * 512 + reg: bnk * 512 + reg + 128],
                    qkv_sb[:, kc * 3072 + mc * 128: kc * 3072 + mc * 128 + 128],
                    xnfmA[:, kc * 128:(kc + 1) * 128],
                    start=(kc == 0 and mc < 2), stop=(kc == 7), **MMK),
                    waits=w, inc=(s_qpeA if (kc == 7 and mc == 7) else None))
        for mc in (range(8) if acut >= 3 else []):
            bnk = 2 + mc % 2
            reg = (mc // 2) * 128
            em.emit("scalar", lambda e, mc=mc, bnk=bnk, reg=reg, qtb=qtb: e.activation(
                qtb[:, mc * 128:(mc + 1) * 128],
                PS[:, bnk * 512 + reg: bnk * 512 + reg + 128],
                AF.Identity, bias=qb_sb[:, mc:mc + 1]),
                waits=[(s_qpeA, j + 1),
                       (s_qsp2[pj], 16 * ((j - 2) // 2 + 1) if j >= 2 else 0)],
                inc=s_qevA)
        if acut >= 3:
            em.emit("scalar", lambda e, j=j, qtb=qtb: e.dma_start(
                out=qscr_d[j].rearrange("c p t -> p c t"),
                in_=qtb.rearrange("p (c t) -> p c t", t=128)),
                waits=[(s_qevA, 8 * (j + 1))], inc=s_qsp2[pj], inc_n=16,
                is_dma=True)

    for i in range(NT_):
        ti = i % TPB
        bi = i // TPB
        par = i % 2
        xbuf = xa[par]
        if i == 0:
            emit_xln(0)
            emit_A(0)

        for half in (range(2) if acut >= 4 else []):
            for kc in range(8):
                w = []
                if half == 0 and kc == 0:
                    w = [(s_trcA, 8 * i + 8)]
                    if i > 0:
                        w.append((s_kevA, i))
                em.emit("tensor", lambda e, half=half, kc=kc: e.matmul(
                    PS[:, half * 512: half * 512 + 512],
                    xnfmA[:, kc * 128:(kc + 1) * 128],
                    qkv_sb[:, kc * 3072 + 1024 + half * 512:
                           kc * 3072 + 1024 + half * 512 + 512],
                    start=(kc == 0), stop=False, **MMK),
                    waits=w, inc=None)
            em.emit("tensor", lambda e, half=half: e.matmul(
                PS[:, half * 512: half * 512 + 512],
                onesb_sb,
                kb_bf[:, half * 512: half * 512 + 512],
                start=False, stop=True, **MMK),
                waits=[(s_initcS, INIT_S)], inc=(s_kpeA if half == 1 else None))
        for half in (range(2) if acut >= 5 else []):
            for kc in range(8):
                w = []
                if half == 0 and kc == 0:
                    w = [(s_qevA, 8 * (i + 1))]
                em.emit("tensor", lambda e, half=half, kc=kc: e.matmul(
                    PS[:, (2 + half) * 512: (2 + half) * 512 + 512],
                    xnfmA[:, kc * 128:(kc + 1) * 128],
                    qkv_sb[:, kc * 3072 + 2048 + half * 512:
                           kc * 3072 + 2048 + half * 512 + 512],
                    start=(kc == 0), stop=False, **MMK),
                    waits=w, inc=None)
            em.emit("tensor", lambda e, half=half: e.matmul(
                PS[:, (2 + half) * 512: (2 + half) * 512 + 512],
                onesb_sb,
                vb_bf[:, half * 512: half * 512 + 512],
                start=False, stop=True, **MMK),
                waits=[], inc=(s_vpeA if half == 1 else None))

        if i + 1 < NT_:
            emit_xln(i + 1)

        def headnorm(srcb, out16, sem_ev, sem_16, pe_sem, copy_extra, sq_extra):
            for half in range(2):
                em.emit("scalar", lambda e, half=half, srcb=srcb: e.activation(
                    kf_buf[:, half * 512: half * 512 + 512],
                    PS[:, (srcb + half) * 512: (srcb + half) * 512 + 512],
                    AF.Copy),
                    waits=([(pe_sem, i + 1)] + copy_extra) if half == 0 else [],
                    inc=(sem_ev if half == 1 else None))
            em.emit("scalar", lambda e: e.activation(sq_buf, kf_buf, AF.Square),
                    waits=sq_extra, inc=s_sqA, tset="arith")
            kf3 = kf_buf.rearrange("p (h dd) -> p h dd", dd=64)
            sq3 = sq_buf.rearrange("p (h dd) -> p h dd", dd=64)
            em.emit("vector", lambda e, kf3=kf3: e.tensor_reduce(
                st16[:, 0:16], kf3, AX.X, ALU.add),
                waits=[(sem_ev, i + 1)], nodrain=True)
            em.emit("vector", lambda e, sq3=sq3: e.tensor_reduce(
                st16[:, 32:48], sq3, AX.X, ALU.add),
                waits=[(s_sqA, s_sqA.n)], nodrain=True)
            em.emit("vector", lambda e: e.tensor_scalar(
                st16[:, 16:32], st16[:, 0:16], 1.0 / 64.0, None, ALU.mult),
                nodrain=True)
            em.emit("vector", lambda e: e.tensor_scalar(
                st16[:, 48:64], st16[:, 32:48], 1.0 / 64.0, None, ALU.mult),
                nodrain=True)
            em.emit("vector", lambda e: e.tensor_tensor(
                st16[:, 0:16], st16[:, 16:32], st16[:, 16:32], ALU.mult),
                nodrain=True)
            em.emit("vector", lambda e: e.tensor_tensor(
                st16[:, 32:48], st16[:, 48:64], st16[:, 0:16], ALU.subtract),
                inc=s_varA)
            em.emit("scalar", lambda e: e.activation(st16[:, 48:64], st16[:, 32:48],
                                                     AF.Sqrt, bias=eps_col),
                    waits=[(s_varA, s_varA.n)], inc=s_sqA, tset="arith")
            em.emit("vector", lambda e: e.reciprocal(st16[:, 0:16],
                                                     st16[:, 48:64]),
                    waits=[(s_sqA, s_sqA.n)], nodrain=True)
            em.emit("vector", lambda e, kf3=kf3, sq3=sq3: e.tensor_tensor(
                sq3, kf3,
                st16[:, 16:32].unsqueeze(2).to_broadcast([128, 16, 64]),
                ALU.subtract), inc=s_hsubA, nodrain=True)
            em.emit("vector", lambda e, o=out16, sq3=sq3: e.tensor_tensor(
                o.rearrange("p (h dd) -> p h dd", dd=64), sq3,
                st16[:, 0:16].unsqueeze(2).to_broadcast([128, 16, 64]), ALU.mult),
                waits=[(s_sumA, i)], inc=sem_16)

        if acut >= 4:
            headnorm(0, k16_sb, s_kevA, s_k16A, s_kpeA, [], [])
        if acut >= 5:
            headnorm(2, v16_sb, s_vevA, s_v16A, s_vpeA,
                     [(s_hsubA, 2 * i + 1)], [(s_k16A, i + 1)])

        for kc in (range(8) if acut >= 7 else []):
            w = []
            if kc == 0:
                w = [(s_kevA, i + 1), (s_conv, 80)]
            em.emit("tensor", lambda e, kc=kc: e.matmul(
                PS[:, 0:64],
                xnfmA[:, kc * 128:(kc + 1) * 128],
                router_sb[:, kc * 64:(kc + 1) * 64],
                start=(kc == 0), stop=False, **MMK),
                waits=w, inc=None)
        if acut >= 7:
            em.emit("tensor", lambda e: e.matmul(
                PS[:, 0:64], onesb_sb, rb_bf,
                start=False, stop=True, **MMK),
                waits=[], inc=s_lpeA)
            em.emit("scalar", lambda e, i=i: e.activation(
                logits_all[:, i * 64:(i + 1) * 64], PS[:, 0:64], AF.Copy),
                waits=[(s_lpeA, i + 1)], inc=s_levA)

        if ti != TPB - 1 and i + 1 < NT_:
            emit_A(i + 1)


        for c in (range(8) if acut >= 6 else []):
            w = []
            if c == 0:
                w = [(s_k16A, i + 1), (s_v16A, i + 1)]
                if ti == 0:
                    w.append((s_kvfA, bi))
            em.emit("tensor", lambda e, c=c, ti=ti: e.matmul(
                PS[:, 4 * 512 + c * 128: 4 * 512 + c * 128 + 128],
                k16_sb[:, c * 128:(c + 1) * 128],
                v16_sb[:, c * 128:(c + 1) * 128],
                start=(ti == 0 and c % 4 == 0), stop=(ti == TPB - 1), **MMK),
                waits=w, inc=(s_kvpeA if c == 7 else None))
        if acut >= 6:
            for half in range(2):
                em.emit("tensor", lambda e, half=half, ti=ti: e.matmul(
                    PS[0:1, 3072 + half * 512: 3072 + half * 512 + 512],
                    onescol_sb,
                    k16_sb[:, half * 512: half * 512 + 512],
                    start=(ti == 0), stop=(ti == TPB - 1), **MMK),
                    waits=[], inc=None)
            for half in range(2):
                em.emit("tensor", lambda e, half=half, ti=ti: e.matmul(
                    PS[32:33, 3072 + half * 512: 3072 + half * 512 + 512],
                    onescol_sb,
                    v16_sb[:, half * 512: half * 512 + 512],
                    start=(ti == 0), stop=(ti == TPB - 1), **MMK),
                    waits=[], inc=(s_sumA if half == 1 else None))

        if ti == TPB - 1 and acut >= 6:
            em.emit("scalar", lambda e: e.activation(
                krow_sb[0:1, 0:1024], PS[0:1, 3072:4096], AF.Copy),
                waits=[(s_sumA, TPB * (bi + 1))], inc=s_fA)
            em.emit("scalar", lambda e: e.activation(
                vrow32_sb, PS[32:33, 3072:4096], AF.Copy),
                inc=s_fA)
            FA0 = s_fA.n
            em.emit("sync", lambda e: e.dma_start(
                out=krow_sb[0:1, 1024:2048], in_=vrow32_sb),
                waits=[(s_fA, FA0)], inc=s_fM, inc_n=16, is_dma=True)
            FM0 = s_fM.n
            em.emit("vector", lambda e: e.tensor_tensor(
                brow_sb, krow_sb[0:1, 1024:2048],
                vg_bc[0:1, :], ALU.mult),
                waits=[(s_fM, FM0)], inc=s_fD)
            FD0 = s_fD.n
            for c in range(8):
                em.emit("tensor", lambda e, c=c: e.matmul(
                    PS[:, c:c + 1],
                    krow_sb[0:1, c * 128:(c + 1) * 128],
                    onesf_sb[0:1, 0:1],
                    start=True, stop=True, **MMK),
                    waits=[(s_fA, FA0), (s_levA, i + 1),
                           (s_kevA, i + 1)] if c == 0 else [],
                    inc=(s_fP if c == 7 else None))
            FP0 = s_fP.n
            for half in range(2):
                em.emit("tensor", lambda e, half=half: e.matmul(
                    PS[:, 512 + half * 512: 512 + half * 512 + 512],
                    onesf_sb, brow_sb[0:1, half * 512: half * 512 + 512],
                    start=True, stop=True, **MMK),
                    waits=[(s_fD, FD0), (s_vevA, i + 1)] if half == 0 else [],
                    inc=(s_fP if half == 1 else None))
            FP1 = s_fP.n
            em.emit("scalar", lambda e: e.activation(
                knsum_col[:, 0:8], PS[:, 0:8], AF.Copy),
                waits=[(s_fP, FP0)], inc=s_fA)
            FA1 = s_fA.n
            em.emit("vector", lambda e: e.tensor_scalar(
                knsum_col[:, 8:16], knsum_col[:, 0:8], cols4_sb[:, 0:1],
                None, ALU.mult), waits=[(s_fA, FA1)])
            em.emit("vector", lambda e: e.tensor_tensor(
                kvtmp, PS[:, 2048:3072], gkgv_pair, ALU.mult),
                waits=[(s_kvpeA, i + 1), (s_initc, INIT_EVAC),
                       (s_initcS, INIT_S)])
            for c in range(8):
                if kvterms >= 2:
                    em.emit("vector", lambda e, c=c: e.scalar_tensor_tensor(
                        kvtmp2[:, c * 128:(c + 1) * 128],
                        vb2_bc[:, c * 128:(c + 1) * 128],
                        knsum_col[:, 8 + c: 9 + c],
                        kvtmp[:, c * 128:(c + 1) * 128],
                        ALU.mult, ALU.add))
                else:
                    em.emit("vector", lambda e, c=c: e.tensor_copy(
                        kvtmp2[:, c * 128:(c + 1) * 128],
                        kvtmp[:, c * 128:(c + 1) * 128]))
            for c in range(8):
                if kvterms >= 3:
                    em.emit("vector", lambda e, c=c: e.scalar_tensor_tensor(
                        kvtmp[:, c * 128:(c + 1) * 128],
                        PS[:, 512 + c * 128: 512 + c * 128 + 128],
                        cols4_sb[:, 1:2],
                        kvtmp2[:, c * 128:(c + 1) * 128],
                        ALU.mult, ALU.add),
                        waits=[(s_fP, FP1)] if c == 0 else [])
                else:
                    em.emit("vector", lambda e, c=c: e.tensor_copy(
                        kvtmp[:, c * 128:(c + 1) * 128],
                        kvtmp2[:, c * 128:(c + 1) * 128]),
                        waits=[(s_fP, FP1)] if c == 0 else [])
            if kvterms >= 4:
                em.emit("vector", lambda e: e.tensor_tensor(
                    kvtmp2, kvtmp, bkbv_pair, ALU.add))
            else:
                em.emit("vector", lambda e: e.tensor_copy(kvtmp2, kvtmp))
            for c in range(8):
                em.emit("vector", lambda e, c=c, bi=bi: e.scalar_tensor_tensor(
                    kv_sb[:, bi * 1024 + c * 128: bi * 1024 + (c + 1) * 128],
                    kvtmp2[:, c * 128:(c + 1) * 128],
                    tempcol[:, c:c + 1],
                    abias_sb[:, c * 128:(c + 1) * 128],
                    ALU.mult, ALU.add),
                    waits=[(s_small, SMALL_ALL)] if c == 0 else [],
                    inc=(s_kvfA if c == 7 else None))

        if ti == TPB - 1 and i + 1 < NT_:
            emit_A(i + 1)


    em.emit("vector", lambda e: e.memset(cwT[64:66, :], 1.0), inc=s_cwT)

    def a3_bufs(i):
        par = i % 2
        return dict(
            par=par,
            zs=logits_all[:, i * 64:(i + 1) * 64],
            zc=a3buf[:, par * 256: par * 256 + 64],
            ez=a3buf[:, par * 256 + 64: par * 256 + 128],
            msk=a3buf[:, par * 256 + 128: par * 256 + 192],
            emu=a3buf[:, par * 256 + 192: par * 256 + 256],
            mx=a3max[:, par * 8: par * 8 + 8],
            ssum=smA[:, par * 8 + 3: par * 8 + 4],
            srec=smA[:, par * 8 + 4: par * 8 + 5],
            cwcols=a3cw[:, par * 128: par * 128 + 64])

    a3_list = list(range(NT_)) if acut >= 8 else []
    assert len(a3_list) % 2 == 0, "A3 pair-zip needs an even tile count"
    for i0 in range(0, len(a3_list) - 1, 2):
        pr = [a3_bufs(i0), a3_bufs(i0 + 1)]
        for s, i in ((0, i0), (1, i0 + 1)):
            b = pr[s]
            w = [(s_levA, i + 1)]
            if i >= 2:
                w.append((s_a3e, i - 1))
            em.emit("vector", lambda e, b=b: e.max(out=b["mx"], in_=b["zs"]),
                    waits=w, nodrain=True)
        for s in range(2):
            b = pr[s]
            em.emit("vector", lambda e, b=b: e.tensor_scalar(
                b["zc"], b["zs"], b["mx"][:, 0:1], None, ALU.subtract),
                nodrain=True)
        dvals = []
        for s in range(2):
            b = pr[s]
            em.emit("vector", lambda e, b=b: e.tensor_scalar(
                b["msk"], b["zs"], b["mx"][:, 7:8], None, ALU.is_ge),
                inc=s_a3d, nodrain=True)
            dvals.append(s_a3d.n)
        evals = []
        for s in range(2):
            b = pr[s]
            em.emit("scalar", lambda e, b=b: e.activation(b["ez"], b["zc"],
                                                          AF.Exp),
                    waits=[(s_a3d, dvals[s])], inc=s_a3e, tset="exp")
            evals.append(s_a3e.n)
        for s in range(2):
            b = pr[s]
            em.emit("vector", lambda e, b=b: e.tensor_tensor(
                b["emu"], b["ez"], b["msk"], ALU.mult),
                waits=[(s_a3e, evals[s])], nodrain=True)
        for s in range(2):
            b = pr[s]
            em.emit("vector", lambda e, b=b: e.reduce_sum(
                b["ssum"], b["emu"], AX.X), nodrain=True)
        for s in range(2):
            b = pr[s]
            em.emit("vector", lambda e, b=b: e.reciprocal(
                b["srec"], b["ssum"]), nodrain=True)
        for s, i in ((0, i0), (1, i0 + 1)):
            b = pr[s]
            w = [(s_a3t, i - 1)] if i >= 2 else []
            em.emit("vector", lambda e, b=b: e.tensor_scalar(
                b["cwcols"], b["emu"], b["srec"], None, ALU.mult),
                waits=w, inc=s_a3p, nodrain=True)
        for s, i in ((0, i0), (1, i0 + 1)):
            b = pr[s]
            bnk = 6 + b["par"]
            em.emit("tensor", lambda e, b=b, bnk=bnk: e.transpose(
                PS[0:64, bnk * 512: bnk * 512 + 128], b["cwcols"], idf_sb),
                waits=[(s_a3p, s_a3p.n - (1 - s)), (s_cwT, s_cwT.n),
                       (s_kvfA, NB_)],
                inc=s_a3t)
            em.emit("scalar", lambda e, i=i, bnk=bnk: e.activation(
                cwT[0:64, i * 128:(i + 1) * 128],
                PS[0:64, bnk * 512: bnk * 512 + 128], AF.Copy),
                waits=[(s_a3t, s_a3t.n)], inc=s_cwT)
    CWT_DONE = s_cwT.n

    QSP_ALL = [(s_qsp2[0], s_qsp2[0].n), (s_qsp2[1], s_qsp2[1].n)]
    if NP_ > 0:
        dma("gpsimd", oh_sb, oh_d, s_ohld, waits=[(s_cwT, CWT_DONE)], future=True)

    c_cwb = 0
    for p, tiles in enumerate(PT):
        ntb = len(tiles)
        T = ntb * 128
        par = p % 2
        xb = x_tm[par]
        qb_ = q_pass[par]
        t0 = tiles[0]

        def x_load(pp):
            xbp = x_tm[pp % 2]
            tl = PT[pp]
            em.emit("gpsimd", lambda e, xbp=xbp, tp0=tl[0], nn=len(tl):
                    e.dma_start(
                out=xbp.rearrange("p (tb dd) -> p tb dd", dd=1024)[:, 0:nn],
                in_=x_d[tp0 * 128:(tp0 + nn) * 128, :].rearrange(
                    "(tb p) dd -> p tb dd", p=128)),
                waits=[(s_odve, pp - 1), (s_ohld, 16)], inc=s_xldB, inc_n=16,
                is_dma=True, future=True)
        if p == 0:
            x_load(0)
        if p + 1 < NP_:
            x_load(p + 1)
        for tb in range(ntb):
            em.emit("gpsimd", lambda e, qb_=qb_, t0=t0, tb=tb: e.dma_start(
                out=qb_.rearrange("p (c x) -> p c x", x=384)[:, :, tb * 128:(tb + 1) * 128],
                in_=qscr_d[t0 + tb].rearrange("c p t -> p c t")),
                waits=QSP_ALL + [(s_attpe, p - 1)], inc=s_qldB,
                inc_n=16, future=True)
        QLD = s_qldB.n

        for u in range(NU):
            g = p * NU + u
            qn = "gpsimd" if g % 2 == 1 else "sync"
            w = [(s_wconv, (32 * NU) if wconv else 0), (s_mm2, g - 2)]
            if p == 0 and u < 3:
                w.append((s_cwT, CWT_DONE))
                w += QSP_ALL
            dma(qn, wu_sb[g % 3], wscr_d[u], s_wld3[g % 2], waits=w, future=True)

        def xn_opB(pp, extra):
            tl = PT[pp]
            xbp = x_tm[pp % 2]

            def op(e, tl=tl, xbp=xbp):
                ins = None
                for j, t in enumerate(tl):
                    ins = e.tensor_scalar(
                        xn_tm[:, j * 1024:(j + 1) * 1024],
                        xbp[:, j * 1024:(j + 1) * 1024],
                        stats_m[:, t:t + 1], stats_r[:, t:t + 1],
                        ALU.subtract, ALU.mult)
                return ins
            em.emit("vector", op,
                    waits=[(s_xldB, 16 * (pp + 1)), (s_cwT, CWT_DONE)] + extra,
                    inc=s_xnB, nodrain=True)
        if p == 0:
            xn_opB(0, [])

        idx = 0
        TRB0 = s_trcB.n
        for kc in range(8):
            for tb in range(ntb):
                tens = idx % 3
                half = (idx // 3) % 2
                col = tens * 1024 + half * 512
                w = [(s_xnB, p + 1)]
                if idx >= 6:
                    w.append((s_trcB, TRB0 + idx - 5))
                if idx == 0:
                    if p == 0:
                        w += [(s_kvfA, NB_), (s_cwT, CWT_DONE)]
                    else:
                        w.append((s_sev, p))
                em.emit("tensor", lambda e, kc=kc, tb=tb, col=col: e.transpose(
                    PS[:, col: col + 128],
                    xn_tm[:, tb * 1024 + kc * 128: tb * 1024 + kc * 128 + 128],
                    idf_sb), waits=w, inc=s_trB)
                em.emit("scalar", lambda e, kc=kc, tb=tb, col=col: e.activation(
                    xn_fm[:, kc * 384 + tb * 128: kc * 384 + tb * 128 + 128],
                    PS[:, col: col + 128], AF.Copy),
                    waits=[(s_trB, s_trB.n)], inc=s_trcB)
                idx += 1
        TRC = s_trcB.n

        def cwb_mm(u_next):
            nonlocal c_cwb
            g_loc = p * NU + u_next
            em.emit("tensor", lambda e, u=u_next, t0=t0, T=T: e.matmul(
                PS[:, 7 * 512: 7 * 512 + T],
                oh_sb[:, u * 128:(u + 1) * 128],
                cwT[0:64, t0 * 128: t0 * 128 + T],
                start=True, stop=True, **MMK),
                waits=[(s_silu, 2 * g_loc), (s_ohld, 16)], inc=s_cwbpe)
            c_cwb += 1
            em.emit("scalar", lambda e, gl=g_loc, T=T: e.activation(
                cwb_sb[gl % 2][:, 0:T], PS[:, 7 * 512: 7 * 512 + T], AF.Copy),
                waits=[(s_cwbpe, s_cwbpe.n)], inc=s_cwbc)

        cwb_mm(0)

        dep_cnt = {}

        def mm1_stage(v, jc):
            gv = p * NU + v
            routed_v = v < 64
            wuv = wu_sb[gv % 3]
            hsbv = hs_sb[gv % 2]
            hscbv = hsc_sb[gv % 2] if routed_v else hs_sb[gv % 2]
            bnk = 6 + jc
            for kc in range(8):
                w = []
                if jc == 0:
                    w.append((s_trcB, TRB0 + ntb * (kc + 1)))
                if kc == 0:
                    w.append((s_wld3[gv % 2], 16 * (gv // 2 + 1)))
                    if jc == 0:
                        w.append((s_silu, max(0, 2 * gv - 1)))
                    else:
                        w.append((s_cwbc, c_cwb))
                        w.append((s_silu, 2 * gv))
                em.emit("tensor", lambda e, wu=wuv, jc=jc, kc=kc, bnk=bnk, T=T:
                        e.matmul(
                            PS[:, bnk * 512: bnk * 512 + T],
                            wu[:, (kc * 2 + jc) * 128:(kc * 2 + jc) * 128 + 128],
                            xn_fm[:, kc * 384: kc * 384 + T],
                            start=(kc == 0), stop=(kc == 7), **MMK),
                        waits=w, inc=(s_peh if kc == 7 else None), future=True)
            em.emit("scalar", lambda e, hsb=hsbv, jc=jc, u=v, bnk=bnk, T=T:
                    e.activation(
                        hsb[:, jc * 384: jc * 384 + T],
                        PS[:, bnk * 512: bnk * 512 + T], AF.Silu,
                        bias=bias1_sb[:, 2 * u + jc: 2 * u + jc + 1]),
                    waits=[(s_peh, s_peh.n)], inc=s_silu, tset="silu")

        def hsc_stage(v, jc):
            gv = p * NU + v
            if v < 64:
                em.emit("vector", lambda e, hscb=hsc_sb[gv % 2],
                        hsb=hs_sb[gv % 2], jc=jc, g=gv, T=T:
                        e.tensor_tensor(
                            hscb[:, jc * 384: jc * 384 + T],
                            hsb[:, jc * 384: jc * 384 + T],
                            cwb_sb[g % 2][:, 0:T], ALU.mult),
                        waits=[(s_silu, s_silu.n), (s_cwbc, c_cwb)],
                        inc=s_hsc, nodrain=True)
                dep_cnt[(v, jc)] = (s_hsc, s_hsc.n)
            else:
                dep_cnt[(v, jc)] = (s_silu, s_silu.n)

        mm1_stage(0, 0)
        hsc_stage(0, 0)
        mm1_stage(0, 1)
        hsc_stage(0, 1)

        for u in range(NU):
            g = p * NU + u
            routed = u < 64
            wu = wu_sb[g % 3]
            v = u + 1

            if v < NU:
                mm1_stage(v, 0)
                if v < 64:
                    cwb_mm(v)
                hsc_stage(v, 0)

            def mm2_group(jc):
                dep = dep_cnt[(u, jc)]
                cnt = 0
                for tb in range(ntb):
                    for dh in range(2):
                        col = tb * 1024 + dh * 512
                        w = [dep] if cnt == 0 else []
                        em.emit("tensor", lambda e, jc=jc, tb=tb, dh=dh,
                                col=col, wu=wu, u=u,
                                hscb=(hsc_sb[g % 2] if routed else hs_sb[g % 2]):
                                e.matmul(
                                    PS[:, col: col + 512],
                                    hscb[:, jc * 384 + tb * 128:
                                         jc * 384 + tb * 128 + 128],
                                    wu[:, 2048 + jc * 1024 + dh * 512:
                                       2048 + jc * 1024 + dh * 512 + 512],
                                    start=(u == 0 and jc == 0), stop=False, **MMK),
                                waits=w,
                                inc=(s_mm2 if (jc == 1 and cnt == 2 * ntb - 1)
                                     else None))
                        cnt += 1
            mm2_group(0)
            if v < NU:
                mm1_stage(v, 1)
                hsc_stage(v, 1)
            mm2_group(1)

        for tb in range(ntb):
            for dh in range(2):
                col = tb * 1024 + dh * 512
                w = [(s_conv, 80)] if (tb == 0 and dh == 0) else []
                em.emit("tensor", lambda e, tb=tb, dh=dh, col=col, t0=t0: e.matmul(
                    PS[:, col: col + 512],
                    cwT[:, (t0 + tb) * 128: (t0 + tb) * 128 + 128],
                    b2e_sb[:, dh * 512: dh * 512 + 512],
                    start=False, stop=True, **MMK),
                    waits=w,
                    inc=(s_rdone if (tb == ntb - 1 and dh == 1) else None))

        for tb in range(ntb):
            for dh in range(2):
                col = tb * 1024 + dh * 512
                w = [(s_rdone, p + 1)]
                if tb == 0 and dh == 0:
                    w.append((s_odve, p))
                em.emit("scalar", lambda e, col=col: e.activation(
                    g_tm[:, col: col + 512], PS[:, col: col + 512], AF.Silu),
                    waits=w, inc=s_gev, tset="silu")
        GEV = s_gev.n

        def att_col(c):
            if c < 6:
                return (c // 2) * 1024 + (c % 2) * 512
            return c * 512
        att_items = []
        for c in range(8):
            for half in range(2):
                segs = []
                sstart = 0
                while sstart < ntb:
                    bb = (t0 + sstart) // TPB
                    send = sstart
                    while send < ntb and (t0 + send) // TPB == bb:
                        send += 1
                    segs.append((sstart, send, bb))
                    sstart = send
                for si, seg in enumerate(segs):
                    att_items.append((c, half, seg))
        for n_it, (c, half, (sa, sb2, bb)) in enumerate(att_items):
            col = att_col(c)
            hrow = half * 64
            w = []
            if n_it == 0:
                w = [(s_gev, GEV), (s_qldB, QLD),
                     (s_silu, s_silu.n), (s_cwbc, c_cwb), (s_kvfA, NB_)]
            em.emit("tensor", lambda e, c=c, hrow=hrow, sa=sa, sb2=sb2, bb=bb,
                    col=col, qb_=qb_: e.matmul(
                        PS[hrow:hrow + 64, col + sa * 128: col + sb2 * 128],
                        kv_sb[hrow:hrow + 64,
                              bb * 1024 + c * 128 + hrow:
                              bb * 1024 + c * 128 + hrow + 64],
                        qb_[hrow:hrow + 64, c * 384 + sa * 128: c * 384 + sb2 * 128],
                        start=True, stop=True, **MMK),
                    waits=w,
                    inc=(s_attpe if n_it == len(att_items) - 1 else None))

        for c in range(8):
            col = att_col(c)
            em.emit("vector", lambda e, c=c, col=col, T=T: e.tensor_tensor(
                att_bf[:, c * 384: c * 384 + T],
                PS[:, col: col + T],
                xn_fm[:, c * 384: c * 384 + T], ALU.add),
                waits=[(s_attpe, p + 1)], inc=s_attev, nodrain=True)

        for c in range(8):
            col = att_col(c)
            for kc in range(8):
                w = [(s_attev, 8 * p + max(kc, c) + 1)]
                if c == 0 and kc == 0 and p == 0:
                    w.append((s_conv, 80))
                em.emit("tensor", lambda e, c=c, kc=kc, col=col, T=T: e.matmul(
                    PS[:, col: col + T],
                    p1_sb[:, kc * 1024 + c * 128: kc * 1024 + c * 128 + 128],
                    att_bf[:, kc * 384: kc * 384 + T],
                    start=(kc == 0), stop=(kc == 7), **MMK),
                    waits=w, inc=(s_p1pe if kc == 7 else None))
            em.emit("scalar", lambda e, c=c, col=col, T=T: e.activation(
                h1_bf[:, c * 384: c * 384 + T], PS[:, col: col + T],
                AF.Gelu, bias=p1b_sb[:, c:c + 1]),
                waits=[(s_p1pe, s_p1pe.n)], inc=s_gelev, tset="gelu")

        P2BASE = s_p2pe.n
        for tb in range(ntb):
            for hc in range(8):
                for dh in range(2):
                    w = []
                    if dh == 0:
                        w = [(s_gelev, 8 * p + max(hc + 1, 2 * tb + 2))]
                        if hc == 0 and tb == 0 and p == 0:
                            w.append((s_conv, 80))
                    em.emit("tensor", lambda e, tb=tb, hc=hc, dh=dh: e.matmul(
                        PS[:, tb * 1024 + dh * 512: tb * 1024 + dh * 512 + 512],
                        h1_bf[:, hc * 384 + tb * 128: hc * 384 + tb * 128 + 128],
                        p2_sb[:, hc * 1024 + dh * 512: hc * 1024 + dh * 512 + 512],
                        start=(hc == 0), stop=False, **MMK),
                        waits=w, inc=None)
            for dh in range(2):
                em.emit("tensor", lambda e, tb=tb, dh=dh: e.matmul(
                    PS[:, tb * 1024 + dh * 512: tb * 1024 + dh * 512 + 512],
                    onesb_sb,
                    p2b_bf[:, dh * 512: dh * 512 + 512],
                    start=False, stop=True, **MMK),
                    waits=[(s_initc, INIT_EVAC)] if (tb == 0 and dh == 0) else [],
                    inc=(s_p2pe if dh == 1 else None))

        for tb in range(ntb):
            w = [(s_p2pe, P2BASE + tb + 1)]
            if tb == 0:
                w.append((s_odma, 16 * p))
            em.emit("vector", lambda e, tb=tb: e.tensor_tensor(
                s_tm[:, tb * 1024:(tb + 1) * 1024],
                PS[:, tb * 1024:(tb + 1) * 1024],
                xn_tm[:, tb * 1024:(tb + 1) * 1024], ALU.add),
                waits=w, inc=(s_sev if tb == ntb - 1 else None), nodrain=True)
        if p + 1 < NP_:
            xn_opB(p + 1, [(s_trB, s_trB.n)])
        for tb in range(ntb):
            em.emit("vector", lambda e, tb=tb: e.tensor_tensor(
                g_tm[:, tb * 1024:(tb + 1) * 1024],
                s_tm[:, tb * 1024:(tb + 1) * 1024],
                g_tm[:, tb * 1024:(tb + 1) * 1024], ALU.mult),
                waits=[(s_gev, GEV)] if tb == 0 else [],
                nodrain=(ntb > 1 or tb > 0))
        for tb in range(ntb):
            em.emit("vector", lambda e, tb=tb, xb=xb: e.tensor_tensor(
                s_tm[:, tb * 1024:(tb + 1) * 1024],
                g_tm[:, tb * 1024:(tb + 1) * 1024],
                xb[:, tb * 1024:(tb + 1) * 1024], ALU.add),
                inc=(s_odve if tb == ntb - 1 else None),
                nodrain=(ntb > 1 or tb > 0))

        em.emit("gpsimd", lambda e, t0=t0, ntb=ntb: e.dma_start(
            out=out_d[t0 * 128:(t0 + ntb) * 128, :].rearrange(
                "(tb p) dd -> p tb dd", p=128),
            in_=s_tm.rearrange("p (tb dd) -> p tb dd", dd=1024)[:, 0:ntb]),
            waits=[(s_odve, p + 1)], inc=s_odma, inc_n=16, is_dma=True)

    if debug == 2 and NP_ > 0:
        s_dbg2 = S("dbg2")
        dw2 = [(s_odma, 16 * NP_)]
        for c3 in range(3):
            em.emit("gpsimd", lambda e, c3=c3: e.dma_start(
                out=out_d[4096 + c3 * 128: 4096 + (c3 + 1) * 128, :],
                in_=g_tm[:, c3 * 1024:(c3 + 1) * 1024]),
                waits=dw2, inc=s_dbg2, inc_n=16)
        for c3 in range(3):
            em.emit("gpsimd", lambda e, c3=c3: e.dma_start(
                out=out_d[4480 + c3 * 128: 4480 + (c3 + 1) * 128, :],
                in_=att_bf[:, c3 * 1024:(c3 + 1) * 1024]),
                inc=s_dbg2, inc_n=16)
        for c3 in range(3):
            em.emit("gpsimd", lambda e, c3=c3: e.dma_start(
                out=out_d[4864 + c3 * 128: 4864 + (c3 + 1) * 128, :],
                in_=h1_bf[:, c3 * 1024:(c3 + 1) * 1024]),
                inc=s_dbg2, inc_n=16)
        for c3 in range(3):
            em.emit("gpsimd", lambda e, c3=c3: e.dma_start(
                out=out_d[5248 + c3 * 128: 5248 + (c3 + 1) * 128, :],
                in_=xn_fm[:, c3 * 1024:(c3 + 1) * 1024]),
                inc=s_dbg2, inc_n=16)
        for c3 in range(3):
            em.emit("gpsimd", lambda e, c3=c3: e.dma_start(
                out=out_d[5632 + c3 * 128: 5632 + (c3 + 1) * 128, :],
                in_=q_pass[0][:, c3 * 1024:(c3 + 1) * 1024]),
                inc=s_dbg2, inc_n=16)
        for c3 in range(3):
            em.emit("gpsimd", lambda e, c3=c3: e.dma_start(
                out=out_d[6016 + c3 * 128: 6016 + (c3 + 1) * 128, :],
                in_=q_pass[1][:, c3 * 1024:(c3 + 1) * 1024]),
                inc=s_dbg2, inc_n=16)
        em.emit("sync", lambda e: e.nop(), waits=[(s_dbg2, s_dbg2.n)])

    if debug == 1:
        s_dbg = S("dbg")
        dw = [(s_cwT, s_cwT.n), (s_kvfA, NB_),
              (s_qsp2[0], s_qsp2[0].n), (s_qsp2[1], s_qsp2[1].n)]
        for c4 in range(4):
            em.emit("gpsimd", lambda e, c4=c4: e.dma_start(
                out=out_d[4096 + c4 * 128: 4096 + (c4 + 1) * 128, :],
                in_=logits_all[:, c4 * 1024:(c4 + 1) * 1024]),
                waits=dw, inc=s_dbg, inc_n=16)
        em.emit("gpsimd", lambda e: e.dma_start(
            out=out_d[4608:4736, 0:64], in_=stats_m), inc=s_dbg, inc_n=16)
        em.emit("gpsimd", lambda e: e.dma_start(
            out=out_d[4736:4864, 0:64], in_=stats_r), inc=s_dbg, inc_n=16)
        for c4 in range(4):
            em.emit("gpsimd", lambda e, c4=c4: e.dma_start(
                out=out_d[4864 + c4 * 128: 4864 + (c4 + 1) * 128, :],
                in_=kv_sb[:, c4 * 1024:(c4 + 1) * 1024]),
                inc=s_dbg, inc_n=16)
        for qi, qt in enumerate([t for t in [0, 1, 2, 3, 33, 40, 49, 50] if t < NT_]):
            em.emit("gpsimd", lambda e, qi=qi, qt=qt: e.dma_start(
                out=out_d[7168 + qi * 128: 7296 + qi * 128, :].rearrange(
                    "p (c t) -> p c t", t=128),
                in_=qscr_d[qt].rearrange("c p t -> p c t")),
                inc=s_dbg, inc_n=16)
        for qt in range(6):
            em.emit("gpsimd", lambda e, qt=qt: e.dma_start(
                out=out_d[6144 + qt * 128: 6272 + qt * 128, :].rearrange(
                    "p (c t) -> p c t", t=128),
                in_=qscr_d[qt].rearrange("c p t -> p c t")),
                inc=s_dbg, inc_n=16)
        em.emit("gpsimd", lambda e: e.dma_start(
            out=out_d[8000:8001, :], in_=krow_sb[0:1, 1024:2048]),
            waits=dw, inc=s_dbg, inc_n=16)
        em.emit("gpsimd", lambda e: e.dma_start(
            out=out_d[8001:8002, :], in_=krow_sb[0:1, 0:1024]),
            waits=dw, inc=s_dbg, inc_n=16)
        em.emit("gpsimd", lambda e: e.dma_start(
            out=out_d[8002:8003, :], in_=brow_sb),
            waits=dw, inc=s_dbg, inc_n=16)
        em.emit("gpsimd", lambda e: e.dma_start(
            out=out_d[8003:8004, 0:16], in_=knsum_col[0:1, 0:16]),
            waits=dw, inc=s_dbg, inc_n=16)
        em.emit("gpsimd", lambda e: e.dma_start(
            out=out_d[7040:7168, :], in_=k16_sb), waits=dw, inc=s_dbg, inc_n=16)
        em.emit("gpsimd", lambda e: e.dma_start(
            out=out_d[6912:7040, :], in_=v16_sb), waits=dw, inc=s_dbg, inc_n=16)
        for c2 in range(2):
            em.emit("gpsimd", lambda e, c2=c2: e.dma_start(
                out=out_d[5504 + c2 * 64: 5504 + (c2 + 1) * 64, :],
                in_=cwT[0:64, c2 * 1024:(c2 + 1) * 1024]),
                inc=s_dbg, inc_n=16)
        em.emit("sync", lambda e: e.nop(), waits=[(s_dbg, s_dbg.n)])
    if NP_ > 0:
        em.emit("sync", lambda e: e.nop(), waits=[(s_odma, 16 * NP_)])

    em.check()

    with nc.Block() as block:
        @block.sync
        def _(eng):
            em.replay("sync", eng)

        @block.scalar
        def _(eng):
            em.replay("scalar", eng)

        @block.vector
        def _(eng):
            em.replay("vector", eng)

        @block.tensor
        def _(eng):
            em.replay("tensor", eng)

        @block.gpsimd
        def _(eng):
            em.replay("gpsimd", eng)

    ctx.close()
    return nc


def _prep_shared(inp):
    f32 = np.float32
    qkv_wT = np.ascontiguousarray(np.asarray(inp["L0_attn_qkv_w"], f32).T)
    p1_wT = np.ascontiguousarray(np.asarray(inp["L0_attn_p1_w"], f32).T)
    p2_wT = np.ascontiguousarray(np.asarray(inp["L0_attn_p2_w"], f32).T)
    rw = np.ascontiguousarray(np.asarray(inp["L0_ffn_router_w"], f32))
    w1 = np.asarray(inp["L0_ffn_w1"], f32)
    w2 = np.asarray(inp["L0_ffn_w2"], f32)
    sw1 = np.asarray(inp["L0_ffn_sw1"], f32)
    sw2 = np.asarray(inp["L0_ffn_sw2"], f32)
    units = []
    for e in range(64):
        units.append(np.concatenate([w1[e].reshape(-1), w2[e].reshape(-1)]))
    for es in range(2):
        for sl in range(8):
            a = np.ascontiguousarray(sw1[es][:, sl * 256:(sl + 1) * 256]).reshape(-1)
            b = np.ascontiguousarray(sw2[es][sl * 256:(sl + 1) * 256, :]).reshape(-1)
            units.append(np.concatenate([a, b]))
    wall = np.ascontiguousarray(np.stack(units))

    b1 = np.asarray(inp["L0_ffn_b1"], f32)
    sb1 = np.asarray(inp["L0_ffn_sb1"], f32)
    bias1 = np.zeros((128, 2 * NU), f32)
    bias1[:, 0:128] = b1.reshape(64, 2, 128).transpose(2, 0, 1).reshape(128, 128)
    bias1[:, 128:160] = sb1.reshape(2, 8, 2, 128).transpose(3, 0, 1, 2).reshape(128, 32)
    b2ext = np.ascontiguousarray(np.concatenate(
        [np.asarray(inp["L0_ffn_b2"], f32), np.asarray(inp["L0_ffn_sb2"], f32)],
        axis=0))

    qkv_b = np.asarray(inp["L0_attn_qkv_b"], f32)
    qb = np.ascontiguousarray(qkv_b[0:1024].reshape(8, 128).T)
    p1b = np.ascontiguousarray(np.asarray(inp["L0_attn_p1_b"], f32).reshape(8, 128).T)
    rows7 = np.concatenate([
        qkv_b[1024:2048], qkv_b[2048:3072],
        np.tile(np.asarray(inp["L0_attn_kn_g"], f32), H),
        np.tile(np.asarray(inp["L0_attn_kn_b"], f32), H),
        np.tile(np.asarray(inp["L0_attn_vn_g"], f32), H),
        np.tile(np.asarray(inp["L0_attn_vn_b"], f32), H),
        np.asarray(inp["L0_attn_p2_b"], f32),
    ])[None, :]
    rb = np.ascontiguousarray(np.asarray(inp["L0_ffn_router_b"], f32)[None, :])
    temp = np.asarray(inp["L0_attn_temperature"], f32).reshape(H)
    temp_eo = np.ascontiguousarray(
        np.concatenate([temp[0::2], temp[1::2]])[None, :])
    abias = np.ascontiguousarray(np.asarray(inp["L0_attn_attn_bias"], f32))

    onescol = np.ones((128, 1), ml_dtypes.bfloat16)
    kn_g_a = np.asarray(inp["L0_attn_kn_g"], f32)
    kn_b_a = np.asarray(inp["L0_attn_kn_b"], f32)
    cols4 = np.zeros((128, 4), f32)
    cols4[:, 0] = np.tile(kn_g_a, 2)
    cols4[:, 1] = np.tile(kn_b_a, 2)
    oh = np.zeros((64, 64 * 128), ml_dtypes.bfloat16)
    for e in range(64):
        oh[e, e * 128:(e + 1) * 128] = 1.0
    idf = np.eye(128, dtype=f32)
    onesf = np.ones((1, 128), f32)
    onesb = np.ones((1, 128), ml_dtypes.bfloat16)

    return dict(qkv_wT=qkv_wT, p1_wT=p1_wT, p2_wT=p2_wT, router_w=rw, wall=wall,
                bias1=bias1, b2ext=b2ext, qb=qb, p1b=p1b, rows7=rows7, rb=rb,
                temp_eo=temp_eo, attn_bias=abias, oh=oh, idf=idf, onesf=onesf,
                onesb=onesb, onescol=onescol, cols4=cols4)


def kernel(**inputs):
    x = np.asarray(inputs["L0_attn_x"], np.float32)
    shared = _prep_shared(inputs)
    nc = build_nc()
    in_maps = []
    for c in range(NCORES):
        m = dict(shared)
        m["x"] = np.ascontiguousarray(x[c * NB:(c + 1) * NB].reshape(NT * 128, D))
        in_maps.append(m)
    res = run_bass_kernel_spmd(nc, in_maps, core_ids=list(range(NCORES)))
    outs = [np.asarray(res.results[c]["out"]).reshape(NB, 2048, D)
            for c in range(NCORES)]
    return np.concatenate(outs, axis=0)


if __name__ == "__main__":
    import jax
    jax.config.update("jax_platforms", "cpu")
    import reference
    inp = reference.setup_inputs()
    got = kernel(**inp)
    ref = np.asarray(reference.reference(**inp), np.float32)
    d = got - ref
    print("rvar:", float((d * d).mean() / (ref * ref).mean()))


# version 12 (final)
# speedup vs baseline: 1.0235x; 1.0235x over previous
import contextlib
import numpy as np
import ml_dtypes
import concourse.bass as bass
import concourse.mybir as mybir
from concourse.bass_utils import run_bass_kernel_spmd

F32 = mybir.dt.float32
BF16 = mybir.dt.bfloat16
U8 = mybir.dt.uint8
AF = mybir.ActivationFunctionType
ALU = mybir.AluOpType
AX = mybir.AxisListType

NCORES = 8
NT = 64
TPB = 16
NB = 4
NU = 80
D = 1024
H = 16
EPS = 1e-5

PASS_TILES = [list(range(i, min(i + 3, NT))) for i in range(0, NT, 3)]
NPASS = len(PASS_TILES)
MMK = dict(skip_group_check=True)


class Ctr:
    def __init__(self, handle, name):
        self.h = handle
        self.name = name
        self.n = 0


class Emitter:
    ENGS = ("sync", "scalar", "vector", "tensor", "gpsimd")

    def __init__(self):
        self.ops = {e: [] for e in self.ENGS}
        self.last_wait = {}
        self.future = []

    def emit(self, eng, fn, waits=(), inc=None, inc_n=1, future=False,
             is_dma=False, nodrain=False, tset=None):
        wl = []
        for (sem, val) in waits:
            if val <= 0:
                continue
            if not future:
                assert val <= sem.n, \
                    f"[{eng}] wait on future count {sem.name} {val} > {sem.n}"
            else:
                self.future.append((sem, val))
            key = (eng, sem.name)
            if self.last_wait.get(key, 0) >= val:
                continue
            self.last_wait[key] = val
            wl.append((sem, val))
        self.ops[eng].append((fn, wl, inc, inc_n, is_dma or nodrain, tset))
        if inc is not None:
            inc.n += inc_n

    def check(self):
        for (sem, val) in self.future:
            assert val <= sem.n, f"future wait {sem.name} {val} > final {sem.n}"

    def replay(self, eng_name, eng):
        POLICY = "C"
        last_tset = None
        prev_inc = False
        for (fn, waits, inc, inc_n, skip_drain, tset) in self.ops[eng_name]:
            drained = False
            if eng_name == "scalar" and POLICY in ("A", "C"):
                eng.drain()
                drained = True
            if eng_name == "vector" and POLICY == "A":
                eng.drain()
                drained = True
            if not drained and eng_name in ("vector", "scalar") and prev_inc:
                eng.drain()
                drained = True
            if (not drained and eng_name == "vector" and not skip_drain):
                eng.drain()
                drained = True
            if eng_name == "scalar" and tset is not None:
                if last_tset is not None and tset != last_tset and not drained:
                    eng.drain()
                last_tset = tset
            prev_inc = inc is not None
            for (sem, val) in waits:
                eng.wait_ge(sem.h, val)
            ins = fn(eng)
            if inc is not None:
                assert ins is not None
                ins.then_inc(inc.h, inc_n)


class Carver:
    def __init__(self, arena_ap, base, limit):
        self.a = arena_ap
        self.off = base
        self.limit = limit

    def alloc(self, shape, dtype):
        esz = {F32: 4, BF16: 2, U8: 1}[dtype]
        nb = shape[-1] * esz
        off = (self.off + 63) & ~63
        self.off = off + nb
        assert self.off <= self.limit, f"carver overflow {self.off} > {self.limit}"
        return self.a[0:shape[0], off:off + nb].bitcast(dtype)


def build_nc(nt=NT, npasses=None, wconv=True, debug=False, acut=99, kvterms=4):
    NT_ = nt
    NB_ = nt // TPB
    PT = [list(range(i, min(i + 3, nt))) for i in range(0, nt, 3)]
    if npasses is not None:
        PT = PT[:npasses]
    NP_ = len(PT)
    nc = bass.Bass("TRN2", target_bir_lowering=False, debug=False,
                   detect_race_conditions=False)

    def din(name, shape, dtype):
        return nc.dram_tensor(name, list(shape), dtype, kind="ExternalInput").ap()

    x_d = din("x", (NT * 128, D), F32)
    qkvwT_d = din("qkv_wT", (D, 3 * D), F32)
    p1wT_d = din("p1_wT", (D, D), F32)
    p2wT_d = din("p2_wT", (D, D), F32)
    rw_d = din("router_w", (D, 64), F32)
    wall_d = din("wall", (NU, 524288), F32)
    bias1_d = din("bias1", (128, 2 * NU), F32)
    b2e_d = din("b2ext", (66, D), F32)
    qb_d = din("qb", (128, 8), F32)
    p1b_d = din("p1b", (128, 8), F32)
    rows7_d = din("rows7", (1, 7 * D), F32)
    rb_d = din("rb", (1, 64), F32)
    tempeo_d = din("temp_eo", (1, 16), F32)
    abias_d = din("attn_bias", (16, 64, 64), F32)
    oh_d = din("oh", (64, 64 * 128), BF16)
    idf_d = din("idf", (128, 128), F32)
    onesf_d = din("onesf", (1, 128), F32)
    onesb_d = din("onesb", (1, 128), BF16)
    onescol_d = din("onescol", (128, 1), BF16)
    cols4_d = din("cols4", (128, 4), F32)

    out_d = nc.dram_tensor("out", [NT * 128, D], F32, kind="ExternalOutput").ap()
    wscr_d = nc.dram_tensor("wscr", [NU, 128, 4096], BF16, kind="Internal").ap()
    qscr_d = nc.dram_tensor("qscr", [NT, 8, 128, 128], BF16, kind="Internal").ap()

    em = Emitter()
    ctx = contextlib.ExitStack()

    def S(name):
        return Ctr(ctx.enter_context(nc.semaphore(name)), name)

    s_small = S("small"); s_conv = S("conv"); s_wconv = S("wconv")
    s_init = S("init"); s_initc = S("initc"); s_initcS = S("initcS")
    s_xldA2 = [S("xldA0"), S("xldA1")]; s_xfreeA = S("xfreeA")
    s_trA = S("trA"); s_trcA = S("trcA")
    s_qpeA = S("qpeA"); s_qevA = S("qevA")
    s_kpeA = S("kpeA"); s_vpeA = S("vpeA"); s_kvpeA = S("kvpeA"); s_lpeA = S("lpeA")
    s_kevA = S("kevA"); s_vevA = S("vevA"); s_levA = S("levA")
    s_k16A = S("k16A"); s_v16A = S("v16A"); s_kvfA = S("kvfA")
    s_xnA = S("xnA"); s_sqA = S("sqA"); s_rstdA = S("rstdA")
    s_hsubA = S("hsubA"); s_varA = S("varA")
    s_sqxA = S("sqxA"); s_sumA = S("sumA")
    s_fA = S("fA"); s_fP = S("fP"); s_fD = S("fD"); s_fM = S("fM")
    s_qsp2 = [S("qsp0"), S("qsp1")]
    s_a3d = S("a3d"); s_a3e = S("a3e"); s_a3p = S("a3p"); s_a3t = S("a3t")
    s_cwT = S("cwT"); s_ohld = S("ohld")
    s_xldB = S("xldB"); s_qldB = S("qldB")
    s_wld3 = [S("wld0"), S("wld1"), S("wld2")]
    s_w2ld = [S("w2ld0"), S("w2ld1")]
    s_xnB = S("xnB"); s_trB = S("trB"); s_trcB = S("trcB")
    s_peh = S("peh"); s_silu = S("silu"); s_hsc = S("hsc")
    s_cwbpe = S("cwbpe"); s_cwbc = S("cwbc"); s_mm2 = S("mm2"); s_rdone = S("rdone")
    s_gev = S("gev"); s_attpe = S("attpe"); s_attev = S("attev")
    s_p1pe = S("p1pe"); s_gelev = S("gelev"); s_p2pe = S("p2pe")
    s_sev = S("sev"); s_odve = S("odve"); s_odma = S("odma")

    def sbuf(name, shape, dtype):
        return ctx.enter_context(nc.sbuf_tensor(name, list(shape), dtype)).ap()

    stats_m = sbuf("stats_m", [128, NT], F32)
    stats_r = sbuf("stats_r", [128, NT], F32)
    cwT = sbuf("cwT", [66, 64 * 128], BF16)
    kv_sb = sbuf("kv_sb", [128, NB * 1024], BF16)
    p1_sb = sbuf("p1_sb", [128, 8 * 1024], BF16)
    p2_sb = sbuf("p2_sb", [128, 8 * 1024], BF16)
    router_sb = sbuf("router_sb", [128, 8 * 64], BF16)
    b2e_sb = sbuf("b2e_sb", [66, 1024], BF16)
    bias1_sb = sbuf("bias1_sb", [128, 2 * NU], F32)
    qb_sb = sbuf("qb_sb", [128, 8], F32)
    p1b_sb = sbuf("p1b_sb", [128, 8], F32)
    p2b_bf = sbuf("p2b_bf", [1, 1024], BF16)
    onesf_sb = sbuf("onesf_sb", [1, 128], F32)
    onesb_sb = sbuf("onesb_sb", [1, 128], BF16)
    idf_sb = sbuf("idf_sb", [128, 128], F32)
    tempcol = sbuf("tempcol", [128, 8], F32)
    tempeo_sb = sbuf("tempeo_sb", [1, 16], F32)
    rbrow_sb = sbuf("rbrow_sb", [1, 64], F32)
    eps_col = sbuf("eps_col", [128, 1], F32)
    onescol_sb = sbuf("onescol_sb", [128, 1], BF16)
    cols4_sb = sbuf("cols4_sb", [128, 4], F32)
    kb_bf = sbuf("kb_bf", [1, 1024], BF16)
    vb_bf = sbuf("vb_bf", [1, 1024], BF16)
    rb_bf = sbuf("rb_bf", [1, 64], BF16)

    arena = ctx.enter_context(nc.sbuf_tensor("ov", [128, 143168], U8)).ap()

    ca = Carver(arena, 0, 143168)
    qkv_sb = ca.alloc([128, 8 * 3072], BF16)
    vg_bc = ca.alloc([128, 1024], F32)
    vb2_bc = ca.alloc([128, 1024], F32)
    gkgv_pair = ca.alloc([128, 1024], F32)
    bkbv_pair = ca.alloc([128, 1024], F32)
    kvtmp2 = ca.alloc([128, 1024], F32)
    knsum_col = ca.alloc([128, 16], F32)
    krow_off = (ca.off + 63) & ~63
    krow_sb = ca.alloc([1, 2048], F32)
    vrow32_sb = arena[32:33, krow_off:krow_off + 4096].bitcast(F32)
    brow_sb = ca.alloc([1, 1024], F32)
    r7base = ca.off
    rows7_sb = ca.alloc([1, 7 * 1024], F32)
    ca.off = r7base
    xa = [ca.alloc([128, 1024], F32) for _ in range(2)]
    xn_tm_a = ca.alloc([128, 1024], F32)
    kvtmp = ca.alloc([128, 1024], F32)
    sq_buf = ca.alloc([128, 1024], F32)
    kf_buf = ca.alloc([128, 1024], F32)
    k16_sb = ca.alloc([128, 1024], BF16)
    v16_sb = ca.alloc([128, 1024], BF16)
    q_tile = [ca.alloc([128, 8 * 128], BF16) for _ in range(2)]
    xnfmA = ca.alloc([128, 8 * 128], BF16)
    smA = ca.alloc([128, 16], F32)
    st16 = ca.alloc([128, 64], F32)
    a3buf = ca.alloc([128, 512], F32)
    a3max = ca.alloc([128, 16], F32)
    a3cw = ca.alloc([128, 256], F32)
    abias_sb = ca.alloc([128, 1024], F32)
    logits_all = ca.alloc([128, NT * 64], F32)

    cb = Carver(arena, 0, 143168)
    x_tm = [cb.alloc([128, 3 * 1024], F32) for _ in range(2)]
    xn_tm = cb.alloc([128, 3 * 1024], F32)
    xn_fm = cb.alloc([128, 8 * 384], BF16)
    q_pass = [cb.alloc([128, 8 * 384], BF16) for _ in range(2)]
    att_bf = cb.alloc([128, 8 * 384], BF16)
    h1_bf = cb.alloc([128, 8 * 384], BF16)
    s_tm = cb.alloc([128, 3 * 1024], F32)
    g_tm = cb.alloc([128, 3 * 1024], F32)
    hs_sb = [cb.alloc([128, 2 * 384], BF16) for _ in range(2)]
    hsc_sb = [cb.alloc([128, 2 * 384], BF16) for _ in range(2)]
    cwb_sb = [cb.alloc([128, 384], BF16) for _ in range(2)]
    wu_sb = [cb.alloc([128, 4096], BF16) for _ in range(3)]
    oh_sb = cb.alloc([64, 64 * 128], BF16)

    PS = ctx.enter_context(nc.psum_tensor("PS", [128, 4096], F32)).ap()

    def dma(eng, dst, src, inc, waits=(), future=False):
        em.emit(eng, lambda e, d=dst, s=src: e.dma_start(out=d, in_=s),
                waits=waits, inc=inc, inc_n=16, future=future, is_dma=True)

    dma("sync", bias1_sb, bias1_d, s_small)
    dma("sync", qb_sb, qb_d, s_small)
    dma("sync", p1b_sb, p1b_d, s_small)
    dma("sync", rows7_sb, rows7_d, s_small)
    dma("sync", onesf_sb, onesf_d, s_small)
    dma("sync", onesb_sb, onesb_d, s_small)
    dma("sync", idf_sb, idf_d, s_small)
    dma("sync", tempeo_sb, tempeo_d, s_small)
    dma("sync", rbrow_sb, rb_d, s_small)
    dma("sync", onescol_sb, onescol_d, s_small)
    dma("sync", cols4_sb, cols4_d, s_small)
    em.emit("vector", lambda e: e.memset(abias_sb, 0.0), inc=s_initc)
    em.emit("vector", lambda e: e.memset(eps_col, EPS), inc=s_initc)
    em.emit("vector", lambda e: e.memset(PS[32:33, 3072:4096], 0.0), inc=s_initc)
    EPSC = s_initc.n
    if debug == 1:
        em.emit("vector", lambda e: e.memset(logits_all, 0.0), inc=s_initc)
        em.emit("vector", lambda e: e.memset(stats_m, 0.0), inc=s_initc)
        em.emit("vector", lambda e: e.memset(stats_r, 0.0), inc=s_initc)
        em.emit("vector", lambda e: e.memset(kv_sb, 0.0), inc=s_initc)
        em.emit("vector", lambda e: e.memset(cwT[0:64, :], 0.0), inc=s_initc)
    ab_even = abias_sb.rearrange("p (c x) -> p c x", x=128)[0:64, :, 0:64]
    ab_odd = abias_sb.rearrange("p (c x) -> p c x", x=128)[64:128, :, 64:128]
    ab_src = abias_d.rearrange("(c par) dd e -> par dd c e", par=2)
    dma("sync", ab_even, ab_src[0], s_small, waits=[(s_initc, 1)])
    dma("sync", ab_odd, ab_src[1], s_small)
    SMALL_ALL = s_small.n

    dma("gpsimd", qkv_sb.rearrange("p (kc n) -> p kc n", n=3072),
        qkvwT_d.rearrange("(kc p) n -> p kc n", p=128), s_conv)
    dma("gpsimd", p1_sb.rearrange("p (kc n) -> p kc n", n=1024),
        p1wT_d.rearrange("(kc p) n -> p kc n", p=128), s_conv)
    dma("gpsimd", p2_sb.rearrange("p (kc n) -> p kc n", n=1024),
        p2wT_d.rearrange("(kc p) n -> p kc n", p=128), s_conv)
    dma("gpsimd", router_sb.rearrange("p (kc n) -> p kc n", n=64),
        rw_d.rearrange("(kc p) n -> p kc n", p=128), s_conv)
    dma("gpsimd", b2e_sb, b2e_d, s_conv)

    bc_targets = {4: vg_bc, 5: vb2_bc}
    for r in (4, 5):
        for hh in range(2):
            em.emit("tensor", lambda e, r=r, hh=hh: e.matmul(
                PS[:, (r % 2) * 512:(r % 2) * 512 + 512],
                onesf_sb,
                rows7_sb[:, r * 1024 + hh * 512: r * 1024 + hh * 512 + 512],
                start=True, stop=True, **MMK),
                waits=[(s_small, SMALL_ALL), (s_initc, s_initc.n)], inc=s_init)
            em.emit("vector", lambda e, r=r, hh=hh: e.tensor_copy(
                bc_targets[r][:, hh * 512: hh * 512 + 512],
                PS[:, (r % 2) * 512:(r % 2) * 512 + 512]),
                waits=[(s_init, s_init.n)], inc=s_initc)
    em.emit("scalar", lambda e: e.copy(kb_bf, rows7_sb[:, 0:1024]),
            waits=[(s_small, SMALL_ALL)], inc=s_initcS)
    em.emit("scalar", lambda e: e.copy(vb_bf, rows7_sb[:, 1024:2048]), inc=s_initcS)
    em.emit("scalar", lambda e: e.copy(rb_bf, rbrow_sb), inc=s_initcS)
    em.emit("vector", lambda e: e.tensor_scalar(
        gkgv_pair, vg_bc, cols4_sb[:, 0:1], None, ALU.mult),
        waits=[(s_initc, s_initc.n)], inc=s_initc)
    em.emit("vector", lambda e: e.tensor_scalar(
        bkbv_pair, vb2_bc, cols4_sb[:, 1:2], 2048.0, ALU.mult, ALU.mult),
        inc=s_initc)
    em.emit("tensor", lambda e: e.matmul(PS[0:64, 3 * 512: 3 * 512 + 8],
                                         onesf_sb[0:1, 0:64], tempeo_sb[0:1, 0:8],
                                         start=True, stop=True, **MMK), inc=s_init)
    em.emit("tensor", lambda e: e.matmul(PS[64:128, 3 * 512: 3 * 512 + 8],
                                         onesf_sb[0:1, 0:64], tempeo_sb[0:1, 8:16],
                                         start=True, stop=True, **MMK), inc=s_init)
    em.emit("scalar", lambda e: e.activation(tempcol, PS[:, 3 * 512: 3 * 512 + 8],
                                             AF.Copy, scale=1.0 / 2048.0),
            waits=[(s_init, s_init.n)], inc=s_initcS)
    em.emit("scalar", lambda e: e.copy(p2b_bf, rows7_sb[:, 6 * 1024: 7 * 1024]),
            waits=[(s_small, SMALL_ALL)], inc=s_initcS)
    INIT_EVAC = s_initc.n
    INIT_S = s_initcS.n

    for u in (range(NU) if wconv else []):
        w1src = wall_d[u, 0:262144].rearrange("(kc p jc j) -> p kc jc j",
                                              kc=8, p=128, jc=2, j=128)
        w2src = wall_d[u, 262144:524288].rearrange("(jcc p dd) -> p jcc dd",
                                                   jcc=2, p=128, dd=1024)
        w1dst = wscr_d[u][:, 0:2048].rearrange("p (kc jc j) -> p kc jc j",
                                               kc=8, jc=2, j=128)
        w2dst = wscr_d[u][:, 2048:4096].rearrange("p (jcc dd) -> p jcc dd",
                                                  jcc=2, dd=1024)
        dma("gpsimd", w1dst, w1src, s_wconv)
        dma("gpsimd", w2dst, w2src, s_wconv)

    def emit_xln(j):
        pj = j % 2
        xb = xa[pj]
        B = pj * 8
        xw = [(s_xfreeA, j - 1), (s_sqxA, j - 1)]
        if j < 2:
            xw += [(s_initc, INIT_EVAC), (s_initcS, INIT_S)]
        dma("sync", xb, x_d[j * 128:(j + 1) * 128, :], s_xldA2[pj],
            waits=xw, future=(j > 1))
        em.emit("vector", lambda e, xb=xb, B=B: e.reduce_sum(
            smA[:, B:B + 1], xb, AX.X),
            waits=[(s_xldA2[pj], 16 * (j // 2 + 1)), (s_initc, INIT_EVAC),
                   (s_initcS, INIT_S)], nodrain=True)
        em.emit("vector", lambda e, B=B: e.scalar_tensor_tensor(
            smA[:, B + 3:B + 4], smA[:, B:B + 1], 1.0 / (1024.0 * 1024.0),
            smA[:, B:B + 1], ALU.mult, ALU.mult))
        sqw = [(s_xldA2[pj], 16 * (j // 2 + 1)), (s_initc, EPSC)]
        if j > 0:
            sqw.append((s_trA, 8 * j))
        em.emit("scalar", lambda e, xb=xb, B=B: e.activation(
            xn_tm_a, xb, AF.Square, accum_out=smA[:, B + 1:B + 2]),
            waits=sqw, inc=s_sqxA, tset="arith")
        em.emit("vector", lambda e, B=B: e.tensor_scalar(
            smA[:, B + 2:B + 3], smA[:, B + 1:B + 2], 1.0 / 1024.0, None,
            ALU.mult),
            waits=[(s_sqxA, j + 1)], nodrain=True)
        em.emit("vector", lambda e, j=j, B=B: e.tensor_scalar(
            stats_m[:, j:j + 1], smA[:, B:B + 1], 1.0 / 1024.0, None,
            ALU.mult), nodrain=True)
        em.emit("vector", lambda e, B=B: e.tensor_tensor(
            smA[:, B + 4:B + 5], smA[:, B + 2:B + 3], smA[:, B + 3:B + 4],
            ALU.subtract), inc=s_varA, nodrain=True)
        em.emit("scalar", lambda e, B=B: e.activation(
            smA[:, B + 2:B + 3], smA[:, B + 4:B + 5], AF.Sqrt, bias=eps_col),
            waits=[(s_varA, s_varA.n)], inc=s_sqA, tset="arith")
        em.emit("vector", lambda e, j=j, B=B: e.reciprocal(
            stats_r[:, j:j + 1], smA[:, B + 2:B + 3]),
            waits=[(s_sqA, s_sqA.n)], inc=s_rstdA, nodrain=True)
        em.emit("vector", lambda e, j=j, B=B: e.tensor_tensor(
            smA[:, B + 5:B + 6], stats_m[:, j:j + 1], stats_r[:, j:j + 1],
            ALU.mult))
        em.emit("vector", lambda e, xb=xb, j=j, B=B: e.tensor_scalar(
            xn_tm_a, xb, stats_r[:, j:j + 1], smA[:, B + 5:B + 6],
            ALU.mult, ALU.subtract), inc=s_xnA)
        em.emit("vector", lambda e: e.engine_nop(), inc=s_xfreeA, nodrain=True)

    def emit_A(j):
        tj = j % TPB
        bj = j // TPB
        pj = j % 2
        qtb = q_tile[pj]
        for c in (range(8) if acut >= 2 else []):
            bnk = c % 2
            w = [(s_xnA, j + 1)]
            if c >= 2:
                w.append((s_trcA, 8 * j + c - 1))
            if j == 0:
                w += [(s_initc, INIT_EVAC), (s_initcS, INIT_S)]
            else:
                if c == 0:
                    w += [(s_kevA, j), (s_levA, j)]
                    if tj == 0:
                        w.append((s_kvfA, bj))
            em.emit("tensor", lambda e, c=c, bnk=bnk: e.transpose(
                PS[:, bnk * 512 + (c // 2) * 128: bnk * 512 + (c // 2) * 128 + 128],
                xn_tm_a[:, c * 128:(c + 1) * 128], idf_sb),
                waits=w, inc=s_trA)
            em.emit("scalar", lambda e, c=c, bnk=bnk: e.activation(
                xnfmA[:, c * 128:(c + 1) * 128],
                PS[:, bnk * 512 + (c // 2) * 128: bnk * 512 + (c // 2) * 128 + 128],
                AF.Copy),
                waits=[(s_trA, s_trA.n)], inc=s_trcA)

        for kc in (range(8) if acut >= 3 else []):
            for mc in range(8):
                bnk = 2 + mc % 2
                reg = (mc // 2) * 128
                w = []
                if mc == 0:
                    w.append((s_trcA, 8 * j + kc + 1))
                    if kc == 0:
                        w.append((s_conv, 80))
                        if j == 0:
                            w += [(s_initc, INIT_EVAC), (s_initcS, INIT_S)]
                        else:
                            w += [(s_vevA, j), (s_qevA, 8 * j)]
                em.emit("tensor", lambda e, kc=kc, mc=mc, bnk=bnk, reg=reg: e.matmul(
                    PS[:, bnk * 512 + reg: bnk * 512 + reg + 128],
                    qkv_sb[:, kc * 3072 + mc * 128: kc * 3072 + mc * 128 + 128],
                    xnfmA[:, kc * 128:(kc + 1) * 128],
                    start=(kc == 0 and mc < 2), stop=(kc == 7), **MMK),
                    waits=w, inc=(s_qpeA if (kc == 7 and mc == 7) else None))
        for mc in (range(8) if acut >= 3 else []):
            bnk = 2 + mc % 2
            reg = (mc // 2) * 128
            em.emit("scalar", lambda e, mc=mc, bnk=bnk, reg=reg, qtb=qtb: e.activation(
                qtb[:, mc * 128:(mc + 1) * 128],
                PS[:, bnk * 512 + reg: bnk * 512 + reg + 128],
                AF.Identity, bias=qb_sb[:, mc:mc + 1]),
                waits=[(s_qpeA, j + 1),
                       (s_qsp2[pj], 16 * ((j - 2) // 2 + 1) if j >= 2 else 0)],
                inc=s_qevA)
        if acut >= 3:
            em.emit("scalar", lambda e, j=j, qtb=qtb: e.dma_start(
                out=qscr_d[j].rearrange("c p t -> p c t"),
                in_=qtb.rearrange("p (c t) -> p c t", t=128)),
                waits=[(s_qevA, 8 * (j + 1))], inc=s_qsp2[pj], inc_n=16,
                is_dma=True)

    for i in range(NT_):
        ti = i % TPB
        bi = i // TPB
        par = i % 2
        xbuf = xa[par]
        if i == 0:
            emit_xln(0)
            emit_A(0)

        for half in (range(2) if acut >= 4 else []):
            for kc in range(8):
                w = []
                if half == 0 and kc == 0:
                    w = [(s_trcA, 8 * i + 8)]
                    if i > 0:
                        w.append((s_kevA, i))
                em.emit("tensor", lambda e, half=half, kc=kc: e.matmul(
                    PS[:, half * 512: half * 512 + 512],
                    xnfmA[:, kc * 128:(kc + 1) * 128],
                    qkv_sb[:, kc * 3072 + 1024 + half * 512:
                           kc * 3072 + 1024 + half * 512 + 512],
                    start=(kc == 0), stop=False, **MMK),
                    waits=w, inc=None)
            em.emit("tensor", lambda e, half=half: e.matmul(
                PS[:, half * 512: half * 512 + 512],
                onesb_sb,
                kb_bf[:, half * 512: half * 512 + 512],
                start=False, stop=True, **MMK),
                waits=[(s_initcS, INIT_S)], inc=(s_kpeA if half == 1 else None))
        for half in (range(2) if acut >= 5 else []):
            for kc in range(8):
                w = []
                if half == 0 and kc == 0:
                    w = [(s_qevA, 8 * (i + 1))]
                em.emit("tensor", lambda e, half=half, kc=kc: e.matmul(
                    PS[:, (2 + half) * 512: (2 + half) * 512 + 512],
                    xnfmA[:, kc * 128:(kc + 1) * 128],
                    qkv_sb[:, kc * 3072 + 2048 + half * 512:
                           kc * 3072 + 2048 + half * 512 + 512],
                    start=(kc == 0), stop=False, **MMK),
                    waits=w, inc=None)
            em.emit("tensor", lambda e, half=half: e.matmul(
                PS[:, (2 + half) * 512: (2 + half) * 512 + 512],
                onesb_sb,
                vb_bf[:, half * 512: half * 512 + 512],
                start=False, stop=True, **MMK),
                waits=[], inc=(s_vpeA if half == 1 else None))

        if i + 1 < NT_:
            emit_xln(i + 1)

        def headnorm(srcb, out16, sem_ev, sem_16, pe_sem, copy_extra, sq_extra):
            for half in range(2):
                em.emit("scalar", lambda e, half=half, srcb=srcb: e.activation(
                    kf_buf[:, half * 512: half * 512 + 512],
                    PS[:, (srcb + half) * 512: (srcb + half) * 512 + 512],
                    AF.Copy),
                    waits=([(pe_sem, i + 1)] + copy_extra) if half == 0 else [],
                    inc=(sem_ev if half == 1 else None))
            em.emit("scalar", lambda e: e.activation(sq_buf, kf_buf, AF.Square),
                    waits=sq_extra, inc=s_sqA, tset="arith")
            kf3 = kf_buf.rearrange("p (h dd) -> p h dd", dd=64)
            sq3 = sq_buf.rearrange("p (h dd) -> p h dd", dd=64)
            em.emit("vector", lambda e, kf3=kf3: e.tensor_reduce(
                st16[:, 0:16], kf3, AX.X, ALU.add),
                waits=[(sem_ev, i + 1)], nodrain=True)
            em.emit("vector", lambda e, sq3=sq3: e.tensor_reduce(
                st16[:, 32:48], sq3, AX.X, ALU.add),
                waits=[(s_sqA, s_sqA.n)], nodrain=True)
            em.emit("vector", lambda e: e.tensor_scalar(
                st16[:, 16:32], st16[:, 0:16], 1.0 / 64.0, None, ALU.mult),
                nodrain=True)
            em.emit("vector", lambda e: e.tensor_scalar(
                st16[:, 48:64], st16[:, 32:48], 1.0 / 64.0, None, ALU.mult),
                nodrain=True)
            em.emit("vector", lambda e: e.tensor_tensor(
                st16[:, 0:16], st16[:, 16:32], st16[:, 16:32], ALU.mult),
                nodrain=True)
            em.emit("vector", lambda e: e.tensor_tensor(
                st16[:, 32:48], st16[:, 48:64], st16[:, 0:16], ALU.subtract),
                inc=s_varA)
            em.emit("scalar", lambda e: e.activation(st16[:, 48:64], st16[:, 32:48],
                                                     AF.Sqrt, bias=eps_col),
                    waits=[(s_varA, s_varA.n)], inc=s_sqA, tset="arith")
            em.emit("vector", lambda e: e.reciprocal(st16[:, 0:16],
                                                     st16[:, 48:64]),
                    waits=[(s_sqA, s_sqA.n)], nodrain=True)
            em.emit("vector", lambda e, kf3=kf3, sq3=sq3: e.tensor_tensor(
                sq3, kf3,
                st16[:, 16:32].unsqueeze(2).to_broadcast([128, 16, 64]),
                ALU.subtract), inc=s_hsubA, nodrain=True)
            em.emit("vector", lambda e, o=out16, sq3=sq3: e.tensor_tensor(
                o.rearrange("p (h dd) -> p h dd", dd=64), sq3,
                st16[:, 0:16].unsqueeze(2).to_broadcast([128, 16, 64]), ALU.mult),
                waits=[(s_sumA, i)], inc=sem_16)

        if acut >= 4:
            headnorm(0, k16_sb, s_kevA, s_k16A, s_kpeA, [], [])
        if acut >= 5:
            headnorm(2, v16_sb, s_vevA, s_v16A, s_vpeA,
                     [(s_hsubA, 2 * i + 1)], [(s_k16A, i + 1)])

        for kc in (range(8) if acut >= 7 else []):
            w = []
            if kc == 0:
                w = [(s_kevA, i + 1), (s_conv, 80)]
            em.emit("tensor", lambda e, kc=kc: e.matmul(
                PS[:, 0:64],
                xnfmA[:, kc * 128:(kc + 1) * 128],
                router_sb[:, kc * 64:(kc + 1) * 64],
                start=(kc == 0), stop=False, **MMK),
                waits=w, inc=None)
        if acut >= 7:
            em.emit("tensor", lambda e: e.matmul(
                PS[:, 0:64], onesb_sb, rb_bf,
                start=False, stop=True, **MMK),
                waits=[], inc=s_lpeA)
            em.emit("scalar", lambda e, i=i: e.activation(
                logits_all[:, i * 64:(i + 1) * 64], PS[:, 0:64], AF.Copy),
                waits=[(s_lpeA, i + 1)], inc=s_levA)

        if ti != TPB - 1 and i + 1 < NT_:
            emit_A(i + 1)


        for c in (range(8) if acut >= 6 else []):
            w = []
            if c == 0:
                w = [(s_k16A, i + 1), (s_v16A, i + 1)]
                if ti == 0:
                    w.append((s_kvfA, bi))
            em.emit("tensor", lambda e, c=c, ti=ti: e.matmul(
                PS[:, 4 * 512 + c * 128: 4 * 512 + c * 128 + 128],
                k16_sb[:, c * 128:(c + 1) * 128],
                v16_sb[:, c * 128:(c + 1) * 128],
                start=(ti == 0 and c % 4 == 0), stop=(ti == TPB - 1), **MMK),
                waits=w, inc=(s_kvpeA if c == 7 else None))
        if acut >= 6:
            for half in range(2):
                em.emit("tensor", lambda e, half=half, ti=ti: e.matmul(
                    PS[0:1, 3072 + half * 512: 3072 + half * 512 + 512],
                    onescol_sb,
                    k16_sb[:, half * 512: half * 512 + 512],
                    start=(ti == 0), stop=(ti == TPB - 1), **MMK),
                    waits=[], inc=None)
            for half in range(2):
                em.emit("tensor", lambda e, half=half, ti=ti: e.matmul(
                    PS[32:33, 3072 + half * 512: 3072 + half * 512 + 512],
                    onescol_sb,
                    v16_sb[:, half * 512: half * 512 + 512],
                    start=(ti == 0), stop=(ti == TPB - 1), **MMK),
                    waits=[], inc=(s_sumA if half == 1 else None))

        if ti == TPB - 1 and acut >= 6:
            em.emit("scalar", lambda e: e.activation(
                krow_sb[0:1, 0:1024], PS[0:1, 3072:4096], AF.Copy),
                waits=[(s_sumA, TPB * (bi + 1))], inc=s_fA)
            em.emit("scalar", lambda e: e.activation(
                vrow32_sb, PS[32:33, 3072:4096], AF.Copy),
                inc=s_fA)
            FA0 = s_fA.n
            em.emit("sync", lambda e: e.dma_start(
                out=krow_sb[0:1, 1024:2048], in_=vrow32_sb),
                waits=[(s_fA, FA0)], inc=s_fM, inc_n=16, is_dma=True)
            FM0 = s_fM.n
            em.emit("vector", lambda e: e.tensor_tensor(
                brow_sb, krow_sb[0:1, 1024:2048],
                vg_bc[0:1, :], ALU.mult),
                waits=[(s_fM, FM0)], inc=s_fD)
            FD0 = s_fD.n
            for c in range(8):
                em.emit("tensor", lambda e, c=c: e.matmul(
                    PS[:, c:c + 1],
                    krow_sb[0:1, c * 128:(c + 1) * 128],
                    onesf_sb[0:1, 0:1],
                    start=True, stop=True, **MMK),
                    waits=[(s_fA, FA0), (s_levA, i + 1),
                           (s_kevA, i + 1)] if c == 0 else [],
                    inc=(s_fP if c == 7 else None))
            FP0 = s_fP.n
            for half in range(2):
                em.emit("tensor", lambda e, half=half: e.matmul(
                    PS[:, 512 + half * 512: 512 + half * 512 + 512],
                    onesf_sb, brow_sb[0:1, half * 512: half * 512 + 512],
                    start=True, stop=True, **MMK),
                    waits=[(s_fD, FD0), (s_vevA, i + 1)] if half == 0 else [],
                    inc=(s_fP if half == 1 else None))
            FP1 = s_fP.n
            em.emit("scalar", lambda e: e.activation(
                knsum_col[:, 0:8], PS[:, 0:8], AF.Copy),
                waits=[(s_fP, FP0)], inc=s_fA)
            FA1 = s_fA.n
            em.emit("vector", lambda e: e.tensor_scalar(
                knsum_col[:, 8:16], knsum_col[:, 0:8], cols4_sb[:, 0:1],
                None, ALU.mult), waits=[(s_fA, FA1)])
            em.emit("vector", lambda e: e.tensor_tensor(
                kvtmp, PS[:, 2048:3072], gkgv_pair, ALU.mult),
                waits=[(s_kvpeA, i + 1), (s_initc, INIT_EVAC),
                       (s_initcS, INIT_S)])
            for c in range(8):
                if kvterms >= 2:
                    em.emit("vector", lambda e, c=c: e.scalar_tensor_tensor(
                        kvtmp2[:, c * 128:(c + 1) * 128],
                        vb2_bc[:, c * 128:(c + 1) * 128],
                        knsum_col[:, 8 + c: 9 + c],
                        kvtmp[:, c * 128:(c + 1) * 128],
                        ALU.mult, ALU.add))
                else:
                    em.emit("vector", lambda e, c=c: e.tensor_copy(
                        kvtmp2[:, c * 128:(c + 1) * 128],
                        kvtmp[:, c * 128:(c + 1) * 128]))
            for c in range(8):
                if kvterms >= 3:
                    em.emit("vector", lambda e, c=c: e.scalar_tensor_tensor(
                        kvtmp[:, c * 128:(c + 1) * 128],
                        PS[:, 512 + c * 128: 512 + c * 128 + 128],
                        cols4_sb[:, 1:2],
                        kvtmp2[:, c * 128:(c + 1) * 128],
                        ALU.mult, ALU.add),
                        waits=[(s_fP, FP1)] if c == 0 else [])
                else:
                    em.emit("vector", lambda e, c=c: e.tensor_copy(
                        kvtmp[:, c * 128:(c + 1) * 128],
                        kvtmp2[:, c * 128:(c + 1) * 128]),
                        waits=[(s_fP, FP1)] if c == 0 else [])
            if kvterms >= 4:
                em.emit("vector", lambda e: e.tensor_tensor(
                    kvtmp2, kvtmp, bkbv_pair, ALU.add))
            else:
                em.emit("vector", lambda e: e.tensor_copy(kvtmp2, kvtmp))
            for c in range(8):
                em.emit("vector", lambda e, c=c, bi=bi: e.scalar_tensor_tensor(
                    kv_sb[:, bi * 1024 + c * 128: bi * 1024 + (c + 1) * 128],
                    kvtmp2[:, c * 128:(c + 1) * 128],
                    tempcol[:, c:c + 1],
                    abias_sb[:, c * 128:(c + 1) * 128],
                    ALU.mult, ALU.add),
                    waits=[(s_small, SMALL_ALL)] if c == 0 else [],
                    inc=(s_kvfA if c == 7 else None))

        if ti == TPB - 1 and i + 1 < NT_:
            emit_A(i + 1)


    em.emit("vector", lambda e: e.memset(cwT[64:66, :], 1.0), inc=s_cwT)

    def a3_bufs(i):
        par = i % 2
        return dict(
            par=par,
            zs=logits_all[:, i * 64:(i + 1) * 64],
            zc=a3buf[:, par * 256: par * 256 + 64],
            ez=a3buf[:, par * 256 + 64: par * 256 + 128],
            msk=a3buf[:, par * 256 + 128: par * 256 + 192],
            emu=a3buf[:, par * 256 + 192: par * 256 + 256],
            mx=a3max[:, par * 8: par * 8 + 8],
            ssum=smA[:, par * 8 + 3: par * 8 + 4],
            srec=smA[:, par * 8 + 4: par * 8 + 5],
            cwcols=a3cw[:, par * 128: par * 128 + 64])

    a3_list = list(range(NT_)) if acut >= 8 else []
    assert len(a3_list) % 2 == 0, "A3 pair-zip needs an even tile count"
    for i0 in range(0, len(a3_list) - 1, 2):
        pr = [a3_bufs(i0), a3_bufs(i0 + 1)]
        for s, i in ((0, i0), (1, i0 + 1)):
            b = pr[s]
            w = [(s_levA, i + 1)]
            if i >= 2:
                w.append((s_a3e, i - 1))
            em.emit("vector", lambda e, b=b: e.max(out=b["mx"], in_=b["zs"]),
                    waits=w, nodrain=True)
        for s in range(2):
            b = pr[s]
            em.emit("vector", lambda e, b=b: e.tensor_scalar(
                b["zc"], b["zs"], b["mx"][:, 0:1], None, ALU.subtract),
                nodrain=True)
        dvals = []
        for s in range(2):
            b = pr[s]
            em.emit("vector", lambda e, b=b: e.tensor_scalar(
                b["msk"], b["zs"], b["mx"][:, 7:8], None, ALU.is_ge),
                inc=s_a3d, nodrain=True)
            dvals.append(s_a3d.n)
        evals = []
        for s in range(2):
            b = pr[s]
            em.emit("scalar", lambda e, b=b: e.activation(b["ez"], b["zc"],
                                                          AF.Exp),
                    waits=[(s_a3d, dvals[s])], inc=s_a3e, tset="exp")
            evals.append(s_a3e.n)
        for s in range(2):
            b = pr[s]
            em.emit("vector", lambda e, b=b: e.tensor_tensor(
                b["emu"], b["ez"], b["msk"], ALU.mult),
                waits=[(s_a3e, evals[s])], nodrain=True)
        for s in range(2):
            b = pr[s]
            em.emit("vector", lambda e, b=b: e.reduce_sum(
                b["ssum"], b["emu"], AX.X), nodrain=True)
        for s in range(2):
            b = pr[s]
            em.emit("vector", lambda e, b=b: e.reciprocal(
                b["srec"], b["ssum"]), nodrain=True)
        for s, i in ((0, i0), (1, i0 + 1)):
            b = pr[s]
            w = [(s_a3t, i - 1)] if i >= 2 else []
            em.emit("vector", lambda e, b=b: e.tensor_scalar(
                b["cwcols"], b["emu"], b["srec"], None, ALU.mult),
                waits=w, inc=s_a3p, nodrain=True)
        for s, i in ((0, i0), (1, i0 + 1)):
            b = pr[s]
            bnk = 6 + b["par"]
            em.emit("tensor", lambda e, b=b, bnk=bnk: e.transpose(
                PS[0:64, bnk * 512: bnk * 512 + 128], b["cwcols"], idf_sb),
                waits=[(s_a3p, s_a3p.n - (1 - s)), (s_cwT, s_cwT.n),
                       (s_kvfA, NB_)],
                inc=s_a3t)
            em.emit("scalar", lambda e, i=i, bnk=bnk: e.activation(
                cwT[0:64, i * 128:(i + 1) * 128],
                PS[0:64, bnk * 512: bnk * 512 + 128], AF.Copy),
                waits=[(s_a3t, s_a3t.n)], inc=s_cwT)
    CWT_DONE = s_cwT.n

    QSP_ALL = [(s_qsp2[0], s_qsp2[0].n), (s_qsp2[1], s_qsp2[1].n)]
    if NP_ > 0:
        dma("gpsimd", oh_sb, oh_d, s_ohld, waits=[(s_cwT, CWT_DONE)], future=True)

    c_cwb = 0
    for p, tiles in enumerate(PT):
        ntb = len(tiles)
        T = ntb * 128
        par = p % 2
        xb = x_tm[par]
        qb_ = q_pass[par]
        t0 = tiles[0]

        def x_load(pp):
            xbp = x_tm[pp % 2]
            tl = PT[pp]
            em.emit("gpsimd", lambda e, xbp=xbp, tp0=tl[0], nn=len(tl):
                    e.dma_start(
                out=xbp.rearrange("p (tb dd) -> p tb dd", dd=1024)[:, 0:nn],
                in_=x_d[tp0 * 128:(tp0 + nn) * 128, :].rearrange(
                    "(tb p) dd -> p tb dd", p=128)),
                waits=[(s_odve, pp - 1), (s_ohld, 16)], inc=s_xldB, inc_n=16,
                is_dma=True, future=True)
        if p == 0:
            x_load(0)
        if p + 1 < NP_:
            x_load(p + 1)
        for tb in range(ntb):
            em.emit("gpsimd", lambda e, qb_=qb_, t0=t0, tb=tb: e.dma_start(
                out=qb_.rearrange("p (c x) -> p c x", x=384)[:, :, tb * 128:(tb + 1) * 128],
                in_=qscr_d[t0 + tb].rearrange("c p t -> p c t")),
                waits=QSP_ALL + [(s_attpe, p - 1)], inc=s_qldB,
                inc_n=16, future=True)
        QLD = s_qldB.n

        for u in range(NU):
            g = p * NU + u
            qn = "gpsimd" if g % 2 == 1 else "sync"
            w1w = [(s_wconv, (32 * NU) if wconv else 0),
                   (s_peh, max(0, 2 * (g - 2)))]
            w2w = [(s_wconv, (32 * NU) if wconv else 0), (s_mm2, g - 2)]
            if p == 0 and u < 3:
                w1w += [(s_cwT, CWT_DONE)] + QSP_ALL
                w2w += [(s_cwT, CWT_DONE)] + QSP_ALL
            dma(qn, wu_sb[g % 3][:, 0:2048], wscr_d[u][:, 0:2048],
                s_wld3[g % 2], waits=w1w, future=True)
            dma(qn, wu_sb[g % 3][:, 2048:4096], wscr_d[u][:, 2048:4096],
                s_w2ld[g % 2], waits=w2w, future=True)

        def xn_opB(pp, extra):
            tl = PT[pp]
            xbp = x_tm[pp % 2]

            def op(e, tl=tl, xbp=xbp):
                ins = None
                for j, t in enumerate(tl):
                    ins = e.tensor_scalar(
                        xn_tm[:, j * 1024:(j + 1) * 1024],
                        xbp[:, j * 1024:(j + 1) * 1024],
                        stats_m[:, t:t + 1], stats_r[:, t:t + 1],
                        ALU.subtract, ALU.mult)
                return ins
            em.emit("vector", op,
                    waits=[(s_xldB, 16 * (pp + 1)), (s_cwT, CWT_DONE)] + extra,
                    inc=s_xnB, nodrain=True)
        if p == 0:
            xn_opB(0, [])

        idx = 0
        TRB0 = s_trcB.n
        for kc in range(8):
            for tb in range(ntb):
                tens = idx % 3
                half = (idx // 3) % 2
                col = tens * 1024 + half * 512
                w = [(s_xnB, p + 1)]
                if idx >= 6:
                    w.append((s_trcB, TRB0 + idx - 5))
                if idx == 0:
                    if p == 0:
                        w += [(s_kvfA, NB_), (s_cwT, CWT_DONE)]
                    else:
                        w.append((s_sev, p))
                em.emit("tensor", lambda e, kc=kc, tb=tb, col=col: e.transpose(
                    PS[:, col: col + 128],
                    xn_tm[:, tb * 1024 + kc * 128: tb * 1024 + kc * 128 + 128],
                    idf_sb), waits=w, inc=s_trB)
                em.emit("scalar", lambda e, kc=kc, tb=tb, col=col: e.activation(
                    xn_fm[:, kc * 384 + tb * 128: kc * 384 + tb * 128 + 128],
                    PS[:, col: col + 128], AF.Copy),
                    waits=[(s_trB, s_trB.n)], inc=s_trcB)
                idx += 1
        TRC = s_trcB.n

        def cwb_mm(u_next):
            nonlocal c_cwb
            g_loc = p * NU + u_next
            em.emit("tensor", lambda e, u=u_next, t0=t0, T=T: e.matmul(
                PS[:, 7 * 512: 7 * 512 + T],
                oh_sb[:, u * 128:(u + 1) * 128],
                cwT[0:64, t0 * 128: t0 * 128 + T],
                start=True, stop=True, **MMK),
                waits=[(s_silu, 2 * g_loc), (s_ohld, 16)], inc=s_cwbpe)
            c_cwb += 1
            em.emit("scalar", lambda e, gl=g_loc, T=T: e.activation(
                cwb_sb[gl % 2][:, 0:T], PS[:, 7 * 512: 7 * 512 + T], AF.Copy),
                waits=[(s_cwbpe, s_cwbpe.n)], inc=s_cwbc)

        cwb_mm(0)

        dep_cnt = {}

        def mm1_stage(v, jc):
            gv = p * NU + v
            routed_v = v < 64
            wuv = wu_sb[gv % 3]
            hsbv = hs_sb[gv % 2]
            hscbv = hsc_sb[gv % 2] if routed_v else hs_sb[gv % 2]
            bnk = 6 + jc
            for kc in range(8):
                w = []
                if jc == 0:
                    w.append((s_trcB, TRB0 + ntb * (kc + 1)))
                if kc == 0:
                    w.append((s_wld3[gv % 2], 16 * (gv // 2 + 1)))
                    if jc == 0:
                        w.append((s_silu, max(0, 2 * gv - 1)))
                    else:
                        w.append((s_cwbc, c_cwb))
                        w.append((s_silu, 2 * gv))
                em.emit("tensor", lambda e, wu=wuv, jc=jc, kc=kc, bnk=bnk, T=T:
                        e.matmul(
                            PS[:, bnk * 512: bnk * 512 + T],
                            wu[:, (kc * 2 + jc) * 128:(kc * 2 + jc) * 128 + 128],
                            xn_fm[:, kc * 384: kc * 384 + T],
                            start=(kc == 0), stop=(kc == 7), **MMK),
                        waits=w, inc=(s_peh if kc == 7 else None), future=True)
            sw = [(s_peh, s_peh.n)]
            if jc == 0 and v >= 2:
                if v - 2 < 64:
                    sw.append(dep_cnt[(v - 2, 1)])
                else:
                    sw.append((s_mm2, gv - 1))
            em.emit("scalar", lambda e, hsb=hsbv, jc=jc, u=v, bnk=bnk, T=T:
                    e.activation(
                        hsb[:, jc * 384: jc * 384 + T],
                        PS[:, bnk * 512: bnk * 512 + T], AF.Silu,
                        bias=bias1_sb[:, 2 * u + jc: 2 * u + jc + 1]),
                    waits=sw, inc=s_silu, tset="silu", future=True)

        def hsc_stage(v, jc):
            gv = p * NU + v
            if v < 64:
                hw = [(s_silu, s_silu.n), (s_cwbc, c_cwb)]
                if jc == 0 and v >= 2:
                    hw.append((s_mm2, gv - 1))
                em.emit("vector", lambda e, hscb=hsc_sb[gv % 2],
                        hsb=hs_sb[gv % 2], jc=jc, g=gv, T=T:
                        e.tensor_tensor(
                            hscb[:, jc * 384: jc * 384 + T],
                            hsb[:, jc * 384: jc * 384 + T],
                            cwb_sb[g % 2][:, 0:T], ALU.mult),
                        waits=hw,
                        inc=s_hsc, nodrain=True, future=True)
                dep_cnt[(v, jc)] = (s_hsc, s_hsc.n)
            else:
                dep_cnt[(v, jc)] = (s_silu, s_silu.n)

        mm1_stage(0, 0)
        hsc_stage(0, 0)
        mm1_stage(0, 1)
        hsc_stage(0, 1)
        mm1_stage(1, 0)
        cwb_mm(1)
        hsc_stage(1, 0)
        mm1_stage(1, 1)
        hsc_stage(1, 1)

        for u in range(NU):
            g = p * NU + u
            routed = u < 64
            wu = wu_sb[g % 3]
            v = u + 2

            if v < NU:
                mm1_stage(v, 0)
                if v < 64:
                    cwb_mm(v)
                hsc_stage(v, 0)

            def mm2_group(jc):
                dep = dep_cnt[(u, jc)]
                cnt = 0
                for tb in range(ntb):
                    for dh in range(2):
                        col = tb * 1024 + dh * 512
                        w = ([dep, (s_w2ld[g % 2], 16 * (g // 2 + 1))]
                             if cnt == 0 else [])
                        em.emit("tensor", lambda e, jc=jc, tb=tb, dh=dh,
                                col=col, wu=wu, u=u,
                                hscb=(hsc_sb[g % 2] if routed else hs_sb[g % 2]):
                                e.matmul(
                                    PS[:, col: col + 512],
                                    hscb[:, jc * 384 + tb * 128:
                                         jc * 384 + tb * 128 + 128],
                                    wu[:, 2048 + jc * 1024 + dh * 512:
                                       2048 + jc * 1024 + dh * 512 + 512],
                                    start=(u == 0 and jc == 0), stop=False, **MMK),
                                waits=w,
                                inc=(s_mm2 if (jc == 1 and cnt == 2 * ntb - 1)
                                     else None))
                        cnt += 1
            mm2_group(0)
            if v < NU:
                mm1_stage(v, 1)
                hsc_stage(v, 1)
            mm2_group(1)

        for tb in range(ntb):
            for dh in range(2):
                col = tb * 1024 + dh * 512
                w = [(s_conv, 80)] if (tb == 0 and dh == 0) else []
                em.emit("tensor", lambda e, tb=tb, dh=dh, col=col, t0=t0: e.matmul(
                    PS[:, col: col + 512],
                    cwT[:, (t0 + tb) * 128: (t0 + tb) * 128 + 128],
                    b2e_sb[:, dh * 512: dh * 512 + 512],
                    start=False, stop=True, **MMK),
                    waits=w,
                    inc=(s_rdone if (tb == ntb - 1 and dh == 1) else None))

        for tb in range(ntb):
            for dh in range(2):
                col = tb * 1024 + dh * 512
                w = [(s_rdone, p + 1)]
                if tb == 0 and dh == 0:
                    w.append((s_odve, p))
                em.emit("scalar", lambda e, col=col: e.activation(
                    g_tm[:, col: col + 512], PS[:, col: col + 512], AF.Silu),
                    waits=w, inc=s_gev, tset="silu")
        GEV = s_gev.n

        def att_col(c):
            if c < 6:
                return (c // 2) * 1024 + (c % 2) * 512
            return c * 512
        att_items = []
        for c in range(8):
            for half in range(2):
                segs = []
                sstart = 0
                while sstart < ntb:
                    bb = (t0 + sstart) // TPB
                    send = sstart
                    while send < ntb and (t0 + send) // TPB == bb:
                        send += 1
                    segs.append((sstart, send, bb))
                    sstart = send
                for si, seg in enumerate(segs):
                    att_items.append((c, half, seg))
        for n_it, (c, half, (sa, sb2, bb)) in enumerate(att_items):
            col = att_col(c)
            hrow = half * 64
            w = []
            if n_it == 0:
                w = [(s_gev, GEV), (s_qldB, QLD),
                     (s_silu, s_silu.n), (s_cwbc, c_cwb), (s_kvfA, NB_)]
            em.emit("tensor", lambda e, c=c, hrow=hrow, sa=sa, sb2=sb2, bb=bb,
                    col=col, qb_=qb_: e.matmul(
                        PS[hrow:hrow + 64, col + sa * 128: col + sb2 * 128],
                        kv_sb[hrow:hrow + 64,
                              bb * 1024 + c * 128 + hrow:
                              bb * 1024 + c * 128 + hrow + 64],
                        qb_[hrow:hrow + 64, c * 384 + sa * 128: c * 384 + sb2 * 128],
                        start=True, stop=True, **MMK),
                    waits=w,
                    inc=(s_attpe if n_it == len(att_items) - 1 else None))

        for c in range(8):
            col = att_col(c)
            em.emit("vector", lambda e, c=c, col=col, T=T: e.tensor_tensor(
                att_bf[:, c * 384: c * 384 + T],
                PS[:, col: col + T],
                xn_fm[:, c * 384: c * 384 + T], ALU.add),
                waits=[(s_attpe, p + 1)], inc=s_attev, nodrain=True)

        for c in range(8):
            col = att_col(c)
            for kc in range(8):
                w = [(s_attev, 8 * p + max(kc, c) + 1)]
                if c == 0 and kc == 0 and p == 0:
                    w.append((s_conv, 80))
                em.emit("tensor", lambda e, c=c, kc=kc, col=col, T=T: e.matmul(
                    PS[:, col: col + T],
                    p1_sb[:, kc * 1024 + c * 128: kc * 1024 + c * 128 + 128],
                    att_bf[:, kc * 384: kc * 384 + T],
                    start=(kc == 0), stop=(kc == 7), **MMK),
                    waits=w, inc=(s_p1pe if kc == 7 else None))
            em.emit("scalar", lambda e, c=c, col=col, T=T: e.activation(
                h1_bf[:, c * 384: c * 384 + T], PS[:, col: col + T],
                AF.Gelu, bias=p1b_sb[:, c:c + 1]),
                waits=[(s_p1pe, s_p1pe.n)], inc=s_gelev, tset="gelu")

        P2BASE = s_p2pe.n
        for tb in range(ntb):
            for hc in range(8):
                for dh in range(2):
                    w = []
                    if dh == 0:
                        w = [(s_gelev, 8 * p + max(hc + 1, 2 * tb + 2))]
                        if hc == 0 and tb == 0 and p == 0:
                            w.append((s_conv, 80))
                    em.emit("tensor", lambda e, tb=tb, hc=hc, dh=dh: e.matmul(
                        PS[:, tb * 1024 + dh * 512: tb * 1024 + dh * 512 + 512],
                        h1_bf[:, hc * 384 + tb * 128: hc * 384 + tb * 128 + 128],
                        p2_sb[:, hc * 1024 + dh * 512: hc * 1024 + dh * 512 + 512],
                        start=(hc == 0), stop=False, **MMK),
                        waits=w, inc=None)
            for dh in range(2):
                em.emit("tensor", lambda e, tb=tb, dh=dh: e.matmul(
                    PS[:, tb * 1024 + dh * 512: tb * 1024 + dh * 512 + 512],
                    onesb_sb,
                    p2b_bf[:, dh * 512: dh * 512 + 512],
                    start=False, stop=True, **MMK),
                    waits=[(s_initc, INIT_EVAC)] if (tb == 0 and dh == 0) else [],
                    inc=(s_p2pe if dh == 1 else None))

        for tb in range(ntb):
            w = [(s_p2pe, P2BASE + tb + 1)]
            if tb == 0:
                w.append((s_odma, 16 * p))
            em.emit("vector", lambda e, tb=tb: e.tensor_tensor(
                s_tm[:, tb * 1024:(tb + 1) * 1024],
                PS[:, tb * 1024:(tb + 1) * 1024],
                xn_tm[:, tb * 1024:(tb + 1) * 1024], ALU.add),
                waits=w, inc=(s_sev if tb == ntb - 1 else None), nodrain=True)
        if p + 1 < NP_:
            xn_opB(p + 1, [(s_trB, s_trB.n)])
        for tb in range(ntb):
            em.emit("vector", lambda e, tb=tb: e.tensor_tensor(
                g_tm[:, tb * 1024:(tb + 1) * 1024],
                s_tm[:, tb * 1024:(tb + 1) * 1024],
                g_tm[:, tb * 1024:(tb + 1) * 1024], ALU.mult),
                waits=[(s_gev, GEV)] if tb == 0 else [],
                nodrain=(ntb > 1 or tb > 0))
        for tb in range(ntb):
            em.emit("vector", lambda e, tb=tb, xb=xb: e.tensor_tensor(
                s_tm[:, tb * 1024:(tb + 1) * 1024],
                g_tm[:, tb * 1024:(tb + 1) * 1024],
                xb[:, tb * 1024:(tb + 1) * 1024], ALU.add),
                inc=(s_odve if tb == ntb - 1 else None),
                nodrain=(ntb > 1 or tb > 0))

        em.emit("gpsimd", lambda e, t0=t0, ntb=ntb: e.dma_start(
            out=out_d[t0 * 128:(t0 + ntb) * 128, :].rearrange(
                "(tb p) dd -> p tb dd", p=128),
            in_=s_tm.rearrange("p (tb dd) -> p tb dd", dd=1024)[:, 0:ntb]),
            waits=[(s_odve, p + 1)], inc=s_odma, inc_n=16, is_dma=True)

    if debug == 2 and NP_ > 0:
        s_dbg2 = S("dbg2")
        dw2 = [(s_odma, 16 * NP_)]
        for c3 in range(3):
            em.emit("gpsimd", lambda e, c3=c3: e.dma_start(
                out=out_d[4096 + c3 * 128: 4096 + (c3 + 1) * 128, :],
                in_=g_tm[:, c3 * 1024:(c3 + 1) * 1024]),
                waits=dw2, inc=s_dbg2, inc_n=16)
        for c3 in range(3):
            em.emit("gpsimd", lambda e, c3=c3: e.dma_start(
                out=out_d[4480 + c3 * 128: 4480 + (c3 + 1) * 128, :],
                in_=att_bf[:, c3 * 1024:(c3 + 1) * 1024]),
                inc=s_dbg2, inc_n=16)
        for c3 in range(3):
            em.emit("gpsimd", lambda e, c3=c3: e.dma_start(
                out=out_d[4864 + c3 * 128: 4864 + (c3 + 1) * 128, :],
                in_=h1_bf[:, c3 * 1024:(c3 + 1) * 1024]),
                inc=s_dbg2, inc_n=16)
        for c3 in range(3):
            em.emit("gpsimd", lambda e, c3=c3: e.dma_start(
                out=out_d[5248 + c3 * 128: 5248 + (c3 + 1) * 128, :],
                in_=xn_fm[:, c3 * 1024:(c3 + 1) * 1024]),
                inc=s_dbg2, inc_n=16)
        for c3 in range(3):
            em.emit("gpsimd", lambda e, c3=c3: e.dma_start(
                out=out_d[5632 + c3 * 128: 5632 + (c3 + 1) * 128, :],
                in_=q_pass[0][:, c3 * 1024:(c3 + 1) * 1024]),
                inc=s_dbg2, inc_n=16)
        for c3 in range(3):
            em.emit("gpsimd", lambda e, c3=c3: e.dma_start(
                out=out_d[6016 + c3 * 128: 6016 + (c3 + 1) * 128, :],
                in_=q_pass[1][:, c3 * 1024:(c3 + 1) * 1024]),
                inc=s_dbg2, inc_n=16)
        em.emit("sync", lambda e: e.nop(), waits=[(s_dbg2, s_dbg2.n)])

    if debug == 1:
        s_dbg = S("dbg")
        dw = [(s_cwT, s_cwT.n), (s_kvfA, NB_),
              (s_qsp2[0], s_qsp2[0].n), (s_qsp2[1], s_qsp2[1].n)]
        for c4 in range(4):
            em.emit("gpsimd", lambda e, c4=c4: e.dma_start(
                out=out_d[4096 + c4 * 128: 4096 + (c4 + 1) * 128, :],
                in_=logits_all[:, c4 * 1024:(c4 + 1) * 1024]),
                waits=dw, inc=s_dbg, inc_n=16)
        em.emit("gpsimd", lambda e: e.dma_start(
            out=out_d[4608:4736, 0:64], in_=stats_m), inc=s_dbg, inc_n=16)
        em.emit("gpsimd", lambda e: e.dma_start(
            out=out_d[4736:4864, 0:64], in_=stats_r), inc=s_dbg, inc_n=16)
        for c4 in range(4):
            em.emit("gpsimd", lambda e, c4=c4: e.dma_start(
                out=out_d[4864 + c4 * 128: 4864 + (c4 + 1) * 128, :],
                in_=kv_sb[:, c4 * 1024:(c4 + 1) * 1024]),
                inc=s_dbg, inc_n=16)
        for qi, qt in enumerate([t for t in [0, 1, 2, 3, 33, 40, 49, 50] if t < NT_]):
            em.emit("gpsimd", lambda e, qi=qi, qt=qt: e.dma_start(
                out=out_d[7168 + qi * 128: 7296 + qi * 128, :].rearrange(
                    "p (c t) -> p c t", t=128),
                in_=qscr_d[qt].rearrange("c p t -> p c t")),
                inc=s_dbg, inc_n=16)
        for qt in range(6):
            em.emit("gpsimd", lambda e, qt=qt: e.dma_start(
                out=out_d[6144 + qt * 128: 6272 + qt * 128, :].rearrange(
                    "p (c t) -> p c t", t=128),
                in_=qscr_d[qt].rearrange("c p t -> p c t")),
                inc=s_dbg, inc_n=16)
        em.emit("gpsimd", lambda e: e.dma_start(
            out=out_d[8000:8001, :], in_=krow_sb[0:1, 1024:2048]),
            waits=dw, inc=s_dbg, inc_n=16)
        em.emit("gpsimd", lambda e: e.dma_start(
            out=out_d[8001:8002, :], in_=krow_sb[0:1, 0:1024]),
            waits=dw, inc=s_dbg, inc_n=16)
        em.emit("gpsimd", lambda e: e.dma_start(
            out=out_d[8002:8003, :], in_=brow_sb),
            waits=dw, inc=s_dbg, inc_n=16)
        em.emit("gpsimd", lambda e: e.dma_start(
            out=out_d[8003:8004, 0:16], in_=knsum_col[0:1, 0:16]),
            waits=dw, inc=s_dbg, inc_n=16)
        em.emit("gpsimd", lambda e: e.dma_start(
            out=out_d[7040:7168, :], in_=k16_sb), waits=dw, inc=s_dbg, inc_n=16)
        em.emit("gpsimd", lambda e: e.dma_start(
            out=out_d[6912:7040, :], in_=v16_sb), waits=dw, inc=s_dbg, inc_n=16)
        for c2 in range(2):
            em.emit("gpsimd", lambda e, c2=c2: e.dma_start(
                out=out_d[5504 + c2 * 64: 5504 + (c2 + 1) * 64, :],
                in_=cwT[0:64, c2 * 1024:(c2 + 1) * 1024]),
                inc=s_dbg, inc_n=16)
        em.emit("sync", lambda e: e.nop(), waits=[(s_dbg, s_dbg.n)])
    if NP_ > 0:
        em.emit("sync", lambda e: e.nop(), waits=[(s_odma, 16 * NP_)])

    em.check()

    with nc.Block() as block:
        @block.sync
        def _(eng):
            em.replay("sync", eng)

        @block.scalar
        def _(eng):
            em.replay("scalar", eng)

        @block.vector
        def _(eng):
            em.replay("vector", eng)

        @block.tensor
        def _(eng):
            em.replay("tensor", eng)

        @block.gpsimd
        def _(eng):
            em.replay("gpsimd", eng)

    ctx.close()
    return nc


def _prep_shared(inp):
    f32 = np.float32
    qkv_wT = np.ascontiguousarray(np.asarray(inp["L0_attn_qkv_w"], f32).T)
    p1_wT = np.ascontiguousarray(np.asarray(inp["L0_attn_p1_w"], f32).T)
    p2_wT = np.ascontiguousarray(np.asarray(inp["L0_attn_p2_w"], f32).T)
    rw = np.ascontiguousarray(np.asarray(inp["L0_ffn_router_w"], f32))
    w1 = np.asarray(inp["L0_ffn_w1"], f32)
    w2 = np.asarray(inp["L0_ffn_w2"], f32)
    sw1 = np.asarray(inp["L0_ffn_sw1"], f32)
    sw2 = np.asarray(inp["L0_ffn_sw2"], f32)
    units = []
    for e in range(64):
        units.append(np.concatenate([w1[e].reshape(-1), w2[e].reshape(-1)]))
    for es in range(2):
        for sl in range(8):
            a = np.ascontiguousarray(sw1[es][:, sl * 256:(sl + 1) * 256]).reshape(-1)
            b = np.ascontiguousarray(sw2[es][sl * 256:(sl + 1) * 256, :]).reshape(-1)
            units.append(np.concatenate([a, b]))
    wall = np.ascontiguousarray(np.stack(units))

    b1 = np.asarray(inp["L0_ffn_b1"], f32)
    sb1 = np.asarray(inp["L0_ffn_sb1"], f32)
    bias1 = np.zeros((128, 2 * NU), f32)
    bias1[:, 0:128] = b1.reshape(64, 2, 128).transpose(2, 0, 1).reshape(128, 128)
    bias1[:, 128:160] = sb1.reshape(2, 8, 2, 128).transpose(3, 0, 1, 2).reshape(128, 32)
    b2ext = np.ascontiguousarray(np.concatenate(
        [np.asarray(inp["L0_ffn_b2"], f32), np.asarray(inp["L0_ffn_sb2"], f32)],
        axis=0))

    qkv_b = np.asarray(inp["L0_attn_qkv_b"], f32)
    qb = np.ascontiguousarray(qkv_b[0:1024].reshape(8, 128).T)
    p1b = np.ascontiguousarray(np.asarray(inp["L0_attn_p1_b"], f32).reshape(8, 128).T)
    rows7 = np.concatenate([
        qkv_b[1024:2048], qkv_b[2048:3072],
        np.tile(np.asarray(inp["L0_attn_kn_g"], f32), H),
        np.tile(np.asarray(inp["L0_attn_kn_b"], f32), H),
        np.tile(np.asarray(inp["L0_attn_vn_g"], f32), H),
        np.tile(np.asarray(inp["L0_attn_vn_b"], f32), H),
        np.asarray(inp["L0_attn_p2_b"], f32),
    ])[None, :]
    rb = np.ascontiguousarray(np.asarray(inp["L0_ffn_router_b"], f32)[None, :])
    temp = np.asarray(inp["L0_attn_temperature"], f32).reshape(H)
    temp_eo = np.ascontiguousarray(
        np.concatenate([temp[0::2], temp[1::2]])[None, :])
    abias = np.ascontiguousarray(np.asarray(inp["L0_attn_attn_bias"], f32))

    onescol = np.ones((128, 1), ml_dtypes.bfloat16)
    kn_g_a = np.asarray(inp["L0_attn_kn_g"], f32)
    kn_b_a = np.asarray(inp["L0_attn_kn_b"], f32)
    cols4 = np.zeros((128, 4), f32)
    cols4[:, 0] = np.tile(kn_g_a, 2)
    cols4[:, 1] = np.tile(kn_b_a, 2)
    oh = np.zeros((64, 64 * 128), ml_dtypes.bfloat16)
    for e in range(64):
        oh[e, e * 128:(e + 1) * 128] = 1.0
    idf = np.eye(128, dtype=f32)
    onesf = np.ones((1, 128), f32)
    onesb = np.ones((1, 128), ml_dtypes.bfloat16)

    return dict(qkv_wT=qkv_wT, p1_wT=p1_wT, p2_wT=p2_wT, router_w=rw, wall=wall,
                bias1=bias1, b2ext=b2ext, qb=qb, p1b=p1b, rows7=rows7, rb=rb,
                temp_eo=temp_eo, attn_bias=abias, oh=oh, idf=idf, onesf=onesf,
                onesb=onesb, onescol=onescol, cols4=cols4)


def kernel(**inputs):
    x = np.asarray(inputs["L0_attn_x"], np.float32)
    shared = _prep_shared(inputs)
    nc = build_nc()
    in_maps = []
    for c in range(NCORES):
        m = dict(shared)
        m["x"] = np.ascontiguousarray(x[c * NB:(c + 1) * NB].reshape(NT * 128, D))
        in_maps.append(m)
    res = run_bass_kernel_spmd(nc, in_maps, core_ids=list(range(NCORES)))
    outs = [np.asarray(res.results[c]["out"]).reshape(NB, 2048, D)
            for c in range(NCORES)]
    return np.concatenate(outs, axis=0)


if __name__ == "__main__":
    import jax
    jax.config.update("jax_platforms", "cpu")
    import reference
    inp = reference.setup_inputs()
    got = kernel(**inp)
    ref = np.asarray(reference.reference(**inp), np.float32)
    d = got - ref
    print("rvar:", float((d * d).mean() / (ref * ref).mean()))


# version 13
# speedup vs baseline: 1.0333x; 1.0096x over previous
import contextlib
import numpy as np
import ml_dtypes
import concourse.bass as bass
import concourse.mybir as mybir
from concourse.bass_utils import run_bass_kernel_spmd

F32 = mybir.dt.float32
BF16 = mybir.dt.bfloat16
U8 = mybir.dt.uint8
AF = mybir.ActivationFunctionType
ALU = mybir.AluOpType
AX = mybir.AxisListType

NCORES = 8
NT = 64
TPB = 16
NB = 4
NU = 80
D = 1024
H = 16
EPS = 1e-5

PASS_TILES = [list(range(i, min(i + 3, NT))) for i in range(0, NT, 3)]
NPASS = len(PASS_TILES)
MMK = dict(skip_group_check=True)


class Ctr:
    def __init__(self, handle, name):
        self.h = handle
        self.name = name
        self.n = 0


class Emitter:
    ENGS = ("sync", "scalar", "vector", "tensor", "gpsimd")

    def __init__(self):
        self.ops = {e: [] for e in self.ENGS}
        self.last_wait = {}
        self.future = []

    def emit(self, eng, fn, waits=(), inc=None, inc_n=1, future=False,
             is_dma=False, nodrain=False, tset=None):
        wl = []
        for (sem, val) in waits:
            if val <= 0:
                continue
            if not future:
                assert val <= sem.n, \
                    f"[{eng}] wait on future count {sem.name} {val} > {sem.n}"
            else:
                self.future.append((sem, val))
            key = (eng, sem.name)
            if self.last_wait.get(key, 0) >= val:
                continue
            self.last_wait[key] = val
            wl.append((sem, val))
        self.ops[eng].append((fn, wl, inc, inc_n, is_dma or nodrain, tset))
        if inc is not None:
            inc.n += inc_n

    def check(self):
        for (sem, val) in self.future:
            assert val <= sem.n, f"future wait {sem.name} {val} > final {sem.n}"

    def replay(self, eng_name, eng):
        POLICY = "C"
        last_tset = None
        prev_inc = False
        for (fn, waits, inc, inc_n, skip_drain, tset) in self.ops[eng_name]:
            drained = False
            if eng_name == "scalar" and POLICY in ("A", "C"):
                eng.drain()
                drained = True
            if eng_name == "vector" and POLICY == "A":
                eng.drain()
                drained = True
            if not drained and eng_name in ("vector", "scalar") and prev_inc:
                eng.drain()
                drained = True
            if (not drained and eng_name == "vector" and not skip_drain):
                eng.drain()
                drained = True
            if eng_name == "scalar" and tset is not None:
                if last_tset is not None and tset != last_tset and not drained:
                    eng.drain()
                last_tset = tset
            prev_inc = inc is not None
            for (sem, val) in waits:
                eng.wait_ge(sem.h, val)
            ins = fn(eng)
            if inc is not None:
                assert ins is not None
                ins.then_inc(inc.h, inc_n)


class Carver:
    def __init__(self, arena_ap, base, limit):
        self.a = arena_ap
        self.off = base
        self.limit = limit

    def alloc(self, shape, dtype):
        esz = {F32: 4, BF16: 2, U8: 1}[dtype]
        nb = shape[-1] * esz
        off = (self.off + 63) & ~63
        self.off = off + nb
        assert self.off <= self.limit, f"carver overflow {self.off} > {self.limit}"
        return self.a[0:shape[0], off:off + nb].bitcast(dtype)


def build_nc(nt=NT, npasses=None, wconv=True, debug=False, acut=99, kvterms=4):
    NT_ = nt
    NB_ = nt // TPB
    PT = [list(range(i, min(i + 3, nt))) for i in range(0, nt, 3)]
    if npasses is not None:
        PT = PT[:npasses]
    NP_ = len(PT)
    nc = bass.Bass("TRN2", target_bir_lowering=False, debug=False,
                   detect_race_conditions=False)

    def din(name, shape, dtype):
        return nc.dram_tensor(name, list(shape), dtype, kind="ExternalInput").ap()

    x_d = din("x", (NT * 128, D), F32)
    qkvwT_d = din("qkv_wT", (D, 3 * D), F32)
    p1wT_d = din("p1_wT", (D, D), F32)
    p2wT_d = din("p2_wT", (D, D), F32)
    rw_d = din("router_w", (D, 64), F32)
    wall_d = din("wall", (NU, 524288), F32)
    bias1_d = din("bias1", (128, 2 * NU), F32)
    b2e_d = din("b2ext", (66, D), F32)
    qb_d = din("qb", (128, 8), F32)
    p1b_d = din("p1b", (128, 8), F32)
    rows7_d = din("rows7", (1, 7 * D), F32)
    rb_d = din("rb", (1, 64), F32)
    tempeo_d = din("temp_eo", (1, 16), F32)
    abias_d = din("attn_bias", (16, 64, 64), F32)
    oh_d = din("oh", (64, 64 * 128), BF16)
    idf_d = din("idf", (128, 128), F32)
    onesf_d = din("onesf", (1, 128), F32)
    onesb_d = din("onesb", (1, 128), BF16)
    onescol_d = din("onescol", (128, 1), BF16)
    cols4_d = din("cols4", (128, 4), F32)

    out_d = nc.dram_tensor("out", [NT * 128, D], F32, kind="ExternalOutput").ap()
    wscr_d = nc.dram_tensor("wscr", [NU, 128, 4096], BF16, kind="Internal").ap()
    qscr_d = nc.dram_tensor("qscr", [NT, 8, 128, 128], BF16, kind="Internal").ap()

    em = Emitter()
    ctx = contextlib.ExitStack()

    def S(name):
        return Ctr(ctx.enter_context(nc.semaphore(name)), name)

    s_small = S("small"); s_conv = S("conv"); s_wconv = S("wconv")
    s_init = S("init"); s_initc = S("initc"); s_initcS = S("initcS")
    s_xldA2 = [S("xldA0"), S("xldA1")]; s_xfreeA = S("xfreeA")
    s_trA = S("trA"); s_trcA = S("trcA")
    s_qpeA = S("qpeA"); s_qevA = S("qevA")
    s_kpeA = S("kpeA"); s_vpeA = S("vpeA"); s_kvpeA = S("kvpeA"); s_lpeA = S("lpeA")
    s_kevA = S("kevA"); s_vevA = S("vevA"); s_levA = S("levA")
    s_k16A = S("k16A"); s_v16A = S("v16A"); s_kvfA = S("kvfA")
    s_xnA = S("xnA"); s_sqA = S("sqA"); s_rstdA = S("rstdA")
    s_hsubA = S("hsubA"); s_varA = S("varA")
    s_sqxA = S("sqxA"); s_sumA = S("sumA")
    s_fA = S("fA"); s_fP = S("fP"); s_fD = S("fD"); s_fM = S("fM")
    s_qsp2 = [S("qsp0"), S("qsp1")]
    s_a3d = S("a3d"); s_a3e = S("a3e"); s_a3p = S("a3p"); s_a3t = S("a3t")
    s_cwT = S("cwT"); s_ohld = S("ohld")
    s_xldB = S("xldB"); s_qldB = S("qldB")
    s_wld3 = [S("wld0"), S("wld1"), S("wld2")]
    s_w2ld = [S("w2ld0"), S("w2ld1")]
    s_xnB = S("xnB"); s_trB = S("trB"); s_trcB = S("trcB")
    s_peh = S("peh"); s_silu = S("silu"); s_hsc = S("hsc")
    s_cwbpe = S("cwbpe"); s_cwbc = S("cwbc"); s_mm2 = S("mm2"); s_rdone = S("rdone")
    s_gev = S("gev"); s_attpe = S("attpe"); s_attev = S("attev")
    s_p1pe = S("p1pe"); s_gelev = S("gelev"); s_p2pe = S("p2pe")
    s_sev = S("sev"); s_odve = S("odve"); s_odma = S("odma")

    def sbuf(name, shape, dtype):
        return ctx.enter_context(nc.sbuf_tensor(name, list(shape), dtype)).ap()

    stats_m = sbuf("stats_m", [128, NT], F32)
    stats_r = sbuf("stats_r", [128, NT], F32)
    cwT = sbuf("cwT", [66, 64 * 128], BF16)
    kv_sb = sbuf("kv_sb", [128, NB * 1024], BF16)
    p1_sb = sbuf("p1_sb", [128, 8 * 1024], BF16)
    p2_sb = sbuf("p2_sb", [128, 8 * 1024], BF16)
    router_sb = sbuf("router_sb", [128, 8 * 64], BF16)
    b2e_sb = sbuf("b2e_sb", [66, 1024], BF16)
    bias1_sb = sbuf("bias1_sb", [128, 2 * NU], F32)
    qb_sb = sbuf("qb_sb", [128, 8], F32)
    p1b_sb = sbuf("p1b_sb", [128, 8], F32)
    p2b_bf = sbuf("p2b_bf", [1, 1024], BF16)
    onesf_sb = sbuf("onesf_sb", [1, 128], F32)
    onesb_sb = sbuf("onesb_sb", [1, 128], BF16)
    idf_sb = sbuf("idf_sb", [128, 128], F32)
    tempcol = sbuf("tempcol", [128, 8], F32)
    tempeo_sb = sbuf("tempeo_sb", [1, 16], F32)
    rbrow_sb = sbuf("rbrow_sb", [1, 64], F32)
    eps_col = sbuf("eps_col", [128, 1], F32)
    onescol_sb = sbuf("onescol_sb", [128, 1], BF16)
    cols4_sb = sbuf("cols4_sb", [128, 4], F32)
    kb_bf = sbuf("kb_bf", [1, 1024], BF16)
    vb_bf = sbuf("vb_bf", [1, 1024], BF16)
    rb_bf = sbuf("rb_bf", [1, 64], BF16)

    arena = ctx.enter_context(nc.sbuf_tensor("ov", [128, 143168], U8)).ap()

    ca = Carver(arena, 0, 143168)
    qkv_sb = ca.alloc([128, 8 * 3072], BF16)
    vg_bc = ca.alloc([128, 1024], F32)
    vb2_bc = ca.alloc([128, 1024], F32)
    gkgv_pair = ca.alloc([128, 1024], F32)
    bkbv_pair = ca.alloc([128, 1024], F32)
    kvtmp2 = ca.alloc([128, 1024], F32)
    knsum_col = ca.alloc([128, 16], F32)
    krow_off = (ca.off + 63) & ~63
    krow_sb = ca.alloc([1, 2048], F32)
    vrow32_sb = arena[32:33, krow_off:krow_off + 4096].bitcast(F32)
    brow_sb = ca.alloc([1, 1024], F32)
    r7base = ca.off
    rows7_sb = ca.alloc([1, 7 * 1024], F32)
    ca.off = r7base
    xa = [ca.alloc([128, 1024], F32) for _ in range(2)]
    xn_tm_a = ca.alloc([128, 1024], F32)
    kvtmp = ca.alloc([128, 1024], F32)
    sq_buf = ca.alloc([128, 1024], F32)
    kf_buf = ca.alloc([128, 1024], F32)
    k16_sb = ca.alloc([128, 1024], BF16)
    v16_sb = ca.alloc([128, 1024], BF16)
    q_tile = [ca.alloc([128, 8 * 128], BF16) for _ in range(2)]
    xnfmA = ca.alloc([128, 8 * 128], BF16)
    smA = ca.alloc([128, 16], F32)
    st16 = ca.alloc([128, 64], F32)
    a3buf = ca.alloc([128, 512], F32)
    a3max = ca.alloc([128, 16], F32)
    a3cw = ca.alloc([128, 256], F32)
    abias_sb = ca.alloc([128, 1024], F32)
    logits_all = ca.alloc([128, NT * 64], F32)

    cb = Carver(arena, 0, 143168)
    x_tm = [cb.alloc([128, 3 * 1024], F32) for _ in range(2)]
    xn_tm = cb.alloc([128, 3 * 1024], F32)
    xn_fm = cb.alloc([128, 8 * 384], BF16)
    q_pass = [cb.alloc([128, 8 * 384], BF16) for _ in range(2)]
    att_bf = cb.alloc([128, 8 * 384], BF16)
    h1_bf = cb.alloc([128, 8 * 384], BF16)
    s_tm = cb.alloc([128, 3 * 1024], F32)
    g_tm = cb.alloc([128, 3 * 1024], F32)
    hs_sb = [cb.alloc([128, 2 * 384], BF16) for _ in range(2)]
    hsc_sb = [cb.alloc([128, 2 * 384], BF16) for _ in range(2)]
    cwb_sb = [cb.alloc([128, 384], BF16) for _ in range(2)]
    wu_sb = [cb.alloc([128, 4096], BF16) for _ in range(3)]
    oh_sb = cb.alloc([64, 64 * 128], BF16)

    PS = ctx.enter_context(nc.psum_tensor("PS", [128, 4096], F32)).ap()

    def dma(eng, dst, src, inc, waits=(), future=False):
        em.emit(eng, lambda e, d=dst, s=src: e.dma_start(out=d, in_=s),
                waits=waits, inc=inc, inc_n=16, future=future, is_dma=True)

    dma("sync", bias1_sb, bias1_d, s_small)
    dma("sync", qb_sb, qb_d, s_small)
    dma("sync", p1b_sb, p1b_d, s_small)
    dma("sync", rows7_sb, rows7_d, s_small)
    dma("sync", onesf_sb, onesf_d, s_small)
    dma("sync", onesb_sb, onesb_d, s_small)
    dma("sync", idf_sb, idf_d, s_small)
    dma("sync", tempeo_sb, tempeo_d, s_small)
    dma("sync", rbrow_sb, rb_d, s_small)
    dma("sync", onescol_sb, onescol_d, s_small)
    dma("sync", cols4_sb, cols4_d, s_small)
    em.emit("vector", lambda e: e.memset(abias_sb, 0.0), inc=s_initc)
    em.emit("vector", lambda e: e.memset(eps_col, EPS), inc=s_initc)
    em.emit("vector", lambda e: e.memset(PS[32:33, 3072:4096], 0.0), inc=s_initc)
    EPSC = s_initc.n
    if debug == 1:
        em.emit("vector", lambda e: e.memset(logits_all, 0.0), inc=s_initc)
        em.emit("vector", lambda e: e.memset(stats_m, 0.0), inc=s_initc)
        em.emit("vector", lambda e: e.memset(stats_r, 0.0), inc=s_initc)
        em.emit("vector", lambda e: e.memset(kv_sb, 0.0), inc=s_initc)
        em.emit("vector", lambda e: e.memset(cwT[0:64, :], 0.0), inc=s_initc)
    ab_even = abias_sb.rearrange("p (c x) -> p c x", x=128)[0:64, :, 0:64]
    ab_odd = abias_sb.rearrange("p (c x) -> p c x", x=128)[64:128, :, 64:128]
    ab_src = abias_d.rearrange("(c par) dd e -> par dd c e", par=2)
    dma("sync", ab_even, ab_src[0], s_small, waits=[(s_initc, 1)])
    dma("sync", ab_odd, ab_src[1], s_small)
    SMALL_ALL = s_small.n

    dma("gpsimd", qkv_sb.rearrange("p (kc n) -> p kc n", n=3072),
        qkvwT_d.rearrange("(kc p) n -> p kc n", p=128), s_conv)
    dma("gpsimd", p1_sb.rearrange("p (kc n) -> p kc n", n=1024),
        p1wT_d.rearrange("(kc p) n -> p kc n", p=128), s_conv)
    dma("gpsimd", p2_sb.rearrange("p (kc n) -> p kc n", n=1024),
        p2wT_d.rearrange("(kc p) n -> p kc n", p=128), s_conv)
    dma("gpsimd", router_sb.rearrange("p (kc n) -> p kc n", n=64),
        rw_d.rearrange("(kc p) n -> p kc n", p=128), s_conv)
    dma("gpsimd", b2e_sb, b2e_d, s_conv)

    bc_targets = {4: vg_bc, 5: vb2_bc}
    for r in (4, 5):
        for hh in range(2):
            em.emit("tensor", lambda e, r=r, hh=hh: e.matmul(
                PS[:, (r % 2) * 512:(r % 2) * 512 + 512],
                onesf_sb,
                rows7_sb[:, r * 1024 + hh * 512: r * 1024 + hh * 512 + 512],
                start=True, stop=True, **MMK),
                waits=[(s_small, SMALL_ALL), (s_initc, s_initc.n)], inc=s_init)
            em.emit("vector", lambda e, r=r, hh=hh: e.tensor_copy(
                bc_targets[r][:, hh * 512: hh * 512 + 512],
                PS[:, (r % 2) * 512:(r % 2) * 512 + 512]),
                waits=[(s_init, s_init.n)], inc=s_initc)
    em.emit("scalar", lambda e: e.copy(kb_bf, rows7_sb[:, 0:1024]),
            waits=[(s_small, SMALL_ALL)], inc=s_initcS)
    em.emit("scalar", lambda e: e.copy(vb_bf, rows7_sb[:, 1024:2048]), inc=s_initcS)
    em.emit("scalar", lambda e: e.copy(rb_bf, rbrow_sb), inc=s_initcS)
    em.emit("vector", lambda e: e.tensor_scalar(
        gkgv_pair, vg_bc, cols4_sb[:, 0:1], None, ALU.mult),
        waits=[(s_initc, s_initc.n)], inc=s_initc)
    em.emit("vector", lambda e: e.tensor_scalar(
        bkbv_pair, vb2_bc, cols4_sb[:, 1:2], 2048.0, ALU.mult, ALU.mult),
        inc=s_initc)
    em.emit("tensor", lambda e: e.matmul(PS[0:64, 3 * 512: 3 * 512 + 8],
                                         onesf_sb[0:1, 0:64], tempeo_sb[0:1, 0:8],
                                         start=True, stop=True, **MMK), inc=s_init)
    em.emit("tensor", lambda e: e.matmul(PS[64:128, 3 * 512: 3 * 512 + 8],
                                         onesf_sb[0:1, 0:64], tempeo_sb[0:1, 8:16],
                                         start=True, stop=True, **MMK), inc=s_init)
    em.emit("scalar", lambda e: e.activation(tempcol, PS[:, 3 * 512: 3 * 512 + 8],
                                             AF.Copy, scale=1.0 / 2048.0),
            waits=[(s_init, s_init.n)], inc=s_initcS)
    em.emit("scalar", lambda e: e.copy(p2b_bf, rows7_sb[:, 6 * 1024: 7 * 1024]),
            waits=[(s_small, SMALL_ALL)], inc=s_initcS)
    INIT_EVAC = s_initc.n
    INIT_S = s_initcS.n

    for u in (range(NU) if wconv else []):
        w1src = wall_d[u, 0:262144].rearrange("(kc p jc j) -> p kc jc j",
                                              kc=8, p=128, jc=2, j=128)
        w2src = wall_d[u, 262144:524288].rearrange("(jcc p dd) -> p jcc dd",
                                                   jcc=2, p=128, dd=1024)
        w1dst = wscr_d[u][:, 0:2048].rearrange("p (kc jc j) -> p kc jc j",
                                               kc=8, jc=2, j=128)
        w2dst = wscr_d[u][:, 2048:4096].rearrange("p (jcc dd) -> p jcc dd",
                                                  jcc=2, dd=1024)
        dma("gpsimd", w1dst, w1src, s_wconv)
        dma("gpsimd", w2dst, w2src, s_wconv)

    def emit_xln(j):
        pj = j % 2
        xb = xa[pj]
        B = pj * 8
        xw = [(s_xfreeA, j - 1), (s_sqxA, j - 1)]
        if j < 2:
            xw += [(s_initc, INIT_EVAC), (s_initcS, INIT_S)]
        dma("sync", xb, x_d[j * 128:(j + 1) * 128, :], s_xldA2[pj],
            waits=xw, future=(j > 1))
        em.emit("vector", lambda e, xb=xb, B=B: e.reduce_sum(
            smA[:, B:B + 1], xb, AX.X),
            waits=[(s_xldA2[pj], 16 * (j // 2 + 1)), (s_initc, INIT_EVAC),
                   (s_initcS, INIT_S)], nodrain=True)
        em.emit("vector", lambda e, B=B: e.scalar_tensor_tensor(
            smA[:, B + 3:B + 4], smA[:, B:B + 1], 1.0 / (1024.0 * 1024.0),
            smA[:, B:B + 1], ALU.mult, ALU.mult))
        sqw = [(s_xldA2[pj], 16 * (j // 2 + 1)), (s_initc, EPSC)]
        if j > 0:
            sqw.append((s_trA, 8 * j))
        em.emit("scalar", lambda e, xb=xb, B=B: e.activation(
            xn_tm_a, xb, AF.Square, accum_out=smA[:, B + 1:B + 2]),
            waits=sqw, inc=s_sqxA, tset="arith")
        em.emit("vector", lambda e, B=B: e.tensor_scalar(
            smA[:, B + 2:B + 3], smA[:, B + 1:B + 2], 1.0 / 1024.0, None,
            ALU.mult),
            waits=[(s_sqxA, j + 1)], nodrain=True)
        em.emit("vector", lambda e, j=j, B=B: e.tensor_scalar(
            stats_m[:, j:j + 1], smA[:, B:B + 1], 1.0 / 1024.0, None,
            ALU.mult), nodrain=True)
        em.emit("vector", lambda e, B=B: e.tensor_tensor(
            smA[:, B + 4:B + 5], smA[:, B + 2:B + 3], smA[:, B + 3:B + 4],
            ALU.subtract), inc=s_varA, nodrain=True)
        em.emit("scalar", lambda e, B=B: e.activation(
            smA[:, B + 2:B + 3], smA[:, B + 4:B + 5], AF.Sqrt, bias=eps_col),
            waits=[(s_varA, s_varA.n)], inc=s_sqA, tset="arith")
        em.emit("vector", lambda e, j=j, B=B: e.reciprocal(
            stats_r[:, j:j + 1], smA[:, B + 2:B + 3]),
            waits=[(s_sqA, s_sqA.n)], inc=s_rstdA, nodrain=True)
        em.emit("vector", lambda e, j=j, B=B: e.tensor_tensor(
            smA[:, B + 5:B + 6], stats_m[:, j:j + 1], stats_r[:, j:j + 1],
            ALU.mult))
        em.emit("vector", lambda e, xb=xb, j=j, B=B: e.tensor_scalar(
            xn_tm_a, xb, stats_r[:, j:j + 1], smA[:, B + 5:B + 6],
            ALU.mult, ALU.subtract), inc=s_xnA)
        em.emit("vector", lambda e: e.engine_nop(), inc=s_xfreeA, nodrain=True)

    def emit_A(j):
        tj = j % TPB
        bj = j // TPB
        pj = j % 2
        qtb = q_tile[pj]
        for c in (range(8) if acut >= 2 else []):
            bnk = c % 2
            w = [(s_xnA, j + 1)]
            if c >= 2:
                w.append((s_trcA, 8 * j + c - 1))
            if j == 0:
                w += [(s_initc, INIT_EVAC), (s_initcS, INIT_S)]
            else:
                if c == 0:
                    w.append((s_kevA, j))
                    if tj == 0:
                        w.append((s_kvfA, bj))
            em.emit("tensor", lambda e, c=c, bnk=bnk: e.transpose(
                PS[:, bnk * 512 + (c // 2) * 128: bnk * 512 + (c // 2) * 128 + 128],
                xn_tm_a[:, c * 128:(c + 1) * 128], idf_sb),
                waits=w, inc=s_trA)
            em.emit("scalar", lambda e, c=c, bnk=bnk: e.activation(
                xnfmA[:, c * 128:(c + 1) * 128],
                PS[:, bnk * 512 + (c // 2) * 128: bnk * 512 + (c // 2) * 128 + 128],
                AF.Copy),
                waits=[(s_trA, s_trA.n)], inc=s_trcA)

        for kc in (range(8) if acut >= 3 else []):
            for mc in range(8):
                bnk = 2 + mc % 2
                reg = (mc // 2) * 128
                w = []
                if mc == 0:
                    w.append((s_trcA, 8 * j + kc + 1))
                    if kc == 0:
                        w.append((s_conv, 80))
                        if j == 0:
                            w += [(s_initc, INIT_EVAC), (s_initcS, INIT_S)]
                        else:
                            w += [(s_vevA, j), (s_qevA, 8 * j), (s_levA, j)]
                em.emit("tensor", lambda e, kc=kc, mc=mc, bnk=bnk, reg=reg: e.matmul(
                    PS[:, bnk * 512 + reg: bnk * 512 + reg + 128],
                    qkv_sb[:, kc * 3072 + mc * 128: kc * 3072 + mc * 128 + 128],
                    xnfmA[:, kc * 128:(kc + 1) * 128],
                    start=(kc == 0 and mc < 2), stop=(kc == 7), **MMK),
                    waits=w, inc=(s_qpeA if (kc == 7 and mc == 7) else None))
        for mc in (range(8) if acut >= 3 else []):
            bnk = 2 + mc % 2
            reg = (mc // 2) * 128
            em.emit("scalar", lambda e, mc=mc, bnk=bnk, reg=reg, qtb=qtb: e.activation(
                qtb[:, mc * 128:(mc + 1) * 128],
                PS[:, bnk * 512 + reg: bnk * 512 + reg + 128],
                AF.Identity, bias=qb_sb[:, mc:mc + 1]),
                waits=[(s_qpeA, j + 1),
                       (s_qsp2[pj], 16 * ((j - 2) // 2 + 1) if j >= 2 else 0)],
                inc=s_qevA)
        if acut >= 3:
            em.emit("scalar", lambda e, j=j, qtb=qtb: e.dma_start(
                out=qscr_d[j].rearrange("c p t -> p c t"),
                in_=qtb.rearrange("p (c t) -> p c t", t=128)),
                waits=[(s_qevA, 8 * (j + 1))], inc=s_qsp2[pj], inc_n=16,
                is_dma=True)

    for i in range(NT_):
        ti = i % TPB
        bi = i // TPB
        par = i % 2
        xbuf = xa[par]
        if i == 0:
            emit_xln(0)
            emit_A(0)

        for half in (range(2) if acut >= 4 else []):
            for kc in range(8):
                w = []
                if half == 0 and kc == 0:
                    w = [(s_trcA, 8 * i + 8)]
                    if i > 0:
                        w.append((s_kevA, i))
                em.emit("tensor", lambda e, half=half, kc=kc: e.matmul(
                    PS[:, half * 512: half * 512 + 512],
                    xnfmA[:, kc * 128:(kc + 1) * 128],
                    qkv_sb[:, kc * 3072 + 1024 + half * 512:
                           kc * 3072 + 1024 + half * 512 + 512],
                    start=(kc == 0), stop=False, **MMK),
                    waits=w, inc=None)
            em.emit("tensor", lambda e, half=half: e.matmul(
                PS[:, half * 512: half * 512 + 512],
                onesb_sb,
                kb_bf[:, half * 512: half * 512 + 512],
                start=False, stop=True, **MMK),
                waits=[(s_initcS, INIT_S)], inc=(s_kpeA if half == 1 else None))
        for half in (range(2) if acut >= 5 else []):
            for kc in range(8):
                w = []
                if half == 0 and kc == 0:
                    w = [(s_qevA, 8 * (i + 1))]
                em.emit("tensor", lambda e, half=half, kc=kc: e.matmul(
                    PS[:, (2 + half) * 512: (2 + half) * 512 + 512],
                    xnfmA[:, kc * 128:(kc + 1) * 128],
                    qkv_sb[:, kc * 3072 + 2048 + half * 512:
                           kc * 3072 + 2048 + half * 512 + 512],
                    start=(kc == 0), stop=False, **MMK),
                    waits=w, inc=None)
            em.emit("tensor", lambda e, half=half: e.matmul(
                PS[:, (2 + half) * 512: (2 + half) * 512 + 512],
                onesb_sb,
                vb_bf[:, half * 512: half * 512 + 512],
                start=False, stop=True, **MMK),
                waits=[], inc=(s_vpeA if half == 1 else None))

        if i + 1 < NT_:
            emit_xln(i + 1)

        def headnorm(srcb, out16, sem_ev, sem_16, pe_sem, copy_extra, sq_extra):
            for half in range(2):
                em.emit("scalar", lambda e, half=half, srcb=srcb: e.activation(
                    kf_buf[:, half * 512: half * 512 + 512],
                    PS[:, (srcb + half) * 512: (srcb + half) * 512 + 512],
                    AF.Copy),
                    waits=([(pe_sem, i + 1)] + copy_extra) if half == 0 else [],
                    inc=(sem_ev if half == 1 else None))
            em.emit("scalar", lambda e: e.activation(sq_buf, kf_buf, AF.Square),
                    waits=sq_extra, inc=s_sqA, tset="arith")
            kf3 = kf_buf.rearrange("p (h dd) -> p h dd", dd=64)
            sq3 = sq_buf.rearrange("p (h dd) -> p h dd", dd=64)
            em.emit("vector", lambda e, kf3=kf3: e.tensor_reduce(
                st16[:, 0:16], kf3, AX.X, ALU.add),
                waits=[(sem_ev, i + 1)], nodrain=True)
            em.emit("vector", lambda e, sq3=sq3: e.tensor_reduce(
                st16[:, 32:48], sq3, AX.X, ALU.add),
                waits=[(s_sqA, s_sqA.n)], nodrain=True)
            em.emit("vector", lambda e: e.tensor_scalar(
                st16[:, 16:32], st16[:, 0:16], 1.0 / 64.0, None, ALU.mult),
                nodrain=True)
            em.emit("vector", lambda e: e.tensor_scalar(
                st16[:, 48:64], st16[:, 32:48], 1.0 / 64.0, None, ALU.mult),
                nodrain=True)
            em.emit("vector", lambda e: e.tensor_tensor(
                st16[:, 0:16], st16[:, 16:32], st16[:, 16:32], ALU.mult),
                nodrain=True)
            em.emit("vector", lambda e: e.tensor_tensor(
                st16[:, 32:48], st16[:, 48:64], st16[:, 0:16], ALU.subtract),
                inc=s_varA)
            em.emit("scalar", lambda e: e.activation(st16[:, 48:64], st16[:, 32:48],
                                                     AF.Sqrt, bias=eps_col),
                    waits=[(s_varA, s_varA.n)], inc=s_sqA, tset="arith")
            em.emit("vector", lambda e: e.reciprocal(st16[:, 0:16],
                                                     st16[:, 48:64]),
                    waits=[(s_sqA, s_sqA.n)], nodrain=True)
            em.emit("vector", lambda e, kf3=kf3, sq3=sq3: e.tensor_tensor(
                sq3, kf3,
                st16[:, 16:32].unsqueeze(2).to_broadcast([128, 16, 64]),
                ALU.subtract), inc=s_hsubA, nodrain=True)
            em.emit("vector", lambda e, o=out16, sq3=sq3: e.tensor_tensor(
                o.rearrange("p (h dd) -> p h dd", dd=64), sq3,
                st16[:, 0:16].unsqueeze(2).to_broadcast([128, 16, 64]), ALU.mult),
                waits=[(s_sumA, i)], inc=sem_16)

        if acut >= 4:
            headnorm(0, k16_sb, s_kevA, s_k16A, s_kpeA, [], [])
        if acut >= 5:
            headnorm(2, v16_sb, s_vevA, s_v16A, s_vpeA,
                     [(s_hsubA, 2 * i + 1)], [(s_k16A, i + 1)])

        for kc in (range(8) if acut >= 7 else []):
            w = []
            if kc == 0:
                w = [(s_vevA, i + 1), (s_conv, 80)]
            em.emit("tensor", lambda e, kc=kc: e.matmul(
                PS[:, 1024:1088],
                xnfmA[:, kc * 128:(kc + 1) * 128],
                router_sb[:, kc * 64:(kc + 1) * 64],
                start=(kc == 0), stop=False, **MMK),
                waits=w, inc=None)
        if acut >= 7:
            em.emit("tensor", lambda e: e.matmul(
                PS[:, 1024:1088], onesb_sb, rb_bf,
                start=False, stop=True, **MMK),
                waits=[], inc=s_lpeA)
            em.emit("scalar", lambda e, i=i: e.activation(
                logits_all[:, i * 64:(i + 1) * 64], PS[:, 1024:1088], AF.Copy),
                waits=[(s_lpeA, i + 1)], inc=s_levA)

        if ti != TPB - 1 and i + 1 < NT_:
            emit_A(i + 1)


        for c in (range(8) if acut >= 6 else []):
            w = []
            if c == 0:
                w = [(s_k16A, i + 1), (s_v16A, i + 1)]
                if ti == 0:
                    w.append((s_kvfA, bi))
            em.emit("tensor", lambda e, c=c, ti=ti: e.matmul(
                PS[:, 4 * 512 + c * 128: 4 * 512 + c * 128 + 128],
                k16_sb[:, c * 128:(c + 1) * 128],
                v16_sb[:, c * 128:(c + 1) * 128],
                start=(ti == 0 and c % 4 == 0), stop=(ti == TPB - 1), **MMK),
                waits=w, inc=(s_kvpeA if c == 7 else None))
        if acut >= 6:
            for half in range(2):
                em.emit("tensor", lambda e, half=half, ti=ti: e.matmul(
                    PS[0:1, 3072 + half * 512: 3072 + half * 512 + 512],
                    onescol_sb,
                    k16_sb[:, half * 512: half * 512 + 512],
                    start=(ti == 0), stop=(ti == TPB - 1), **MMK),
                    waits=[], inc=None)
            for half in range(2):
                em.emit("tensor", lambda e, half=half, ti=ti: e.matmul(
                    PS[32:33, 3072 + half * 512: 3072 + half * 512 + 512],
                    onescol_sb,
                    v16_sb[:, half * 512: half * 512 + 512],
                    start=(ti == 0), stop=(ti == TPB - 1), **MMK),
                    waits=[], inc=(s_sumA if half == 1 else None))

        if ti == TPB - 1 and acut >= 6:
            em.emit("scalar", lambda e: e.activation(
                krow_sb[0:1, 0:1024], PS[0:1, 3072:4096], AF.Copy),
                waits=[(s_sumA, TPB * (bi + 1))], inc=s_fA)
            em.emit("scalar", lambda e: e.activation(
                vrow32_sb, PS[32:33, 3072:4096], AF.Copy),
                inc=s_fA)
            FA0 = s_fA.n
            em.emit("sync", lambda e: e.dma_start(
                out=krow_sb[0:1, 1024:2048], in_=vrow32_sb),
                waits=[(s_fA, FA0)], inc=s_fM, inc_n=16, is_dma=True)
            FM0 = s_fM.n
            em.emit("vector", lambda e: e.tensor_tensor(
                brow_sb, krow_sb[0:1, 1024:2048],
                vg_bc[0:1, :], ALU.mult),
                waits=[(s_fM, FM0)], inc=s_fD)
            FD0 = s_fD.n
            for c in range(8):
                em.emit("tensor", lambda e, c=c: e.matmul(
                    PS[:, c:c + 1],
                    krow_sb[0:1, c * 128:(c + 1) * 128],
                    onesf_sb[0:1, 0:1],
                    start=True, stop=True, **MMK),
                    waits=[(s_fA, FA0), (s_levA, i + 1),
                           (s_kevA, i + 1)] if c == 0 else [],
                    inc=(s_fP if c == 7 else None))
            FP0 = s_fP.n
            for half in range(2):
                em.emit("tensor", lambda e, half=half: e.matmul(
                    PS[:, 512 + half * 512: 512 + half * 512 + 512],
                    onesf_sb, brow_sb[0:1, half * 512: half * 512 + 512],
                    start=True, stop=True, **MMK),
                    waits=[(s_fD, FD0), (s_vevA, i + 1),
                           (s_levA, i + 1)] if half == 0 else [],
                    inc=(s_fP if half == 1 else None))
            FP1 = s_fP.n
            em.emit("scalar", lambda e: e.activation(
                knsum_col[:, 0:8], PS[:, 0:8], AF.Copy),
                waits=[(s_fP, FP0)], inc=s_fA)
            FA1 = s_fA.n
            em.emit("vector", lambda e: e.tensor_scalar(
                knsum_col[:, 8:16], knsum_col[:, 0:8], cols4_sb[:, 0:1],
                None, ALU.mult), waits=[(s_fA, FA1)])
            em.emit("vector", lambda e: e.tensor_tensor(
                kvtmp, PS[:, 2048:3072], gkgv_pair, ALU.mult),
                waits=[(s_kvpeA, i + 1), (s_initc, INIT_EVAC),
                       (s_initcS, INIT_S)])
            for c in range(8):
                if kvterms >= 2:
                    em.emit("vector", lambda e, c=c: e.scalar_tensor_tensor(
                        kvtmp2[:, c * 128:(c + 1) * 128],
                        vb2_bc[:, c * 128:(c + 1) * 128],
                        knsum_col[:, 8 + c: 9 + c],
                        kvtmp[:, c * 128:(c + 1) * 128],
                        ALU.mult, ALU.add))
                else:
                    em.emit("vector", lambda e, c=c: e.tensor_copy(
                        kvtmp2[:, c * 128:(c + 1) * 128],
                        kvtmp[:, c * 128:(c + 1) * 128]))
            for c in range(8):
                if kvterms >= 3:
                    em.emit("vector", lambda e, c=c: e.scalar_tensor_tensor(
                        kvtmp[:, c * 128:(c + 1) * 128],
                        PS[:, 512 + c * 128: 512 + c * 128 + 128],
                        cols4_sb[:, 1:2],
                        kvtmp2[:, c * 128:(c + 1) * 128],
                        ALU.mult, ALU.add),
                        waits=[(s_fP, FP1)] if c == 0 else [])
                else:
                    em.emit("vector", lambda e, c=c: e.tensor_copy(
                        kvtmp[:, c * 128:(c + 1) * 128],
                        kvtmp2[:, c * 128:(c + 1) * 128]),
                        waits=[(s_fP, FP1)] if c == 0 else [])
            if kvterms >= 4:
                em.emit("vector", lambda e: e.tensor_tensor(
                    kvtmp2, kvtmp, bkbv_pair, ALU.add))
            else:
                em.emit("vector", lambda e: e.tensor_copy(kvtmp2, kvtmp))
            for c in range(8):
                em.emit("vector", lambda e, c=c, bi=bi: e.scalar_tensor_tensor(
                    kv_sb[:, bi * 1024 + c * 128: bi * 1024 + (c + 1) * 128],
                    kvtmp2[:, c * 128:(c + 1) * 128],
                    tempcol[:, c:c + 1],
                    abias_sb[:, c * 128:(c + 1) * 128],
                    ALU.mult, ALU.add),
                    waits=[(s_small, SMALL_ALL)] if c == 0 else [],
                    inc=(s_kvfA if c == 7 else None))

        if ti == TPB - 1 and i + 1 < NT_:
            emit_A(i + 1)


    em.emit("vector", lambda e: e.memset(cwT[64:66, :], 1.0), inc=s_cwT)

    def a3_bufs(i):
        par = i % 2
        return dict(
            par=par,
            zs=logits_all[:, i * 64:(i + 1) * 64],
            zc=a3buf[:, par * 256: par * 256 + 64],
            ez=a3buf[:, par * 256 + 64: par * 256 + 128],
            msk=a3buf[:, par * 256 + 128: par * 256 + 192],
            emu=a3buf[:, par * 256 + 192: par * 256 + 256],
            mx=a3max[:, par * 8: par * 8 + 8],
            ssum=smA[:, par * 8 + 3: par * 8 + 4],
            srec=smA[:, par * 8 + 4: par * 8 + 5],
            cwcols=a3cw[:, par * 128: par * 128 + 64])

    a3_list = list(range(NT_)) if acut >= 8 else []
    assert len(a3_list) % 2 == 0, "A3 pair-zip needs an even tile count"
    for i0 in range(0, len(a3_list) - 1, 2):
        pr = [a3_bufs(i0), a3_bufs(i0 + 1)]
        for s, i in ((0, i0), (1, i0 + 1)):
            b = pr[s]
            w = [(s_levA, i + 1)]
            if i >= 2:
                w.append((s_a3e, i - 1))
            em.emit("vector", lambda e, b=b: e.max(out=b["mx"], in_=b["zs"]),
                    waits=w, nodrain=True)
        for s in range(2):
            b = pr[s]
            em.emit("vector", lambda e, b=b: e.tensor_scalar(
                b["zc"], b["zs"], b["mx"][:, 0:1], None, ALU.subtract),
                nodrain=True)
        dvals = []
        for s in range(2):
            b = pr[s]
            em.emit("vector", lambda e, b=b: e.tensor_scalar(
                b["msk"], b["zs"], b["mx"][:, 7:8], None, ALU.is_ge),
                inc=s_a3d, nodrain=True)
            dvals.append(s_a3d.n)
        evals = []
        for s in range(2):
            b = pr[s]
            em.emit("scalar", lambda e, b=b: e.activation(b["ez"], b["zc"],
                                                          AF.Exp),
                    waits=[(s_a3d, dvals[s])], inc=s_a3e, tset="exp")
            evals.append(s_a3e.n)
        for s in range(2):
            b = pr[s]
            em.emit("vector", lambda e, b=b: e.tensor_tensor(
                b["emu"], b["ez"], b["msk"], ALU.mult),
                waits=[(s_a3e, evals[s])], nodrain=True)
        for s in range(2):
            b = pr[s]
            em.emit("vector", lambda e, b=b: e.reduce_sum(
                b["ssum"], b["emu"], AX.X), nodrain=True)
        for s in range(2):
            b = pr[s]
            em.emit("vector", lambda e, b=b: e.reciprocal(
                b["srec"], b["ssum"]), nodrain=True)
        for s, i in ((0, i0), (1, i0 + 1)):
            b = pr[s]
            w = [(s_a3t, i - 1)] if i >= 2 else []
            em.emit("vector", lambda e, b=b: e.tensor_scalar(
                b["cwcols"], b["emu"], b["srec"], None, ALU.mult),
                waits=w, inc=s_a3p, nodrain=True)
        for s, i in ((0, i0), (1, i0 + 1)):
            b = pr[s]
            bnk = 6 + b["par"]
            em.emit("tensor", lambda e, b=b, bnk=bnk: e.transpose(
                PS[0:64, bnk * 512: bnk * 512 + 128], b["cwcols"], idf_sb),
                waits=[(s_a3p, s_a3p.n - (1 - s)), (s_cwT, s_cwT.n),
                       (s_kvfA, NB_)],
                inc=s_a3t)
            em.emit("scalar", lambda e, i=i, bnk=bnk: e.activation(
                cwT[0:64, i * 128:(i + 1) * 128],
                PS[0:64, bnk * 512: bnk * 512 + 128], AF.Copy),
                waits=[(s_a3t, s_a3t.n)], inc=s_cwT)
    CWT_DONE = s_cwT.n

    QSP_ALL = [(s_qsp2[0], s_qsp2[0].n), (s_qsp2[1], s_qsp2[1].n)]
    if NP_ > 0:
        dma("gpsimd", oh_sb, oh_d, s_ohld, waits=[(s_cwT, CWT_DONE)], future=True)

    c_cwb = 0
    for p, tiles in enumerate(PT):
        ntb = len(tiles)
        T = ntb * 128
        par = p % 2
        xb = x_tm[par]
        qb_ = q_pass[par]
        t0 = tiles[0]

        def x_load(pp):
            xbp = x_tm[pp % 2]
            tl = PT[pp]
            em.emit("gpsimd", lambda e, xbp=xbp, tp0=tl[0], nn=len(tl):
                    e.dma_start(
                out=xbp.rearrange("p (tb dd) -> p tb dd", dd=1024)[:, 0:nn],
                in_=x_d[tp0 * 128:(tp0 + nn) * 128, :].rearrange(
                    "(tb p) dd -> p tb dd", p=128)),
                waits=[(s_odve, pp - 1), (s_ohld, 16)], inc=s_xldB, inc_n=16,
                is_dma=True, future=True)
        if p == 0:
            x_load(0)
        if p + 1 < NP_:
            x_load(p + 1)
        for tb in range(ntb):
            em.emit("gpsimd", lambda e, qb_=qb_, t0=t0, tb=tb: e.dma_start(
                out=qb_.rearrange("p (c x) -> p c x", x=384)[:, :, tb * 128:(tb + 1) * 128],
                in_=qscr_d[t0 + tb].rearrange("c p t -> p c t")),
                waits=QSP_ALL + [(s_attpe, p - 1)], inc=s_qldB,
                inc_n=16, future=True)
        QLD = s_qldB.n

        for u in range(NU):
            g = p * NU + u
            qn = "gpsimd" if g % 2 == 1 else "sync"
            w1w = [(s_wconv, (32 * NU) if wconv else 0),
                   (s_peh, max(0, 2 * (g - 2)))]
            w2w = [(s_wconv, (32 * NU) if wconv else 0), (s_mm2, g - 2)]
            if p == 0 and u < 3:
                w1w += [(s_cwT, CWT_DONE)] + QSP_ALL
                w2w += [(s_cwT, CWT_DONE)] + QSP_ALL
            dma(qn, wu_sb[g % 3][:, 0:2048], wscr_d[u][:, 0:2048],
                s_wld3[g % 2], waits=w1w, future=True)
            dma(qn, wu_sb[g % 3][:, 2048:4096], wscr_d[u][:, 2048:4096],
                s_w2ld[g % 2], waits=w2w, future=True)

        def xn_opB(pp, extra):
            tl = PT[pp]
            xbp = x_tm[pp % 2]

            def op(e, tl=tl, xbp=xbp):
                ins = None
                for j, t in enumerate(tl):
                    ins = e.tensor_scalar(
                        xn_tm[:, j * 1024:(j + 1) * 1024],
                        xbp[:, j * 1024:(j + 1) * 1024],
                        stats_m[:, t:t + 1], stats_r[:, t:t + 1],
                        ALU.subtract, ALU.mult)
                return ins
            em.emit("vector", op,
                    waits=[(s_xldB, 16 * (pp + 1)), (s_cwT, CWT_DONE)] + extra,
                    inc=s_xnB, nodrain=True)
        if p == 0:
            xn_opB(0, [])

        idx = 0
        TRB0 = s_trcB.n
        for kc in range(8):
            for tb in range(ntb):
                tens = idx % 3
                half = (idx // 3) % 2
                col = tens * 1024 + half * 512
                w = [(s_xnB, p + 1)]
                if idx >= 6:
                    w.append((s_trcB, TRB0 + idx - 5))
                if idx == 0:
                    if p == 0:
                        w += [(s_kvfA, NB_), (s_cwT, CWT_DONE)]
                    else:
                        w.append((s_sev, p))
                em.emit("tensor", lambda e, kc=kc, tb=tb, col=col: e.transpose(
                    PS[:, col: col + 128],
                    xn_tm[:, tb * 1024 + kc * 128: tb * 1024 + kc * 128 + 128],
                    idf_sb), waits=w, inc=s_trB)
                em.emit("scalar", lambda e, kc=kc, tb=tb, col=col: e.activation(
                    xn_fm[:, kc * 384 + tb * 128: kc * 384 + tb * 128 + 128],
                    PS[:, col: col + 128], AF.Copy),
                    waits=[(s_trB, s_trB.n)], inc=s_trcB)
                idx += 1
        TRC = s_trcB.n

        def cwb_mm(u_next):
            nonlocal c_cwb
            g_loc = p * NU + u_next
            em.emit("tensor", lambda e, u=u_next, t0=t0, T=T: e.matmul(
                PS[:, 7 * 512: 7 * 512 + T],
                oh_sb[:, u * 128:(u + 1) * 128],
                cwT[0:64, t0 * 128: t0 * 128 + T],
                start=True, stop=True, **MMK),
                waits=[(s_silu, 2 * g_loc), (s_ohld, 16)], inc=s_cwbpe)
            c_cwb += 1
            em.emit("scalar", lambda e, gl=g_loc, T=T: e.activation(
                cwb_sb[gl % 2][:, 0:T], PS[:, 7 * 512: 7 * 512 + T], AF.Copy),
                waits=[(s_cwbpe, s_cwbpe.n)], inc=s_cwbc)

        cwb_mm(0)

        dep_cnt = {}

        def mm1_stage(v, jc):
            gv = p * NU + v
            routed_v = v < 64
            wuv = wu_sb[gv % 3]
            hsbv = hs_sb[gv % 2]
            hscbv = hsc_sb[gv % 2] if routed_v else hs_sb[gv % 2]
            bnk = 6 + jc
            for kc in range(8):
                w = []
                if jc == 0:
                    w.append((s_trcB, TRB0 + ntb * (kc + 1)))
                if kc == 0:
                    w.append((s_wld3[gv % 2], 16 * (gv // 2 + 1)))
                    if jc == 0:
                        w.append((s_silu, max(0, 2 * gv - 1)))
                    else:
                        w.append((s_cwbc, c_cwb))
                        w.append((s_silu, 2 * gv))
                em.emit("tensor", lambda e, wu=wuv, jc=jc, kc=kc, bnk=bnk, T=T:
                        e.matmul(
                            PS[:, bnk * 512: bnk * 512 + T],
                            wu[:, (kc * 2 + jc) * 128:(kc * 2 + jc) * 128 + 128],
                            xn_fm[:, kc * 384: kc * 384 + T],
                            start=(kc == 0), stop=(kc == 7), **MMK),
                        waits=w, inc=(s_peh if kc == 7 else None), future=True)
            sw = [(s_peh, s_peh.n)]
            if jc == 0 and v >= 2:
                if v - 2 < 64:
                    sw.append(dep_cnt[(v - 2, 1)])
                else:
                    sw.append((s_mm2, gv - 1))
            em.emit("scalar", lambda e, hsb=hsbv, jc=jc, u=v, bnk=bnk, T=T:
                    e.activation(
                        hsb[:, jc * 384: jc * 384 + T],
                        PS[:, bnk * 512: bnk * 512 + T], AF.Silu,
                        bias=bias1_sb[:, 2 * u + jc: 2 * u + jc + 1]),
                    waits=sw, inc=s_silu, tset="silu", future=True)

        def hsc_stage(v, jc):
            gv = p * NU + v
            if v < 64:
                hw = [(s_silu, s_silu.n), (s_cwbc, c_cwb)]
                if jc == 0 and v >= 2:
                    hw.append((s_mm2, gv - 1))
                em.emit("vector", lambda e, hscb=hsc_sb[gv % 2],
                        hsb=hs_sb[gv % 2], jc=jc, g=gv, T=T:
                        e.tensor_tensor(
                            hscb[:, jc * 384: jc * 384 + T],
                            hsb[:, jc * 384: jc * 384 + T],
                            cwb_sb[g % 2][:, 0:T], ALU.mult),
                        waits=hw,
                        inc=s_hsc, nodrain=True, future=True)
                dep_cnt[(v, jc)] = (s_hsc, s_hsc.n)
            else:
                dep_cnt[(v, jc)] = (s_silu, s_silu.n)

        mm1_stage(0, 0)
        hsc_stage(0, 0)
        mm1_stage(0, 1)
        hsc_stage(0, 1)
        mm1_stage(1, 0)
        cwb_mm(1)
        hsc_stage(1, 0)
        mm1_stage(1, 1)
        hsc_stage(1, 1)

        for u in range(NU):
            g = p * NU + u
            routed = u < 64
            wu = wu_sb[g % 3]
            v = u + 2

            if v < NU:
                mm1_stage(v, 0)
                if v < 64:
                    cwb_mm(v)
                hsc_stage(v, 0)

            def mm2_group(jc):
                dep = dep_cnt[(u, jc)]
                cnt = 0
                for tb in range(ntb):
                    for dh in range(2):
                        col = tb * 1024 + dh * 512
                        w = ([dep, (s_w2ld[g % 2], 16 * (g // 2 + 1))]
                             if cnt == 0 else [])
                        em.emit("tensor", lambda e, jc=jc, tb=tb, dh=dh,
                                col=col, wu=wu, u=u,
                                hscb=(hsc_sb[g % 2] if routed else hs_sb[g % 2]):
                                e.matmul(
                                    PS[:, col: col + 512],
                                    hscb[:, jc * 384 + tb * 128:
                                         jc * 384 + tb * 128 + 128],
                                    wu[:, 2048 + jc * 1024 + dh * 512:
                                       2048 + jc * 1024 + dh * 512 + 512],
                                    start=(u == 0 and jc == 0), stop=False, **MMK),
                                waits=w,
                                inc=(s_mm2 if (jc == 1 and cnt == 2 * ntb - 1)
                                     else None))
                        cnt += 1
            mm2_group(0)
            if v < NU:
                mm1_stage(v, 1)
                hsc_stage(v, 1)
            mm2_group(1)

        for tb in range(ntb):
            for dh in range(2):
                col = tb * 1024 + dh * 512
                w = [(s_conv, 80)] if (tb == 0 and dh == 0) else []
                em.emit("tensor", lambda e, tb=tb, dh=dh, col=col, t0=t0: e.matmul(
                    PS[:, col: col + 512],
                    cwT[:, (t0 + tb) * 128: (t0 + tb) * 128 + 128],
                    b2e_sb[:, dh * 512: dh * 512 + 512],
                    start=False, stop=True, **MMK),
                    waits=w,
                    inc=(s_rdone if (tb == ntb - 1 and dh == 1) else None))

        for tb in range(ntb):
            for dh in range(2):
                col = tb * 1024 + dh * 512
                w = [(s_rdone, p + 1)]
                if tb == 0 and dh == 0:
                    w.append((s_odve, p))
                em.emit("scalar", lambda e, col=col: e.activation(
                    g_tm[:, col: col + 512], PS[:, col: col + 512], AF.Silu),
                    waits=w, inc=s_gev, tset="silu")
        GEV = s_gev.n

        def att_col(c):
            if c < 6:
                return (c // 2) * 1024 + (c % 2) * 512
            return c * 512
        att_items = []
        for c in range(8):
            for half in range(2):
                segs = []
                sstart = 0
                while sstart < ntb:
                    bb = (t0 + sstart) // TPB
                    send = sstart
                    while send < ntb and (t0 + send) // TPB == bb:
                        send += 1
                    segs.append((sstart, send, bb))
                    sstart = send
                for si, seg in enumerate(segs):
                    att_items.append((c, half, seg))
        for n_it, (c, half, (sa, sb2, bb)) in enumerate(att_items):
            col = att_col(c)
            hrow = half * 64
            w = []
            if n_it == 0:
                w = [(s_gev, GEV), (s_qldB, QLD),
                     (s_silu, s_silu.n), (s_cwbc, c_cwb), (s_kvfA, NB_)]
            em.emit("tensor", lambda e, c=c, hrow=hrow, sa=sa, sb2=sb2, bb=bb,
                    col=col, qb_=qb_: e.matmul(
                        PS[hrow:hrow + 64, col + sa * 128: col + sb2 * 128],
                        kv_sb[hrow:hrow + 64,
                              bb * 1024 + c * 128 + hrow:
                              bb * 1024 + c * 128 + hrow + 64],
                        qb_[hrow:hrow + 64, c * 384 + sa * 128: c * 384 + sb2 * 128],
                        start=True, stop=True, **MMK),
                    waits=w,
                    inc=(s_attpe if n_it == len(att_items) - 1 else None))

        for c in range(8):
            col = att_col(c)
            em.emit("vector", lambda e, c=c, col=col, T=T: e.tensor_tensor(
                att_bf[:, c * 384: c * 384 + T],
                PS[:, col: col + T],
                xn_fm[:, c * 384: c * 384 + T], ALU.add),
                waits=[(s_attpe, p + 1)], inc=s_attev, nodrain=True)

        for c in range(8):
            col = att_col(c)
            for kc in range(8):
                w = [(s_attev, 8 * p + max(kc, c) + 1)]
                if c == 0 and kc == 0 and p == 0:
                    w.append((s_conv, 80))
                em.emit("tensor", lambda e, c=c, kc=kc, col=col, T=T: e.matmul(
                    PS[:, col: col + T],
                    p1_sb[:, kc * 1024 + c * 128: kc * 1024 + c * 128 + 128],
                    att_bf[:, kc * 384: kc * 384 + T],
                    start=(kc == 0), stop=(kc == 7), **MMK),
                    waits=w, inc=(s_p1pe if kc == 7 else None))
            em.emit("scalar", lambda e, c=c, col=col, T=T: e.activation(
                h1_bf[:, c * 384: c * 384 + T], PS[:, col: col + T],
                AF.Gelu, bias=p1b_sb[:, c:c + 1]),
                waits=[(s_p1pe, s_p1pe.n)], inc=s_gelev, tset="gelu")

        P2BASE = s_p2pe.n
        for tb in range(ntb):
            for hc in range(8):
                for dh in range(2):
                    w = []
                    if dh == 0:
                        w = [(s_gelev, 8 * p + max(hc + 1, 2 * tb + 2))]
                        if hc == 0 and tb == 0 and p == 0:
                            w.append((s_conv, 80))
                    em.emit("tensor", lambda e, tb=tb, hc=hc, dh=dh: e.matmul(
                        PS[:, tb * 1024 + dh * 512: tb * 1024 + dh * 512 + 512],
                        h1_bf[:, hc * 384 + tb * 128: hc * 384 + tb * 128 + 128],
                        p2_sb[:, hc * 1024 + dh * 512: hc * 1024 + dh * 512 + 512],
                        start=(hc == 0), stop=False, **MMK),
                        waits=w, inc=None)
            for dh in range(2):
                em.emit("tensor", lambda e, tb=tb, dh=dh: e.matmul(
                    PS[:, tb * 1024 + dh * 512: tb * 1024 + dh * 512 + 512],
                    onesb_sb,
                    p2b_bf[:, dh * 512: dh * 512 + 512],
                    start=False, stop=True, **MMK),
                    waits=[(s_initc, INIT_EVAC)] if (tb == 0 and dh == 0) else [],
                    inc=(s_p2pe if dh == 1 else None))

        for tb in range(ntb):
            w = [(s_p2pe, P2BASE + tb + 1)]
            if tb == 0:
                w.append((s_odma, 16 * p))
            em.emit("vector", lambda e, tb=tb: e.tensor_tensor(
                s_tm[:, tb * 1024:(tb + 1) * 1024],
                PS[:, tb * 1024:(tb + 1) * 1024],
                xn_tm[:, tb * 1024:(tb + 1) * 1024], ALU.add),
                waits=w, inc=(s_sev if tb == ntb - 1 else None), nodrain=True)
        if p + 1 < NP_:
            xn_opB(p + 1, [(s_trB, s_trB.n)])
        for tb in range(ntb):
            em.emit("vector", lambda e, tb=tb: e.tensor_tensor(
                g_tm[:, tb * 1024:(tb + 1) * 1024],
                s_tm[:, tb * 1024:(tb + 1) * 1024],
                g_tm[:, tb * 1024:(tb + 1) * 1024], ALU.mult),
                waits=[(s_gev, GEV)] if tb == 0 else [],
                nodrain=(ntb > 1 or tb > 0))
        for tb in range(ntb):
            em.emit("vector", lambda e, tb=tb, xb=xb: e.tensor_tensor(
                s_tm[:, tb * 1024:(tb + 1) * 1024],
                g_tm[:, tb * 1024:(tb + 1) * 1024],
                xb[:, tb * 1024:(tb + 1) * 1024], ALU.add),
                inc=(s_odve if tb == ntb - 1 else None),
                nodrain=(ntb > 1 or tb > 0))

        em.emit("gpsimd", lambda e, t0=t0, ntb=ntb: e.dma_start(
            out=out_d[t0 * 128:(t0 + ntb) * 128, :].rearrange(
                "(tb p) dd -> p tb dd", p=128),
            in_=s_tm.rearrange("p (tb dd) -> p tb dd", dd=1024)[:, 0:ntb]),
            waits=[(s_odve, p + 1)], inc=s_odma, inc_n=16, is_dma=True)

    if debug == 2 and NP_ > 0:
        s_dbg2 = S("dbg2")
        dw2 = [(s_odma, 16 * NP_)]
        for c3 in range(3):
            em.emit("gpsimd", lambda e, c3=c3: e.dma_start(
                out=out_d[4096 + c3 * 128: 4096 + (c3 + 1) * 128, :],
                in_=g_tm[:, c3 * 1024:(c3 + 1) * 1024]),
                waits=dw2, inc=s_dbg2, inc_n=16)
        for c3 in range(3):
            em.emit("gpsimd", lambda e, c3=c3: e.dma_start(
                out=out_d[4480 + c3 * 128: 4480 + (c3 + 1) * 128, :],
                in_=att_bf[:, c3 * 1024:(c3 + 1) * 1024]),
                inc=s_dbg2, inc_n=16)
        for c3 in range(3):
            em.emit("gpsimd", lambda e, c3=c3: e.dma_start(
                out=out_d[4864 + c3 * 128: 4864 + (c3 + 1) * 128, :],
                in_=h1_bf[:, c3 * 1024:(c3 + 1) * 1024]),
                inc=s_dbg2, inc_n=16)
        for c3 in range(3):
            em.emit("gpsimd", lambda e, c3=c3: e.dma_start(
                out=out_d[5248 + c3 * 128: 5248 + (c3 + 1) * 128, :],
                in_=xn_fm[:, c3 * 1024:(c3 + 1) * 1024]),
                inc=s_dbg2, inc_n=16)
        for c3 in range(3):
            em.emit("gpsimd", lambda e, c3=c3: e.dma_start(
                out=out_d[5632 + c3 * 128: 5632 + (c3 + 1) * 128, :],
                in_=q_pass[0][:, c3 * 1024:(c3 + 1) * 1024]),
                inc=s_dbg2, inc_n=16)
        for c3 in range(3):
            em.emit("gpsimd", lambda e, c3=c3: e.dma_start(
                out=out_d[6016 + c3 * 128: 6016 + (c3 + 1) * 128, :],
                in_=q_pass[1][:, c3 * 1024:(c3 + 1) * 1024]),
                inc=s_dbg2, inc_n=16)
        em.emit("sync", lambda e: e.nop(), waits=[(s_dbg2, s_dbg2.n)])

    if debug == 1:
        s_dbg = S("dbg")
        dw = [(s_cwT, s_cwT.n), (s_kvfA, NB_),
              (s_qsp2[0], s_qsp2[0].n), (s_qsp2[1], s_qsp2[1].n)]
        for c4 in range(4):
            em.emit("gpsimd", lambda e, c4=c4: e.dma_start(
                out=out_d[4096 + c4 * 128: 4096 + (c4 + 1) * 128, :],
                in_=logits_all[:, c4 * 1024:(c4 + 1) * 1024]),
                waits=dw, inc=s_dbg, inc_n=16)
        em.emit("gpsimd", lambda e: e.dma_start(
            out=out_d[4608:4736, 0:64], in_=stats_m), inc=s_dbg, inc_n=16)
        em.emit("gpsimd", lambda e: e.dma_start(
            out=out_d[4736:4864, 0:64], in_=stats_r), inc=s_dbg, inc_n=16)
        for c4 in range(4):
            em.emit("gpsimd", lambda e, c4=c4: e.dma_start(
                out=out_d[4864 + c4 * 128: 4864 + (c4 + 1) * 128, :],
                in_=kv_sb[:, c4 * 1024:(c4 + 1) * 1024]),
                inc=s_dbg, inc_n=16)
        for qi, qt in enumerate([t for t in [0, 1, 2, 3, 33, 40, 49, 50] if t < NT_]):
            em.emit("gpsimd", lambda e, qi=qi, qt=qt: e.dma_start(
                out=out_d[7168 + qi * 128: 7296 + qi * 128, :].rearrange(
                    "p (c t) -> p c t", t=128),
                in_=qscr_d[qt].rearrange("c p t -> p c t")),
                inc=s_dbg, inc_n=16)
        for qt in range(6):
            em.emit("gpsimd", lambda e, qt=qt: e.dma_start(
                out=out_d[6144 + qt * 128: 6272 + qt * 128, :].rearrange(
                    "p (c t) -> p c t", t=128),
                in_=qscr_d[qt].rearrange("c p t -> p c t")),
                inc=s_dbg, inc_n=16)
        em.emit("gpsimd", lambda e: e.dma_start(
            out=out_d[8000:8001, :], in_=krow_sb[0:1, 1024:2048]),
            waits=dw, inc=s_dbg, inc_n=16)
        em.emit("gpsimd", lambda e: e.dma_start(
            out=out_d[8001:8002, :], in_=krow_sb[0:1, 0:1024]),
            waits=dw, inc=s_dbg, inc_n=16)
        em.emit("gpsimd", lambda e: e.dma_start(
            out=out_d[8002:8003, :], in_=brow_sb),
            waits=dw, inc=s_dbg, inc_n=16)
        em.emit("gpsimd", lambda e: e.dma_start(
            out=out_d[8003:8004, 0:16], in_=knsum_col[0:1, 0:16]),
            waits=dw, inc=s_dbg, inc_n=16)
        em.emit("gpsimd", lambda e: e.dma_start(
            out=out_d[7040:7168, :], in_=k16_sb), waits=dw, inc=s_dbg, inc_n=16)
        em.emit("gpsimd", lambda e: e.dma_start(
            out=out_d[6912:7040, :], in_=v16_sb), waits=dw, inc=s_dbg, inc_n=16)
        for c2 in range(2):
            em.emit("gpsimd", lambda e, c2=c2: e.dma_start(
                out=out_d[5504 + c2 * 64: 5504 + (c2 + 1) * 64, :],
                in_=cwT[0:64, c2 * 1024:(c2 + 1) * 1024]),
                inc=s_dbg, inc_n=16)
        em.emit("sync", lambda e: e.nop(), waits=[(s_dbg, s_dbg.n)])
    if NP_ > 0:
        em.emit("sync", lambda e: e.nop(), waits=[(s_odma, 16 * NP_)])

    em.check()

    with nc.Block() as block:
        @block.sync
        def _(eng):
            em.replay("sync", eng)

        @block.scalar
        def _(eng):
            em.replay("scalar", eng)

        @block.vector
        def _(eng):
            em.replay("vector", eng)

        @block.tensor
        def _(eng):
            em.replay("tensor", eng)

        @block.gpsimd
        def _(eng):
            em.replay("gpsimd", eng)

    ctx.close()
    return nc


def _prep_shared(inp):
    f32 = np.float32
    qkv_wT = np.ascontiguousarray(np.asarray(inp["L0_attn_qkv_w"], f32).T)
    p1_wT = np.ascontiguousarray(np.asarray(inp["L0_attn_p1_w"], f32).T)
    p2_wT = np.ascontiguousarray(np.asarray(inp["L0_attn_p2_w"], f32).T)
    rw = np.ascontiguousarray(np.asarray(inp["L0_ffn_router_w"], f32))
    w1 = np.asarray(inp["L0_ffn_w1"], f32)
    w2 = np.asarray(inp["L0_ffn_w2"], f32)
    sw1 = np.asarray(inp["L0_ffn_sw1"], f32)
    sw2 = np.asarray(inp["L0_ffn_sw2"], f32)
    units = []
    for e in range(64):
        units.append(np.concatenate([w1[e].reshape(-1), w2[e].reshape(-1)]))
    for es in range(2):
        for sl in range(8):
            a = np.ascontiguousarray(sw1[es][:, sl * 256:(sl + 1) * 256]).reshape(-1)
            b = np.ascontiguousarray(sw2[es][sl * 256:(sl + 1) * 256, :]).reshape(-1)
            units.append(np.concatenate([a, b]))
    wall = np.ascontiguousarray(np.stack(units))

    b1 = np.asarray(inp["L0_ffn_b1"], f32)
    sb1 = np.asarray(inp["L0_ffn_sb1"], f32)
    bias1 = np.zeros((128, 2 * NU), f32)
    bias1[:, 0:128] = b1.reshape(64, 2, 128).transpose(2, 0, 1).reshape(128, 128)
    bias1[:, 128:160] = sb1.reshape(2, 8, 2, 128).transpose(3, 0, 1, 2).reshape(128, 32)
    b2ext = np.ascontiguousarray(np.concatenate(
        [np.asarray(inp["L0_ffn_b2"], f32), np.asarray(inp["L0_ffn_sb2"], f32)],
        axis=0))

    qkv_b = np.asarray(inp["L0_attn_qkv_b"], f32)
    qb = np.ascontiguousarray(qkv_b[0:1024].reshape(8, 128).T)
    p1b = np.ascontiguousarray(np.asarray(inp["L0_attn_p1_b"], f32).reshape(8, 128).T)
    rows7 = np.concatenate([
        qkv_b[1024:2048], qkv_b[2048:3072],
        np.tile(np.asarray(inp["L0_attn_kn_g"], f32), H),
        np.tile(np.asarray(inp["L0_attn_kn_b"], f32), H),
        np.tile(np.asarray(inp["L0_attn_vn_g"], f32), H),
        np.tile(np.asarray(inp["L0_attn_vn_b"], f32), H),
        np.asarray(inp["L0_attn_p2_b"], f32),
    ])[None, :]
    rb = np.ascontiguousarray(np.asarray(inp["L0_ffn_router_b"], f32)[None, :])
    temp = np.asarray(inp["L0_attn_temperature"], f32).reshape(H)
    temp_eo = np.ascontiguousarray(
        np.concatenate([temp[0::2], temp[1::2]])[None, :])
    abias = np.ascontiguousarray(np.asarray(inp["L0_attn_attn_bias"], f32))

    onescol = np.ones((128, 1), ml_dtypes.bfloat16)
    kn_g_a = np.asarray(inp["L0_attn_kn_g"], f32)
    kn_b_a = np.asarray(inp["L0_attn_kn_b"], f32)
    cols4 = np.zeros((128, 4), f32)
    cols4[:, 0] = np.tile(kn_g_a, 2)
    cols4[:, 1] = np.tile(kn_b_a, 2)
    oh = np.zeros((64, 64 * 128), ml_dtypes.bfloat16)
    for e in range(64):
        oh[e, e * 128:(e + 1) * 128] = 1.0
    idf = np.eye(128, dtype=f32)
    onesf = np.ones((1, 128), f32)
    onesb = np.ones((1, 128), ml_dtypes.bfloat16)

    return dict(qkv_wT=qkv_wT, p1_wT=p1_wT, p2_wT=p2_wT, router_w=rw, wall=wall,
                bias1=bias1, b2ext=b2ext, qb=qb, p1b=p1b, rows7=rows7, rb=rb,
                temp_eo=temp_eo, attn_bias=abias, oh=oh, idf=idf, onesf=onesf,
                onesb=onesb, onescol=onescol, cols4=cols4)


def kernel(**inputs):
    x = np.asarray(inputs["L0_attn_x"], np.float32)
    shared = _prep_shared(inputs)
    nc = build_nc()
    in_maps = []
    for c in range(NCORES):
        m = dict(shared)
        m["x"] = np.ascontiguousarray(x[c * NB:(c + 1) * NB].reshape(NT * 128, D))
        in_maps.append(m)
    res = run_bass_kernel_spmd(nc, in_maps, core_ids=list(range(NCORES)))
    outs = [np.asarray(res.results[c]["out"]).reshape(NB, 2048, D)
            for c in range(NCORES)]
    return np.concatenate(outs, axis=0)


if __name__ == "__main__":
    import jax
    jax.config.update("jax_platforms", "cpu")
    import reference
    inp = reference.setup_inputs()
    got = kernel(**inp)
    ref = np.asarray(reference.reference(**inp), np.float32)
    d = got - ref
    print("rvar:", float((d * d).mean() / (ref * ref).mean()))
